# version 13
# baseline (speedup 1.0000x reference)
import math
import numpy as np
import ml_dtypes
import concourse.bass as bass
import concourse.mybir as mybir
from concourse.bass_utils import run_bass_kernel_spmd

F32 = mybir.dt.float32
BF16 = mybir.dt.bfloat16
I32 = mybir.dt.int32
AF = mybir.ActivationFunctionType
ALU = mybir.AluOpType

NCORES = 8
S = 8192
H = 4096
TOK = 1024
HALO = 512
LTOK = TOK + HALO
NHT = H // 128
NH = 32
NKV = 8
HD = 128
GQ = NH // NKV
II = 11008
NIT = II // 128
SCALE = 1.0 / math.sqrt(HD)
CSH = 12.0
EPS = 1e-5
THETA = 10000.0
KB = 1024


class Prog:
    def __init__(self):
        self.nc = bass.Bass("TRN2", target_bir_lowering=False)
        self.sems = {}
        self.cnt = {}
        self.eng_sems = {}

    def sem(self, name):
        if name not in self.sems:
            self.sems[name] = self.nc.alloc_semaphore(name)
            self.cnt[name] = 0
        return self.sems[name]

    def inc(self, instr, name, n=1):
        instr.then_inc(self.sem(name), n)
        self.cnt[name] += n
        eng = str(instr.ins.engine)
        self.eng_sems.setdefault(eng, set()).add(name)
        return self.cnt[name]

    def val(self, name):
        self.sem(name)
        return self.cnt[name]

    def wge(self, eng, name, v):
        eng.wait_ge(self.sem(name), v)

    SBUF_BASE = 16512

    def sb(self, name, shape, dtype, off):
        return self.nc.alloc_sbuf_tensor_at(name, shape, dtype,
                                            offset=self.SBUF_BASE + off)


def build_program():
    P = Prog()
    nc = P.nc

    din = {}
    def dram_in(name, shape, dt):
        din[name] = nc.dram_tensor(name, list(shape), dt, kind="ExternalInput").ap()
        return din[name]

    x0s = dram_in("x0s", [LTOK, H], F32)
    wq_t = dram_in("wq_t", [NH, 4, 128, 1024], F32)
    wk_t = dram_in("wk_t", [NKV, 128, NHT * 128], F32)
    wv_t = dram_in("wv_t", [4, 128, NHT * 256], F32)
    wo_t = dram_in("wo_t", [8, 2, 128, NHT * 256], F32)
    csc_in = dram_in("csc", [128, LTOK], F32)
    css_in = dram_in("css", [128, LTOK], F32)
    bk_in = dram_in("bk_in", [128, 12], F32)
    m0_in = dram_in("m0_in", [128, 512], BF16)
    m4_in = dram_in("m4_in", [128, 512], BF16)
    id_in = dram_in("id_in", [128, 128], BF16)
    qg_in = dram_in("qg_in", [II, H], I32)
    qu_in = dram_in("qu_in", [II, H], I32)
    qd_in = dram_in("qd_in", [H, II], I32)
    sg_in = dram_in("sg_in", [II, 32], F32)
    zg_in = dram_in("zg_in", [II, 32], F32)
    su_in = dram_in("su_in", [II, 32], F32)
    zu_in = dram_in("zu_in", [II, 32], F32)
    sd_in = dram_in("sd_in", [H, NIT], F32)
    zd_in = dram_in("zd_in", [H, NIT], F32)

    y_t = nc.dram_tensor("y_t", [H, TOK], F32, kind="ExternalOutput").ap()

    scr_g = nc.dram_tensor("scr_g", [II, H], BF16, kind="Internal").ap()
    scr_u = nc.dram_tensor("scr_u", [II, H], BF16, kind="Internal").ap()
    scr_d2 = nc.dram_tensor("scr_d2", [NIT, 128, H], BF16, kind="Internal").ap()
    scr_x1t = nc.dram_tensor("scr_x1t", [H, TOK], BF16, kind="Internal").ap()

    cs_cos = P.sb("cs_cos", [128, LTOK], F32, 0)
    cs_sin = P.sb("cs_sin", [128, LTOK], F32, 6144)
    m0 = P.sb("m0", [128, 512], BF16, 12288)
    m4 = P.sb("m4", [128, 512], BF16, 13312)
    biask = P.sb("biask", [128, 12], F32, 14336)
    ident = P.sb("ident", [128, 128], BF16, 14400)
    ones_r = P.sb("ones_r", [1, 128], BF16, 14656)
    ones_c = P.sb("ones_c", [128, 1], BF16, 14912)
    c_zero = P.sb("c_zero", [128, 1], F32, 14944)
    c_eps = P.sb("c_eps", [128, 1], F32, 14976)
    OFF_A = 15 * KB
    OFF_D = 79 * KB
    OFF_E = 143 * KB
    OFF_F = 167 * KB
    OFF_G = 191 * KB

    x1nT_h = P.sb("x1nT_h", [128, NHT * HALO], BF16, OFF_A)
    x1nT_o = P.sb("x1nT_o", [128, NHT * TOK], BF16, OFF_A)
    attnT = P.sb("attnT", [128, NH * TOK], BF16, OFF_A)
    x2nT = P.sb("x2nT", [128, NHT * TOK], BF16, OFF_A)
    qT = P.sb("qT", [128, NH * TOK], BF16, OFF_D)
    x1 = P.sb("x1", [128, 8 * H], BF16, OFF_D)
    hT = P.sb("hT", [128, NIT * 512], BF16, OFF_D)
    kT = P.sb("kT", [128, NKV * LTOK], BF16, OFF_E)
    vB = P.sb("vB", [128, 12 * 1024], BF16, OFF_F)

    bar_n = [0]

    block_cm = nc.Block()
    block = block_cm.__enter__()

    def bodies(sync_f=None, tensor_f=None, vector_f=None, scalar_f=None,
               gpsimd_f=None, bar=True):
        def drain(eng, ename):
            for s in sorted(P.eng_sems.get(ename, ())):
                P.wge(eng, s, P.val(s))
            P.eng_sems[ename] = set()
        @block.sync
        def _(sync):
            P.wge(sync, "BAR", bar_n[0])
            if sync_f:
                sync_f(sync)
            if bar:
                drain(sync, "EngineType.SP")
                P.inc(sync.nop(), "BAR")
        @block.tensor
        def _(tensor):
            P.wge(tensor, "BAR", bar_n[0])
            if tensor_f:
                tensor_f(tensor)
            if bar:
                drain(tensor, "EngineType.PE")
                P.inc(tensor.nop(), "BAR")
        @block.vector
        def _(vector):
            P.wge(vector, "BAR", bar_n[0])
            if vector_f:
                vector_f(vector)
            if bar:
                drain(vector, "EngineType.DVE")
                P.inc(vector.nop(), "BAR")
        @block.scalar
        def _(scalar):
            P.wge(scalar, "BAR", bar_n[0])
            if scalar_f:
                scalar_f(scalar)
            if bar:
                drain(scalar, "EngineType.Activation")
                P.inc(scalar.nop(), "BAR")
        @block.gpsimd
        def _(gpsimd):
            P.wge(gpsimd, "BAR", bar_n[0])
            if gpsimd_f:
                gpsimd_f(gpsimd)
            if bar:
                drain(gpsimd, "EngineType.Pool")
                P.inc(gpsimd.nop(), "BAR")
        if bar:
            bar_n[0] = P.val("BAR")

    def p0_sync(sync):
        for dst, src in [(cs_cos, csc_in), (cs_sin, css_in), (m0, m0_in),
                         (m4, m4_in), (biask, bk_in), (ident, id_in)]:
            P.inc(sync.dma_start(dst[:], src[:]), "cLD", 16)
        P.wge(sync, "cLD", P.val("cLD"))

    def p0_vec(vector):
        vector.memset(c_zero[:], 0.0)
        vector.memset(c_eps[:], EPS)
        vector.memset(ones_r[:], 1.0)
        P.inc(vector.memset(ones_c[:], 1.0), "vP")
        P.wge(vector, "vP", P.val("vP"))

    bodies(sync_f=p0_sync, vector_f=p0_vec)

    qsl_g = [P.sb(f"qsl_g{i}", [128, H], I32, OFF_D + i * 16 * KB) for i in range(3)]
    qsl_u = [P.sb(f"qsl_u{i}", [128, H], I32, [OFF_D + 48 * KB, OFF_E, OFF_E + 16 * KB][i]) for i in range(3)]
    wb_g = [P.sb(f"wb_g{i}", [128, H], BF16, OFF_A + 44 * KB + i * 8 * KB) for i in range(2)]
    wb_u = [P.sb(f"wb_u{i}", [128, H], BF16, OFF_G + i * 8 * KB) for i in range(2)]
    s_g = P.sb("s_g", [128, NIT * 32], F32, OFF_A)
    nzs_g = P.sb("nzs_g", [128, NIT * 32], F32, OFF_A + 11 * KB)
    s_u = P.sb("s_u", [128, NIT * 32], F32, OFF_A + 22 * KB)
    nzs_u = P.sb("nzs_u", [128, NIT * 32], F32, OFF_A + 33 * KB)

    vp0 = P.val("vP")
    USPL = 22

    def p1gu_sync(sync):
        for dst, src in [(s_g, sg_in), (nzs_g, zg_in), (s_u, su_in), (nzs_u, zu_in)]:
            P.inc(sync.dma_start(
                dst.ap().rearrange("p (a b) -> p a b", b=32),
                src.rearrange("(a p) b -> p a b", p=128)), "szA", 16)
        for i in range(NIT):
            if i >= 3:
                P.wge(sync, "vDq", i - 2)
            P.inc(sync.dma_start(qsl_g[i % 3][:], qg_in[i * 128:(i + 1) * 128, :]),
                  "qg%d" % (i % 3), 16)
            if i >= 3:
                P.wge(sync, "aDq", i - 2)
                P.wge(sync, "uDv", i - 2)
            P.inc(sync.dma_start(qsl_u[i % 3][:], qu_in[i * 128:(i + 1) * 128, :]),
                  "qu%d" % (i % 3), 16)

    def p1gu_vec(vector):
        P.wge(vector, "szA", 64)
        P.inc(vector.tensor_scalar(nzs_u[:], nzs_u[:], -1.0, None, ALU.mult), "vP")
        P.wge(vector, "vP", P.val("vP"))
        P.inc(vector.tensor_tensor(nzs_u[:], nzs_u[:], s_u[:], ALU.mult), "vP")
        P.inc(vector.tensor_scalar(nzs_g[:], nzs_g[:], -1.0, None, ALU.mult), "vP")
        P.wge(vector, "vP", P.val("vP"))
        P.inc(vector.tensor_tensor(nzs_g[:], nzs_g[:], s_g[:], ALU.mult), "vP")
        P.wge(vector, "vP", P.val("vP"))
        sg3 = s_g.ap().rearrange("p (a b) -> p a b", b=32)
        ng3 = nzs_g.ap().rearrange("p (a b) -> p a b", b=32)
        su3 = s_u.ap().rearrange("p (a b) -> p a b", b=32)
        nu3 = nzs_u.ap().rearrange("p (a b) -> p a b", b=32)
        for i in range(NIT):
            P.wge(vector, "qg%d" % (i % 3), (i // 3 + 1) * 16)
            if i >= 2:
                P.wge(vector, "wg%d" % (i % 2), ((i - 2) // 2 + 1) * 16)
            last = None
            for g in range(32):
                last = vector.tensor_scalar(
                    wb_g[i % 2][:, g * 128:(g + 1) * 128],
                    qsl_g[i % 3][:, g * 128:(g + 1) * 128],
                    sg3[:, i, g:g + 1], ng3[:, i, g:g + 1],
                    ALU.mult, ALU.add)
            P.inc(last, "vDq")
            P.wge(vector, "qu%d" % (i % 3), (i // 3 + 1) * 16)
            if i >= 2:
                P.wge(vector, "wu%d" % (i % 2), ((i - 2) // 2 + 1) * 16)
            last = None
            for g in range(USPL, 32):
                last = vector.tensor_scalar(
                    wb_u[i % 2][:, g * 128:(g + 1) * 128],
                    qsl_u[i % 3][:, g * 128:(g + 1) * 128],
                    su3[:, i, g:g + 1], nu3[:, i, g:g + 1],
                    ALU.mult, ALU.add)
            P.inc(last, "uDv")

    def p1gu_sca(scalar):
        P.wge(scalar, "szA", 64)
        P.wge(scalar, "vP", vp0 + 2)
        su3 = s_u.ap().rearrange("p (a b) -> p a b", b=32)
        nu3 = nzs_u.ap().rearrange("p (a b) -> p a b", b=32)
        for i in range(NIT):
            P.wge(scalar, "qu%d" % (i % 3), (i // 3 + 1) * 16)
            if i >= 2:
                P.wge(scalar, "wu%d" % (i % 2), ((i - 2) // 2 + 1) * 16)
            last = None
            for g in range(USPL):
                last = scalar.activation(
                    wb_u[i % 2][:, g * 128:(g + 1) * 128],
                    qsl_u[i % 3][:, g * 128:(g + 1) * 128],
                    AF.Identity, bias=nu3[:, i, g:g + 1], scale=su3[:, i, g:g + 1])
            P.inc(last, "aDq")
            P.wge(scalar, "aDq", i + 1)
            P.wge(scalar, "uDv", i + 1)
            P.inc(scalar.dma_start(scr_u[i * 128:(i + 1) * 128, :], wb_u[i % 2][:]),
                  "wu%d" % (i % 2), 16)
            P.wge(scalar, "vDq", i + 1)
            P.inc(scalar.dma_start(scr_g[i * 128:(i + 1) * 128, :], wb_g[i % 2][:]),
                  "wg%d" % (i % 2), 16)

    bodies(sync_f=p1gu_sync, vector_f=p1gu_vec, scalar_f=p1gu_sca)

    DS = 16
    qsl_d = [P.sb(f"qsl_d{j}", [128, NHT * 128], I32, OFF_D + j * 16 * KB) for j in range(2)]
    wb_d = [P.sb(f"wb_d{j}", [128, H], BF16, OFF_D + 32 * KB + j * 8 * KB) for j in range(2)]
    sst = [P.sb(f"sst{j}", [128, H], BF16, OFF_D + 48 * KB + j * 8 * KB) for j in range(2)]
    s_d = P.sb("s_d", [128, NHT * NIT], F32, OFF_A)
    nzs_d = P.sb("nzs_d", [128, NHT * NIT], F32, OFF_A + 11 * KB)

    ptd_cm = [nc.psum_tensor(f"ptd{j}", [128, H], BF16) for j in range(2)]
    ptd = [cm.__enter__() for cm in ptd_cm]
    qd3 = qd_in.rearrange("(a p) q -> p a q", p=128)

    def p1d_sync(sync):
        for dst, src in [(s_d, sd_in), (nzs_d, zd_in)]:
            P.inc(sync.dma_start(
                dst.ap().rearrange("p (a b) -> p a b", b=NIT),
                src.rearrange("(a p) b -> p a b", p=128)), "szD", 16)
        for it in range(NIT):
            if it >= 2:
                P.wge(sync, "vDd", it - 1)
                P.wge(sync, "aDd", it - 1)
            P.inc(sync.dma_start(
                qsl_d[it % 2].ap().rearrange("p (a q) -> p a q", q=128),
                qd3[:, :, it * 128:(it + 1) * 128]), "qd%d" % (it % 2), 16)

    def p1d_pe(tensor):
        for it in range(NIT):
            P.wge(tensor, "vDd", it + 1)
            P.wge(tensor, "aDd", it + 1)
            if it >= 2:
                P.wge(tensor, "vEv", it - 1)
                P.wge(tensor, "aEv", it - 1)
            last = None
            for hs in range(NHT):
                last = tensor.transpose(
                    ptd[it % 2][:, hs * 128:(hs + 1) * 128],
                    wb_d[it % 2][:, hs * 128:(hs + 1) * 128], ident[:])
            P.inc(last, "pTd")

    def p1d_vec(vector):
        P.wge(vector, "szD", 32)
        P.inc(vector.tensor_scalar(nzs_d[:], nzs_d[:], -1.0, None, ALU.mult), "vP")
        P.wge(vector, "vP", P.val("vP"))
        P.inc(vector.tensor_tensor(nzs_d[:], nzs_d[:], s_d[:], ALU.mult), "vP")
        P.wge(vector, "vP", P.val("vP"))
        sd3 = s_d.ap().rearrange("p (a b) -> p a b", b=NIT)
        nd3 = nzs_d.ap().rearrange("p (a b) -> p a b", b=NIT)
        q3 = [q.ap().rearrange("p (a q) -> p a q", q=128) for q in qsl_d]

        def evac(it):
            P.wge(vector, "pTd", it + 1)
            if it >= 2:
                P.wge(vector, "w2%d" % (it % 2), (it // 2) * 16)
            P.inc(vector.tensor_copy(sst[it % 2][:, 0:DS * 128],
                                     ptd[it % 2][:, 0:DS * 128]), "vEv")

        for it in range(NIT):
            P.wge(vector, "qd%d" % (it % 2), (it // 2 + 1) * 16)
            if it >= 2:
                P.wge(vector, "pTd", it - 1)
            last = None
            for hs in range(DS):
                last = vector.tensor_scalar(
                    wb_d[it % 2][:, hs * 128:(hs + 1) * 128],
                    q3[it % 2][:, hs, :],
                    sd3[:, hs, it:it + 1], nd3[:, hs, it:it + 1],
                    ALU.mult, ALU.add)
            P.inc(last, "vDd")
            if it >= 1:
                evac(it - 1)
        evac(NIT - 1)

    def p1d_sca(scalar):
        P.wge(scalar, "vP", P.val("vP"))
        sd3 = s_d.ap().rearrange("p (a b) -> p a b", b=NIT)
        nd3 = nzs_d.ap().rearrange("p (a b) -> p a b", b=NIT)
        q3 = [q.ap().rearrange("p (a q) -> p a q", q=128) for q in qsl_d]

        def evst(it):
            P.wge(scalar, "pTd", it + 1)
            if it >= 2:
                P.wge(scalar, "w2%d" % (it % 2), (it // 2) * 16)
            P.inc(scalar.activation(sst[it % 2][:, DS * 128:], ptd[it % 2][:, DS * 128:],
                                    AF.Copy), "aEv")
            P.wge(scalar, "aEv", it + 1)
            P.wge(scalar, "vEv", it + 1)
            P.inc(scalar.dma_start(scr_d2[it, :, :], sst[it % 2][:]),
                  "w2%d" % (it % 2), 16)

        for it in range(NIT):
            P.wge(scalar, "qd%d" % (it % 2), (it // 2 + 1) * 16)
            if it >= 2:
                P.wge(scalar, "pTd", it - 1)
            last = None
            for hs in range(DS, NHT):
                last = scalar.activation(
                    wb_d[it % 2][:, hs * 128:(hs + 1) * 128],
                    q3[it % 2][:, hs, :],
                    AF.Identity, bias=nd3[:, hs, it:it + 1], scale=sd3[:, hs, it:it + 1])
            P.inc(last, "aDd")
            if it >= 1:
                evst(it - 1)
        evst(NIT - 1)

    bodies(sync_f=p1d_sync, tensor_f=p1d_pe, vector_f=p1d_vec, scalar_f=p1d_sca)
    for cm in ptd_cm[::-1]:
        cm.__exit__(None, None, None)

    x0st = [P.sb(f"x0st{i}", [128, H], F32, OFF_D + i * 16 * KB) for i in range(2)]
    junk = [P.sb(f"junk{i}", [128, H], BF16, OFF_D + 32 * KB + i * 8 * KB) for i in range(2)]
    xnst = [P.sb(f"xnst{i}", [128, H], BF16, OFF_D + 48 * KB + i * 8 * KB) for i in range(2)]
    stt = [P.sb(f"stt{i}", [128, 8], F32, 15008 + i * 64) for i in range(2)]

    ptr_cm = [nc.psum_tensor(f"ptr{i}", [128, 1024], BF16) for i in range(2)]
    ptr = [cm.__enter__() for cm in ptr_cm]

    def emit_ln_tiles(rows_list, dst, dstW, tag, prewaits=None):
        n = len(rows_list)
        aN = "aN" + tag; aS = "aS" + tag; aL = "aL" + tag
        vS = "vS" + tag; vT = "vT" + tag; pT = "pT" + tag
        pw = prewaits or {}

        def f_sync(sync):
            for eng_sem, v in pw.get("sync", []):
                P.wge(sync, eng_sem, v)
            for t in range(n):
                if t >= 2:
                    P.wge(sync, aN, t - 1)
                P.inc(sync.dma_start(x0st[t % 2][:],
                                     x0s[rows_list[t][0]:rows_list[t][0] + 128, :]),
                      "x0%s%d" % (tag, t % 2), 16)

        def f_sca(scalar):
            for eng_sem, v in pw.get("scalar", []):
                P.wge(scalar, eng_sem, v)
            for t in range(n):
                P.wge(scalar, "x0%s%d" % (tag, t % 2), (t // 2 + 1) * 16)
                if t >= 2:
                    P.wge(scalar, vS, 5 * (t - 1))
                st = stt[t % 2]
                P.inc(scalar.activation(junk[0][:], x0st[t % 2][:], AF.Square,
                                        bias=c_zero[:], accum_out=st[:, 1:2]), aS)
                P.inc(scalar.activation(junk[1][:], x0st[t % 2][:], AF.Copy,
                                        accum_out=st[:, 0:1]), aS)
                P.wge(scalar, vS, 5 * t + 4)
                P.inc(scalar.activation(st[:, 4:5], st[:, 3:4], AF.Ln, bias=c_eps[:]), aL)
                P.wge(scalar, aL, P.val(aL))
                P.inc(scalar.activation(st[:, 4:5], st[:, 4:5], AF.Exp, bias=c_zero[:],
                                        scale=-0.5), aL)
                P.wge(scalar, vS, 5 * t + 5)
                P.inc(scalar.activation(xnst[t % 2][:], x0st[t % 2][:], AF.Identity,
                                        bias=st[:, 5:6], scale=st[:, 4:5]), aN)

        def f_vec(vector):
            for eng_sem, v in pw.get("vector", []):
                P.wge(vector, eng_sem, v)
            for t in range(n):
                st = stt[t % 2]
                P.wge(vector, aS, 2 * t + 2)
                P.inc(vector.tensor_scalar(st[:, 2:3], st[:, 0:1], 1.0 / H, None,
                                           ALU.mult), vS)
                P.inc(vector.tensor_scalar(st[:, 3:4], st[:, 1:2], 1.0 / H, None,
                                           ALU.mult), vS)
                P.wge(vector, vS, 5 * t + 2)
                P.inc(vector.tensor_tensor(st[:, 6:7], st[:, 2:3], st[:, 2:3], ALU.mult), vS)
                P.wge(vector, vS, 5 * t + 3)
                P.inc(vector.tensor_tensor(st[:, 3:4], st[:, 3:4], st[:, 6:7],
                                           ALU.subtract), vS)
                P.wge(vector, aL, 2 * t + 2)
                P.inc(vector.tensor_scalar(st[:, 5:6], st[:, 2:3], st[:, 4:5], -1.0,
                                           ALU.mult, ALU.mult), vS)
                col0 = rows_list[t][1]
                for ht in range(NHT):
                    P.wge(vector, pT, NHT * t + ht + 1)
                    P.inc(vector.tensor_copy(
                        dst[:, ht * dstW + col0: ht * dstW + col0 + 128],
                        ptr[(NHT * t + ht) % 2][:, 0:128]), vT)

        def f_pe(tensor):
            for eng_sem, v in pw.get("tensor", []):
                P.wge(tensor, eng_sem, v)
            for t in range(n):
                P.wge(tensor, aN, t + 1)
                for ht in range(NHT):
                    if NHT * t + ht >= 2:
                        P.wge(tensor, vT, NHT * t + ht - 1)
                    P.inc(tensor.transpose(ptr[(NHT * t + ht) % 2][:, 0:128],
                                           xnst[t % 2][:, ht * 128:(ht + 1) * 128],
                                           ident[:]), pT)

        return f_sync, f_pe, f_vec, f_sca

    halo_rows = [(i * 128, i * 128) for i in range(4)]
    fs, fp, fv, fa = emit_ln_tiles(halo_rows, x1nT_h, HALO, "h")
    bodies(sync_f=fs, tensor_f=fp, vector_f=fv, scalar_f=fa, bar=False)

    wkv_sl = P.sb("wkv_sl", [128, NHT * 256], BF16, OFF_G)
    wk_sl = P.sb("wk_sl", [128, NHT * 128], BF16, OFF_G)
    rt1a = P.sb("rt1a", [128, 512], F32, OFF_G + 8 * KB)
    rt2a = P.sb("rt2a", [128, 512], F32, OFF_G + 10 * KB)

    pk_cm = [nc.psum_tensor(f"pk{i}", [128, 512], F32) for i in range(2)]
    pk = [cm.__enter__() for cm in pk_cm]
    pvv_cm = [nc.psum_tensor(f"pvv{i}", [128, 512], F32) for i in range(2)]
    pvv = [cm.__enter__() for cm in pvv_cm]

    def rope_evac(vector, psrc, c0, dst, dcols, tag, rt1, rt2):
        P.wge(vector, tag, P.val(tag))
        W = 512
        P.inc(vector.tensor_tensor(rt1[0:64, :], psrc[64:128, :],
                                   cs_sin[0:64, c0:c0 + W], ALU.mult), "vR")
        P.inc(vector.tensor_tensor(rt1[64:128, :], psrc[0:64, :],
                                   cs_sin[64:128, c0:c0 + W], ALU.mult), "vR")
        P.inc(vector.tensor_tensor(rt2[:, :], psrc[:, :],
                                   cs_cos[:, c0:c0 + W], ALU.mult), "vR")
        P.wge(vector, "vR", P.val("vR"))
        P.inc(vector.tensor_tensor(dst[:, dcols:dcols + W], rt2[:, :], rt1[:, :],
                                   ALU.add), tag)

    def rope_evac_q(vector, psrc, c0, dst, dcols, tag, rt1):
        P.wge(vector, tag, P.val(tag))
        W = 512
        P.inc(vector.tensor_tensor(rt1[0:64, :], psrc[64:128, :],
                                   cs_sin[0:64, c0:c0 + W], ALU.mult), "vR")
        P.inc(vector.tensor_tensor(rt1[64:128, :], psrc[0:64, :],
                                   cs_sin[64:128, c0:c0 + W], ALU.mult), "vR")
        P.inc(vector.tensor_tensor(dst[:, dcols:dcols + W], psrc[:, :],
                                   cs_cos[:, c0:c0 + W], ALU.mult), "vR")
        P.wge(vector, "vR", P.val("vR"))
        P.inc(vector.tensor_tensor(dst[:, dcols:dcols + W], dst[:, dcols:dcols + W],
                                   rt1[:, :], ALU.add), tag)

    def p3a_gp(gpsimd):
        for kv in range(NKV):
            if kv >= 1:
                P.wge(gpsimd, "pK", kv)
                P.wge(gpsimd, "vK", kv)
            P.inc(gpsimd.dma_start(wk_sl[:], wk_t[kv]), "lKV", 16)

    lkv_a = P.val("lKV")

    def p3a_pe(tensor):
        P.wge(tensor, "vTh", 4 * NHT)
        for kv in range(NKV):
            P.wge(tensor, "lKV", lkv_a + (kv + 1) * 16)
            if kv >= 2:
                P.wge(tensor, "vK", kv - 1)
            last = None
            for ht in range(NHT):
                last = tensor.matmul(pk[kv % 2][:],
                                     wk_sl[:, ht * 128:(ht + 1) * 128],
                                     x1nT_h[:, ht * HALO:(ht + 1) * HALO],
                                     start=(ht == 0), stop=(ht == NHT - 1))
            P.inc(last, "pK")

    def p3a_vec(vector):
        for kv in range(NKV):
            P.wge(vector, "pK", kv + 1)
            rope_evac(vector, pk[kv % 2], 0, kT, kv * LTOK, "vK", rt1a, rt2a)

    bodies(tensor_f=p3a_pe, vector_f=p3a_vec, gpsimd_f=p3a_gp, bar=False)

    def emit_v(jts, xsrc, xW, jt_base, tag):
        lkv_b = P.val("lKV")
        pV = "pV" + tag; vV = "vV" + tag
        nj = len(jts)

        def f_gp(gpsimd):
            for vc in range(4):
                if vc >= 1:
                    P.wge(gpsimd, pV, vc * nj)
                else:
                    P.wge(gpsimd, "pK", P.val("pK"))
                    P.wge(gpsimd, "vK", P.val("vK"))
                P.inc(gpsimd.dma_start(wkv_sl[:], wv_t[vc]), "lKV", 16)

        def f_pe(tensor):
            for vc in range(4):
                P.wge(tensor, "lKV", lkv_b + (vc + 1) * 16)
                for ji, jt in enumerate(jts):
                    k = vc * nj + ji
                    if k >= 2:
                        P.wge(tensor, vV, k - 1)
                    elif tag == "b":
                        P.wge(tensor, "vVa", 15 + k)
                    col = (jt - jt_base) * 128
                    last = None
                    for ht in range(NHT):
                        last = tensor.matmul(
                            pvv[k % 2][:, 0:256],
                            xsrc[:, ht * xW + col: ht * xW + col + 128],
                            wkv_sl[:, ht * 256:(ht + 1) * 256],
                            start=(ht == 0), stop=(ht == NHT - 1))
                    P.inc(last, pV)

        def f_vec(vector):
            for vc in range(4):
                for ji, jt in enumerate(jts):
                    k = vc * nj + ji
                    P.wge(vector, pV, k + 1)
                    P.inc(vector.tensor_copy(
                        vB[:, jt * 1024 + vc * 256: jt * 1024 + (vc + 1) * 256],
                        pvv[k % 2][:, 0:256]), vV)

        return f_gp, f_pe, f_vec

    fg, fpe, fve = emit_v([0, 1, 2, 3], x1nT_h, HALO, 0, "a")
    bodies(tensor_f=fpe, vector_f=fve, gpsimd_f=fg, bar=False)

    own_rows = [(HALO + i * 128, i * 128) for i in range(8)]
    fs, fpe, fve, fae = emit_ln_tiles(
        own_rows, x1nT_o, TOK, "o",
        prewaits={
            "scalar": [("aNh", 4), ("vSh", 20), ("pTh", 4 * NHT), ("pK", 8), ("pVa", 16)],
            "sync": [("aNh", 4)],
            "vector": [("vSh", 20)],
            "tensor": [("vTh", 4 * NHT), ("pVa", 16)],
        })
    bodies(sync_f=fs, tensor_f=fpe, vector_f=fve, scalar_f=fae, bar=False)

    pk_b = P.val("pK")
    lkv_c = P.val("lKV")

    def p3b_gp(gpsimd):
        for kv in range(NKV):
            if kv >= 1:
                P.wge(gpsimd, "pK", pk_b + 2 * kv)
                P.wge(gpsimd, "vK", NKV + 2 * kv)
            else:
                P.wge(gpsimd, "pVa", 16)
                P.wge(gpsimd, "vVa", 16)
            P.inc(gpsimd.dma_start(wk_sl[:], wk_t[kv]), "lKV", 16)

    def p3b_pe(tensor):
        P.wge(tensor, "vTo", 8 * NHT)
        for kv in range(NKV):
            P.wge(tensor, "lKV", lkv_c + (kv + 1) * 16)
            for ch in range(2):
                k = 2 * kv + ch
                if k >= 2:
                    P.wge(tensor, "vK", NKV + k - 1)
                else:
                    P.wge(tensor, "vK", NKV - 2 + k + 1)
                last = None
                for ht in range(NHT):
                    last = tensor.matmul(
                        pk[k % 2][:],
                        wk_sl[:, ht * 128:(ht + 1) * 128],
                        x1nT_o[:, ht * TOK + ch * 512:ht * TOK + (ch + 1) * 512],
                        start=(ht == 0), stop=(ht == NHT - 1))
                P.inc(last, "pK")

    def p3b_vec(vector):
        for kv in range(NKV):
            for ch in range(2):
                k = 2 * kv + ch
                P.wge(vector, "pK", pk_b + k + 1)
                rope_evac(vector, pk[k % 2], HALO + ch * 512, kT,
                          kv * LTOK + HALO + ch * 512, "vK", rt1a, rt2a)

    bodies(tensor_f=p3b_pe, vector_f=p3b_vec, gpsimd_f=p3b_gp, bar=False)

    fg, fpe, fve = emit_v([4, 5, 6, 7, 8, 9, 10, 11], x1nT_o, TOK, 4, "b")
    bodies(tensor_f=fpe, vector_f=fve, gpsimd_f=fg, bar=False)


    for cm in pvv_cm[::-1]:
        cm.__exit__(None, None, None)
    for cm in pk_cm[::-1]:
        cm.__exit__(None, None, None)

    wq_sl = [P.sb(f"wq_sl{i}", [128, 8 * 128], BF16, OFF_G + i * 2 * KB) for i in range(6)]
    wqf = [P.sb(f"wqf{i}", [128, 512], F32, OFF_G + 12 * KB + i * 2 * KB) for i in range(2)]
    pq_cm = [nc.psum_tensor(f"pq{i}", [128, 512], F32) for i in range(2)]
    pq = [cm.__enter__() for cm in pq_cm]
    rt1q_cm = nc.psum_tensor("rt1q", [128, 512], F32)
    rt1q = rt1q_cm.__enter__()

    def p4_sync(sync):
        P.wge(sync, "pVb", 32)
        P.wge(sync, "vVb", 32)
        for u in range(8 * NH):
            qt, a, hf = u // 8, (u % 8) // 2, u % 2
            if u >= 2:
                P.wge(sync, "aQC", u - 1)
            P.inc(sync.dma_start(
                wqf[u % 2][:],
                wq_t[qt, a, :, hf * 512:(hf + 1) * 512]), "lqf%d" % (u % 2), 16)

    def p4_sca(scalar):
        P.wge(scalar, "pVb", 32)
        P.wge(scalar, "vK", NKV + 2 * NKV)
        for u in range(8 * NH):
            qt, a, hf = u // 8, (u % 8) // 2, u % 2
            P.wge(scalar, "lqf%d" % (u % 2), (u // 2 + 1) * 16)
            q = 4 * qt + a
            if hf == 0 and q >= 6:
                P.wge(scalar, "pQ", 2 * ((q - 6) // 4 + 1))
            P.inc(scalar.activation(
                wq_sl[q % 6][:, hf * 512:(hf + 1) * 512],
                wqf[u % 2][:], AF.Copy), "aQC")

    def p4_pe(tensor):
        P.wge(tensor, "vTo", 8 * NHT)
        P.wge(tensor, "vVb", 32)
        for qt in range(NH):
            P.wge(tensor, "aQC", 8 * (qt + 1))
            for ch in range(2):
                k = 2 * qt + ch
                if k >= 2:
                    P.wge(tensor, "vQ", k - 1)
                else:
                    P.wge(tensor, "vK", NKV + 16)
                last = None
                for ht in range(NHT):
                    last = tensor.matmul(
                        pq[k % 2][:],
                        wq_sl[(4 * qt + ht // 8) % 6][:, (ht % 8) * 128:(ht % 8 + 1) * 128],
                        x1nT_o[:, ht * TOK + ch * 512:ht * TOK + (ch + 1) * 512],
                        start=(ht == 0), stop=(ht == NHT - 1))
                P.inc(last, "pQ")

    def p4_vec(vector):
        P.wge(vector, "pVb", 32)
        P.wge(vector, "vK", NKV + 2 * NKV)
        for qt in range(NH):
            for ch in range(2):
                k = 2 * qt + ch
                P.wge(vector, "pQ", k + 1)
                rope_evac_q(vector, pq[k % 2], HALO + ch * 512, qT,
                            qt * TOK + ch * 512, "vQ", rt1q)

    bodies(sync_f=p4_sync, tensor_f=p4_pe, vector_f=p4_vec, scalar_f=p4_sca, bar=False)

    rt1q_cm.__exit__(None, None, None)
    for cm in pq_cm[::-1]:
        cm.__exit__(None, None, None)
    for cm in ptr_cm[::-1]:
        cm.__exit__(None, None, None)

    probs = [P.sb(f"probs{i}", [128, 5 * 512], BF16, OFF_G + i * 5 * KB) for i in range(2)]
    bc_sb = [P.sb(f"bc_sb{i}", [128, 512], BF16, OFF_G + 10 * KB + i * KB) for i in range(2)]
    r_sb = P.sb("r_sb", [1, 512], F32, OFF_G + 12 * KB)
    r_bf = P.sb("r_bf", [1, 512], BF16, OFF_G + 14 * KB)

    ps_s_cm = nc.psum_tensor("ps_s", [128, 5 * 512], F32)
    ps_s = ps_s_cm.__enter__()
    ps_pv_cm = nc.psum_tensor("ps_pv", [128, 512], F32)
    ps_pv = ps_pv_cm.__enter__()
    ps_sum_cm = nc.psum_tensor("ps_sum", [1, 512], F32)
    ps_sum = ps_sum_cm.__enter__()
    ps_bc_cm = nc.psum_tensor("ps_bc", [128, 512], F32)
    ps_bc = ps_bc_cm.__enter__()

    qT3 = qT.ap().rearrange("p (h t) -> p h t", t=TOK)
    attnT3 = attnT.ap().rearrange("p (h t) -> p h t", t=TOK)

    NID = NKV * 8

    def p5a_scores(tensor, idx):
        kv, qt = idx // 8, idx % 8
        rhs_q = qT3[:, kv * GQ:(kv + 1) * GQ, qt * 128:(qt + 1) * 128]
        for j in range(5):
            if idx >= 1:
                P.wge(tensor, "aEX", 5 * (idx - 1) + j + 1)
            P.inc(tensor.matmul(
                ps_s[:, j * 512:(j + 1) * 512],
                kT[:, kv * LTOK + (qt + j) * 128: kv * LTOK + (qt + j + 1) * 128],
                rhs_q, start=True, stop=True), "pSC")

    def p5a_tail(tensor, idx):
        kv, qt = idx // 8, idx % 8
        pb = probs[idx % 2]
        P.wge(tensor, "vMK", 2 * (idx + 1))
        P.wge(tensor, "vEV", idx)
        last = None
        for j in range(5):
            last = tensor.matmul(
                ps_pv[:],
                vB[:, (qt + j) * 1024 + kv * 128:(qt + j) * 1024 + (kv + 1) * 128],
                pb[:, j * 512:(j + 1) * 512],
                start=(j == 0), stop=(j == 4))
        P.inc(last, "pPV")
        P.wge(tensor, "aRC", 2 * idx)
        last = None
        for j in range(5):
            last = tensor.matmul(
                ps_sum[:],
                ones_c[:],
                pb[:, j * 512:(j + 1) * 512],
                start=(j == 0), stop=(j == 4))
        P.inc(last, "pSM")
        P.wge(tensor, "aRC", 2 * (idx + 1))
        P.wge(tensor, "aBC", idx)
        P.inc(tensor.matmul(ps_bc[:], ones_r[:], r_bf[:],
                            start=True, stop=True), "pBC")

    def p5a_pe(tensor):
        P.wge(tensor, "vQ", 2 * NH)
        p5a_scores(tensor, 0)
        for idx in range(1, NID):
            p5a_scores(tensor, idx)
            p5a_tail(tensor, idx - 1)
        p5a_tail(tensor, NID - 1)

    def p5a_sca(scalar):
        for idx in range(NID):
            for j in range(5):
                P.wge(scalar, "pSC", 5 * idx + j + 1)
                if idx >= 2 and j == 0:
                    P.wge(scalar, "pSM", idx - 1)
                P.inc(scalar.activation(
                    probs[idx % 2][:, j * 512:(j + 1) * 512],
                    ps_s[:, j * 512:(j + 1) * 512],
                    AF.Exp, bias=biask[:, idx % 8 + j:idx % 8 + j + 1], scale=SCALE),
                    "aEX")
            if idx >= 1:
                P.wge(scalar, "pBC", idx)
                if idx >= 2:
                    P.wge(scalar, "vEV", idx - 1)
                P.inc(scalar.activation(bc_sb[(idx - 1) % 2][:], ps_bc[:], AF.Copy),
                      "aBC")
            P.wge(scalar, "pSM", idx + 1)
            P.inc(scalar.activation(r_sb[:], ps_sum[:], AF.Ln, bias=c_zero[0:1, :]), "aRC")
            P.wge(scalar, "aRC", P.val("aRC"))
            P.inc(scalar.activation(r_bf[:], r_sb[:], AF.Exp, bias=c_zero[0:1, :],
                                    scale=-1.0), "aRC")
        P.wge(scalar, "pBC", NID)
        P.wge(scalar, "vEV", NID - 1)
        P.inc(scalar.activation(bc_sb[(NID - 1) % 2][:], ps_bc[:], AF.Copy), "aBC")

    def p5a_vec(vector):
        for idx in range(NID):
            kv, qt = idx // 8, idx % 8
            pb = probs[idx % 2]
            P.wge(vector, "aEX", 5 * idx + 5)
            P.inc(vector.tensor_tensor(pb[:, 0:512], pb[:, 0:512], m0[:],
                                       ALU.mult), "vMK")
            P.inc(vector.tensor_tensor(pb[:, 4 * 512:5 * 512],
                                       pb[:, 4 * 512:5 * 512], m4[:],
                                       ALU.mult), "vMK")
            P.wge(vector, "pPV", idx + 1)
            P.wge(vector, "aBC", idx + 1)
            P.inc(vector.tensor_tensor(
                attnT3[:, kv * GQ:(kv + 1) * GQ, qt * 128:(qt + 1) * 128],
                ps_pv[:], bc_sb[idx % 2][:], ALU.mult), "vEV")

    bodies(tensor_f=p5a_pe, vector_f=p5a_vec, scalar_f=p5a_sca)

    for cm in [ps_bc_cm, ps_sum_cm, ps_pv_cm, ps_s_cm]:
        cm.__exit__(None, None, None)

    wo_sl = [P.sb(f"wo_sl{i}", [128, NHT * 256], BF16, OFF_E + i * 16 * KB) for i in range(3)]
    x0c = P.sb("x0c", [128, 8 * 512], F32, OFF_G)
    po_cm = [nc.psum_tensor(f"po{i}", [128, 512], F32) for i in range(2)]
    po = [cm.__enter__() for cm in po_cm]
    def p5b_sync(sync):
        for oc in range(8):
            if oc >= 1:
                P.wge(sync, "vO", 8 * oc)
            P.inc(sync.dma_start(
                x0c.ap().rearrange("p (t q) -> p t q", q=512),
                x0s.rearrange("(t p) q -> p t q", p=128)[:, 4:12, oc * 512:(oc + 1) * 512]),
                "lxc", 16)

    def p5b_gp(gpsimd):
        h = 0
        for oc in range(8):
            for half in range(2):
                if h >= 3:
                    P.wge(gpsimd, "pO", 8 * ((h - 3) // 2) + 8)
                P.inc(gpsimd.dma_start(wo_sl[h % 3][:], wo_t[oc, half]),
                      "lo%d" % (h % 3), 16)
                h += 1

    def p5b_pe(tensor):
        for oc in range(8):
            P.wge(tensor, "lo%d" % ((2 * oc) % 3), ((2 * oc) // 3 + 1) * 16)
            P.wge(tensor, "lo%d" % ((2 * oc + 1) % 3), ((2 * oc + 1) // 3 + 1) * 16)
            for tt in range(8):
                k = 8 * oc + tt
                if k >= 2:
                    P.wge(tensor, "vO", k - 1)
                last = None
                for half in range(2):
                    sl = wo_sl[(2 * oc + half) % 3]
                    for dt in range(NHT):
                        last = tensor.matmul(
                            po[k % 2][:, half * 256:(half + 1) * 256],
                            attnT[:, dt * TOK + tt * 128:dt * TOK + (tt + 1) * 128],
                            sl[:, dt * 256:(dt + 1) * 256],
                            start=(dt == 0), stop=(dt == NHT - 1))
                P.inc(last, "pO")

    def p5b_vec(vector):
        for oc in range(8):
            for tt in range(8):
                k = 8 * oc + tt
                P.wge(vector, "pO", k + 1)
                P.wge(vector, "lxc", (oc + 1) * 16)
                P.inc(vector.tensor_tensor(
                    x1[:, tt * H + oc * 512:tt * H + (oc + 1) * 512],
                    po[k % 2][:], x0c[:, tt * 512:(tt + 1) * 512], ALU.add), "vO")

    bodies(sync_f=p5b_sync, tensor_f=p5b_pe, vector_f=p5b_vec, gpsimd_f=p5b_gp)

    for cm in po_cm[::-1]:
        cm.__exit__(None, None, None)

    x2st = [P.sb(f"x2st{i}", [128, H], BF16, OFF_E + i * 8 * KB) for i in range(2)]
    x1tst = [P.sb(f"x1tst{i}", [128, NHT * 128], BF16, OFF_E + 16 * KB + i * 8 * KB) for i in range(2)]
    junkb = [P.sb(f"junkb{i}", [128, H], BF16, OFF_F + 8 * KB + i * 8 * KB) for i in range(2)]

    ptr2_cm = [nc.psum_tensor(f"ptr2_{i}", [128, 1024], BF16) for i in range(2)]
    ptr2 = [cm.__enter__() for cm in ptr2_cm]

    def p5c_sca(scalar):
        for tt in range(8):
            st = stt[tt % 2]
            if tt >= 2:
                P.wge(scalar, "vS2", 5 * (tt - 1))
            xt = x1[:, tt * H:(tt + 1) * H]
            P.inc(scalar.activation(junkb[0][:], xt, AF.Square, bias=c_zero[:],
                                    accum_out=st[:, 1:2]), "aS2")
            P.inc(scalar.activation(junkb[1][:], xt, AF.Copy, accum_out=st[:, 0:1]), "aS2")
            P.wge(scalar, "vS2", 5 * tt + 4)
            P.inc(scalar.activation(st[:, 4:5], st[:, 3:4], AF.Ln, bias=c_eps[:]), "aL2")
            P.wge(scalar, "aL2", P.val("aL2"))
            P.inc(scalar.activation(st[:, 4:5], st[:, 4:5], AF.Exp, bias=c_zero[:],
                                    scale=-0.5), "aL2")
            P.wge(scalar, "vS2", 5 * tt + 5)
            if tt >= 2:
                P.wge(scalar, "vT2", 64 * (tt - 1))
            P.inc(scalar.activation(x2st[tt % 2][:], xt, AF.Identity,
                                    bias=st[:, 5:6], scale=st[:, 4:5]), "aN2")

    def p5c_vec(vector):
        for tt in range(8):
            st = stt[tt % 2]
            P.wge(vector, "aS2", 2 * tt + 2)
            P.inc(vector.tensor_scalar(st[:, 2:3], st[:, 0:1], 1.0 / H, None, ALU.mult), "vS2")
            P.inc(vector.tensor_scalar(st[:, 3:4], st[:, 1:2], 1.0 / H, None, ALU.mult), "vS2")
            P.wge(vector, "vS2", 5 * tt + 2)
            P.inc(vector.tensor_tensor(st[:, 6:7], st[:, 2:3], st[:, 2:3], ALU.mult), "vS2")
            P.wge(vector, "vS2", 5 * tt + 3)
            P.inc(vector.tensor_tensor(st[:, 3:4], st[:, 3:4], st[:, 6:7], ALU.subtract), "vS2")
            P.wge(vector, "aL2", 2 * tt + 2)
            P.inc(vector.tensor_scalar(st[:, 5:6], st[:, 2:3], st[:, 4:5], -1.0,
                                       ALU.mult, ALU.mult), "vS2")
            for ht in range(NHT):
                P.wge(vector, "pT2", 64 * tt + ht + 1)
                P.inc(vector.tensor_copy(
                    x2nT[:, ht * TOK + tt * 128:ht * TOK + (tt + 1) * 128],
                    ptr2[(64 * tt + ht) % 2][:, 0:128]), "vT2")
            for ht in range(NHT):
                P.wge(vector, "pT2", 64 * tt + 32 + ht + 1)
                P.inc(vector.tensor_copy(
                    x1tst[tt % 2][:, ht * 128:(ht + 1) * 128],
                    ptr2[(64 * tt + 32 + ht) % 2][:, 0:128]), "vT2")


    def p5c_pe(tensor):
        for tt in range(8):
            P.wge(tensor, "aN2", tt + 1)
            for ht in range(NHT):
                if 64 * tt + ht >= 2:
                    P.wge(tensor, "vT2", 64 * tt + ht - 1)
                P.inc(tensor.transpose(ptr2[(64 * tt + ht) % 2][:, 0:128],
                                       x2st[tt % 2][:, ht * 128:(ht + 1) * 128],
                                       ident[:]), "pT2")
            if tt >= 2:
                P.wge(tensor, "sx%d" % (tt % 2), (tt // 2) * 32)
            for ht in range(NHT):
                P.wge(tensor, "vT2", 64 * tt + 32 + ht - 1)
                P.inc(tensor.transpose(ptr2[(64 * tt + 32 + ht) % 2][:, 0:128],
                                       x1[:, tt * H + ht * 128:tt * H + (ht + 1) * 128],
                                       ident[:]), "pT2")

    def p5c_gp(gpsimd):
        for tt in range(8):
            P.wge(gpsimd, "vT2", 64 * tt + 64)
            for hh in range(2):
                P.inc(gpsimd.dma_start(
                    scr_x1t.rearrange("(a p) t -> p a t", p=128)[:, hh * 16:(hh + 1) * 16,
                                                                tt * 128:(tt + 1) * 128],
                    x1tst[tt % 2].ap().rearrange("p (a q) -> p a q", q=128)[:, hh * 16:(hh + 1) * 16, :]),
                    "sx%d" % (tt % 2), 16)
        for s in ["sx0", "sx1"]:
            P.wge(gpsimd, s, P.val(s))

    bodies(tensor_f=p5c_pe, vector_f=p5c_vec, scalar_f=p5c_sca, gpsimd_f=p5c_gp)

    for cm in ptr2_cm[::-1]:
        cm.__exit__(None, None, None)

    wgu_sl = [P.sb(f"wgu_sl{i}", [128, NHT * 128], BF16, OFF_F + i * 8 * KB) for i in range(4)]
    wgu_sl[3] = P.sb("wgu_sl3", [128, NHT * 128], BF16, OFF_G)
    sil_st = [P.sb(f"sil_st{i}", [128, 512], F32, OFF_G + 8 * KB + i * 2 * KB) for i in range(2)]
    wd_sl = [P.sb(f"wd_sl{i}", [128, 8 * 1024], BF16, OFF_F + i * 16 * KB) for i in range(2)]
    x1t_sl = [P.sb(f"x1t_sl{i}", [128, 512], BF16, OFF_G + 8 * KB + i * KB) for i in range(2)]
    out_st = [P.sb(f"out_st{i}", [128, 512], F32, OFF_G + 10 * KB + i * 2 * KB) for i in range(2)]

    NGRP = 11

    for tc in range(2):
        pg_cm = [nc.psum_tensor(f"pgg{tc}_{i}", [128, 512], F32) for i in range(2)]
        pgg = [cm.__enter__() for cm in pg_cm]
        pu_cm = [nc.psum_tensor(f"puu{tc}_{i}", [128, 512], F32) for i in range(2)]
        puu = [cm.__enter__() for cm in pu_cm]
        tg_base = [P.val("tg0"), P.val("tg1")]
        tu_base = [P.val("tu0"), P.val("tu1")]

        def p6gu_sync(sync, tc=tc):
            for it in range(NIT):
                if it >= 2:
                    P.wge(sync, "pGU", tc * NIT + it - 1)
                P.inc(sync.dma_start(
                    wgu_sl[2 * (it % 2)].ap().rearrange("p (t q) -> p t q", q=128),
                    scr_g[it * 128:(it + 1) * 128, :], transpose=True),
                    "tg%d" % (it % 2), 16)
                P.inc(sync.dma_start(
                    wgu_sl[2 * (it % 2) + 1].ap().rearrange("p (t q) -> p t q", q=128),
                    scr_u[it * 128:(it + 1) * 128, :], transpose=True),
                    "tu%d" % (it % 2), 16)

        def p6gu_pe(tensor, tc=tc, tg_base=tg_base, tu_base=tu_base):
            for it in range(NIT):
                P.wge(tensor, "tg%d" % (it % 2), tg_base[it % 2] + (it // 2 + 1) * 16)
                P.wge(tensor, "tu%d" % (it % 2), tu_base[it % 2] + (it // 2 + 1) * 16)
                if it >= 2:
                    P.wge(tensor, "vH", tc * NIT + it - 1)
                last = None
                for ht in range(NHT):
                    last = tensor.matmul(
                        pgg[it % 2][:],
                        wgu_sl[2 * (it % 2)][:, ht * 128:(ht + 1) * 128],
                        x2nT[:, ht * TOK + tc * 512:ht * TOK + (tc + 1) * 512],
                        start=(ht == 0), stop=(ht == NHT - 1))
                P.inc(last, "pGs")
                last = None
                for ht in range(NHT):
                    last = tensor.matmul(
                        puu[it % 2][:],
                        wgu_sl[2 * (it % 2) + 1][:, ht * 128:(ht + 1) * 128],
                        x2nT[:, ht * TOK + tc * 512:ht * TOK + (tc + 1) * 512],
                        start=(ht == 0), stop=(ht == NHT - 1))
                P.inc(last, "pGU")

        def p6gu_sca(scalar, tc=tc):
            for it in range(NIT):
                P.wge(scalar, "pGs", tc * NIT + it + 1)
                if it >= 2:
                    P.wge(scalar, "vH", tc * NIT + it - 1)
                P.inc(scalar.activation(sil_st[it % 2][:], pgg[it % 2][:], AF.Silu,
                                        bias=c_zero[:]), "aSI")

        def p6gu_vec(vector, tc=tc):
            for it in range(NIT):
                P.wge(vector, "aSI", tc * NIT + it + 1)
                P.wge(vector, "pGU", tc * NIT + it + 1)
                P.inc(vector.tensor_tensor(
                    hT[:, it * 512:(it + 1) * 512],
                    sil_st[it % 2][:], puu[it % 2][:], ALU.mult), "vH")

        bodies(sync_f=p6gu_sync, tensor_f=p6gu_pe, vector_f=p6gu_vec, scalar_f=p6gu_sca)

        for cm in pu_cm[::-1]:
            cm.__exit__(None, None, None)
        for cm in pg_cm[::-1]:
            cm.__exit__(None, None, None)

        pd_cm = nc.psum_tensor(f"pd{tc}", [128, 8 * 512], F32)
        pd = pd_cm.__enter__()
        ld_base = [P.val("ld0"), P.val("ld1")]
        gy_base = [P.val("gy0"), P.val("gy1")]
        lx_base = [P.val("lx0"), P.val("lx1")]
        vdo_d = P.val("vDO")
        pd_d = P.val("pD")

        def p6d_sync(sync, tc=tc, pd_d=pd_d):
            for og in range(4):
                for grp in range(NGRP):
                    nit_g = 8 if grp < 10 else 6
                    it0 = grp * 8
                    k = og * NGRP + grp
                    if k >= 2:
                        P.wge(sync, "pD", pd_d + k - 1)
                    P.inc(sync.dma_start(
                        wd_sl[k % 2].ap().rearrange("p (a q) -> p a q", q=1024)[:, 0:nit_g, :],
                        scr_d2.rearrange("a p q -> p a q")[:, it0:it0 + nit_g,
                                                          og * 1024:(og + 1) * 1024]),
                        "ld%d" % (k % 2), 16)

        def p6d_pe(tensor, tc=tc, ld_base=ld_base, vdo_d=vdo_d):
            for og in range(4):
                if og >= 1:
                    P.wge(tensor, "vDO", vdo_d + og * 8)
                for grp in range(NGRP):
                    nit_g = 8 if grp < 10 else 6
                    it0 = grp * 8
                    k = og * NGRP + grp
                    P.wge(tensor, "ld%d" % (k % 2), ld_base[k % 2] + (k // 2 + 1) * 16)
                    last = None
                    for gi in range(nit_g):
                        it = it0 + gi
                        for oi in range(8):
                            last = tensor.matmul(
                                pd[:, oi * 512:(oi + 1) * 512],
                                wd_sl[(og * NGRP + grp) % 2][:, gi * 1024 + oi * 128:
                                                             gi * 1024 + (oi + 1) * 128],
                                hT[:, it * 512:(it + 1) * 512],
                                start=(it == 0), stop=(it == NIT - 1))
                    P.inc(last, "pD")

        def p6d_vec(vector, tc=tc, gy_base=gy_base, lx_base=lx_base, pd_d=pd_d):
            for og in range(4):
                P.wge(vector, "pD", pd_d + (og + 1) * NGRP)
                for oi in range(8):
                    k = og * 8 + oi
                    P.wge(vector, "lx%d" % (k % 2), lx_base[k % 2] + (k // 2 + 1) * 16)
                    if k >= 2:
                        P.wge(vector, "gy%d" % (k % 2), gy_base[k % 2] + (k // 2) * 16)
                    P.inc(vector.tensor_tensor(
                        out_st[k % 2][:], pd[:, oi * 512:(oi + 1) * 512],
                        x1t_sl[k % 2][:], ALU.add), "vDO")

        def p6d_gp(gpsimd, tc=tc, vdo_d=vdo_d):
            for og in range(4):
                for oi in range(8):
                    k = og * 8 + oi
                    if k >= 2:
                        P.wge(gpsimd, "vDO", vdo_d + k - 1)
                    ot = og * 8 + oi
                    P.inc(gpsimd.dma_start(
                        x1t_sl[k % 2][:],
                        scr_x1t[ot * 128:(ot + 1) * 128, tc * 512:(tc + 1) * 512]),
                        "lx%d" % (k % 2), 16)
                    P.wge(gpsimd, "vDO", vdo_d + k + 1)
                    P.inc(gpsimd.dma_start(
                        y_t[ot * 128:(ot + 1) * 128, tc * 512:(tc + 1) * 512],
                        out_st[k % 2][:]), "gy%d" % (k % 2), 16)

        bodies(sync_f=p6d_sync, tensor_f=p6d_pe, vector_f=p6d_vec, gpsimd_f=p6d_gp)
        pd_cm.__exit__(None, None, None)

    def fin_gp(gpsimd):
        for s in ["gy0", "gy1"]:
            P.wge(gpsimd, s, P.val(s))

    bodies(gpsimd_f=fin_gp, bar=False)

    block_cm.__exit__(None, None, None)
    nc.finalize()
    return nc



_CACHE = {}


def _host_inputs(inputs):
    x0 = np.ascontiguousarray(np.asarray(inputs["L0_attn_hidden_states"],
                                         dtype=np.float32).reshape(S, H))
    wq = np.ascontiguousarray(np.asarray(inputs["L0_attn_Wq"], dtype=np.float32))
    wk = np.ascontiguousarray(np.asarray(inputs["L0_attn_Wk"], dtype=np.float32))
    wv = np.ascontiguousarray(np.asarray(inputs["L0_attn_Wv"], dtype=np.float32))
    wo = np.ascontiguousarray(np.asarray(inputs["L0_attn_Wo"], dtype=np.float32))

    wq_t = np.ascontiguousarray(
        wq.reshape(4, 8, 128, NH, 128).transpose(3, 0, 2, 1, 4)).reshape(NH, 4, 128, 1024)
    wk_t = np.ascontiguousarray(
        wk.reshape(32, 128, 8, 128).transpose(2, 1, 0, 3)).reshape(8, 128, 32 * 128)
    wv_t = np.ascontiguousarray(
        wv.reshape(32, 128, 4, 256).transpose(2, 1, 0, 3)).reshape(4, 128, 32 * 256)
    wo_t = np.ascontiguousarray(
        wo.reshape(32, 128, 8, 2, 256).transpose(2, 3, 1, 0, 4)).reshape(8, 2, 128, 32 * 256)

    qg = np.ascontiguousarray(np.asarray(inputs["L0_ffn_qw_gate"], dtype=np.int32))
    qu = np.ascontiguousarray(np.asarray(inputs["L0_ffn_qw_up"], dtype=np.int32))
    qd = np.ascontiguousarray(np.asarray(inputs["L0_ffn_qw_down"], dtype=np.int32))
    sg = np.ascontiguousarray(np.asarray(inputs["L0_ffn_s_gate"], dtype=np.float32))
    zg = np.ascontiguousarray(np.asarray(inputs["L0_ffn_z_gate"], dtype=np.float32))
    su = np.ascontiguousarray(np.asarray(inputs["L0_ffn_s_up"], dtype=np.float32))
    zu = np.ascontiguousarray(np.asarray(inputs["L0_ffn_z_up"], dtype=np.float32))
    sd = np.ascontiguousarray(np.asarray(inputs["L0_ffn_s_down"], dtype=np.float32))
    zd = np.ascontiguousarray(np.asarray(inputs["L0_ffn_z_down"], dtype=np.float32))

    ident = np.eye(128, dtype=np.float32).astype(ml_dtypes.bfloat16)
    r = np.arange(128)
    m0b = (r[None, :] <= r[:, None]).astype(np.float32)
    m4b = (r[None, :] >= r[:, None]).astype(np.float32)
    m0 = np.ascontiguousarray(np.tile(m0b[:, None, :], (1, 4, 1)).reshape(128, 512)
                              ).astype(ml_dtypes.bfloat16)
    m4 = np.ascontiguousarray(np.tile(m4b[:, None, :], (1, 4, 1)).reshape(128, 512)
                              ).astype(ml_dtypes.bfloat16)

    inv = 1.0 / THETA ** (np.arange(0, HD, 2, dtype=np.float64) / HD)

    in_maps = []
    for c in range(NCORES):
        lo = c * TOK - HALO
        if lo < 0:
            xs = np.concatenate([np.zeros((-lo, H), np.float32), x0[:c * TOK + TOK]], axis=0)
        else:
            xs = x0[lo:c * TOK + TOK]
        xs = np.ascontiguousarray(xs)

        pos = np.maximum(np.arange(lo, c * TOK + TOK, dtype=np.float64), 0.0)
        ang = pos[None, :] * inv[:, None]
        cosv = np.cos(ang)
        sinv = np.sin(ang)
        csc = np.ascontiguousarray(np.concatenate([cosv, cosv], axis=0).astype(np.float32))
        css = np.ascontiguousarray(np.concatenate([-sinv, sinv], axis=0).astype(np.float32))

        bk = np.full((128, 12), -CSH, dtype=np.float32)
        if c == 0:
            bk[:, 0:4] = -1e30
        in_maps.append({
            "x0s": xs, "wq_t": wq_t, "wk_t": wk_t, "wv_t": wv_t, "wo_t": wo_t,
            "csc": csc, "css": css, "bk_in": bk, "m0_in": m0, "m4_in": m4,
            "id_in": ident,
            "qg_in": qg, "qu_in": qu, "qd_in": qd,
            "sg_in": sg, "zg_in": zg, "su_in": su, "zu_in": zu,
            "sd_in": sd, "zd_in": zd,
        })
    return in_maps


def kernel(**inputs):
    if "nc" not in _CACHE:
        _CACHE["nc"] = build_program()
    nc = _CACHE["nc"]
    in_maps = _host_inputs(inputs)
    res = run_bass_kernel_spmd(nc, in_maps, core_ids=list(range(NCORES)))
    parts = [np.asarray(res.results[c]["y_t"], dtype=np.float32).T for c in range(NCORES)]
    return np.concatenate(parts, axis=0).reshape(1, S, H)


# version 14
# speedup vs baseline: 1.0211x; 1.0211x over previous
import math
import numpy as np
import ml_dtypes
import concourse.bass as bass
import concourse.mybir as mybir
from concourse.bass_utils import run_bass_kernel_spmd

F32 = mybir.dt.float32
BF16 = mybir.dt.bfloat16
I32 = mybir.dt.int32
AF = mybir.ActivationFunctionType
ALU = mybir.AluOpType

NCORES = 8
S = 8192
H = 4096
TOK = 1024
HALO = 512
LTOK = TOK + HALO
NHT = H // 128
NH = 32
NKV = 8
HD = 128
GQ = NH // NKV
II = 11008
NIT = II // 128
SCALE = 1.0 / math.sqrt(HD)
CSH = 12.0
EPS = 1e-5
THETA = 10000.0
KB = 1024


class Prog:
    def __init__(self):
        self.nc = bass.Bass("TRN2", target_bir_lowering=False)
        self.sems = {}
        self.cnt = {}
        self.eng_sems = {}

    def sem(self, name):
        if name not in self.sems:
            self.sems[name] = self.nc.alloc_semaphore(name)
            self.cnt[name] = 0
        return self.sems[name]

    def inc(self, instr, name, n=1):
        instr.then_inc(self.sem(name), n)
        self.cnt[name] += n
        eng = str(instr.ins.engine)
        self.eng_sems.setdefault(eng, set()).add(name)
        return self.cnt[name]

    def val(self, name):
        self.sem(name)
        return self.cnt[name]

    def wge(self, eng, name, v):
        eng.wait_ge(self.sem(name), v)

    SBUF_BASE = 16512

    def sb(self, name, shape, dtype, off):
        return self.nc.alloc_sbuf_tensor_at(name, shape, dtype,
                                            offset=self.SBUF_BASE + off)


def build_program():
    P = Prog()
    nc = P.nc

    din = {}
    def dram_in(name, shape, dt):
        din[name] = nc.dram_tensor(name, list(shape), dt, kind="ExternalInput").ap()
        return din[name]

    x0s = dram_in("x0s", [LTOK, H], F32)
    wq_t = dram_in("wq_t", [NH, 4, 128, 1024], F32)
    wk_t = dram_in("wk_t", [NKV, 128, NHT * 128], F32)
    wv_t = dram_in("wv_t", [4, 128, NHT * 256], F32)
    wo_t = dram_in("wo_t", [8, 2, 128, NHT * 256], F32)
    csc_in = dram_in("csc", [128, LTOK], F32)
    css_in = dram_in("css", [128, LTOK], F32)
    bk_in = dram_in("bk_in", [128, 12], F32)
    m0_in = dram_in("m0_in", [128, 512], BF16)
    m4_in = dram_in("m4_in", [128, 512], BF16)
    id_in = dram_in("id_in", [128, 128], BF16)
    qg_in = dram_in("qg_in", [II, H], I32)
    qu_in = dram_in("qu_in", [II, H], I32)
    qd_in = dram_in("qd_in", [H, II], I32)
    sg_in = dram_in("sg_in", [II, 32], F32)
    zg_in = dram_in("zg_in", [II, 32], F32)
    su_in = dram_in("su_in", [II, 32], F32)
    zu_in = dram_in("zu_in", [II, 32], F32)
    sd_in = dram_in("sd_in", [H, NIT], F32)
    zd_in = dram_in("zd_in", [H, NIT], F32)

    y_t = nc.dram_tensor("y_t", [H, TOK], F32, kind="ExternalOutput").ap()

    scr_g = nc.dram_tensor("scr_g", [II, H], BF16, kind="Internal").ap()
    scr_u = nc.dram_tensor("scr_u", [II, H], BF16, kind="Internal").ap()
    scr_d2 = nc.dram_tensor("scr_d2", [NIT, 128, H], BF16, kind="Internal").ap()
    scr_x1t = nc.dram_tensor("scr_x1t", [H, TOK], BF16, kind="Internal").ap()

    cs_cos = P.sb("cs_cos", [128, LTOK], F32, 0)
    cs_sin = P.sb("cs_sin", [128, LTOK], F32, 6144)
    m0 = P.sb("m0", [128, 512], BF16, 12288)
    m4 = P.sb("m4", [128, 512], BF16, 13312)
    biask = P.sb("biask", [128, 12], F32, 14336)
    ident = P.sb("ident", [128, 128], BF16, 14400)
    ones_r = P.sb("ones_r", [1, 128], BF16, 14656)
    ones_c = P.sb("ones_c", [128, 1], BF16, 14912)
    c_zero = P.sb("c_zero", [128, 1], F32, 14944)
    c_eps = P.sb("c_eps", [128, 1], F32, 14976)
    OFF_A = 15 * KB
    OFF_D = 79 * KB
    OFF_E = 143 * KB
    OFF_F = 167 * KB
    OFF_G = 191 * KB

    x1nT_h = P.sb("x1nT_h", [128, NHT * HALO], BF16, OFF_A)
    x1nT_o = P.sb("x1nT_o", [128, NHT * TOK], BF16, OFF_A)
    attnT = P.sb("attnT", [128, NH * TOK], BF16, OFF_A)
    x2nT = P.sb("x2nT", [128, NHT * TOK], BF16, OFF_A)
    qT = P.sb("qT", [128, NH * TOK], BF16, OFF_D)
    x1 = P.sb("x1", [128, 8 * H], BF16, OFF_D)
    hT = P.sb("hT", [128, NIT * 512], BF16, OFF_D)
    kT = P.sb("kT", [128, NKV * LTOK], BF16, OFF_E)
    vB = P.sb("vB", [128, 12 * 1024], BF16, OFF_F)

    bar_n = [0]

    block_cm = nc.Block()
    block = block_cm.__enter__()

    def bodies(sync_f=None, tensor_f=None, vector_f=None, scalar_f=None,
               gpsimd_f=None, bar=True):
        def drain(eng, ename):
            for s in sorted(P.eng_sems.get(ename, ())):
                P.wge(eng, s, P.val(s))
            P.eng_sems[ename] = set()
        @block.sync
        def _(sync):
            P.wge(sync, "BAR", bar_n[0])
            if sync_f:
                sync_f(sync)
            if bar:
                drain(sync, "EngineType.SP")
                P.inc(sync.nop(), "BAR")
        @block.tensor
        def _(tensor):
            P.wge(tensor, "BAR", bar_n[0])
            if tensor_f:
                tensor_f(tensor)
            if bar:
                drain(tensor, "EngineType.PE")
                P.inc(tensor.nop(), "BAR")
        @block.vector
        def _(vector):
            P.wge(vector, "BAR", bar_n[0])
            if vector_f:
                vector_f(vector)
            if bar:
                drain(vector, "EngineType.DVE")
                P.inc(vector.nop(), "BAR")
        @block.scalar
        def _(scalar):
            P.wge(scalar, "BAR", bar_n[0])
            if scalar_f:
                scalar_f(scalar)
            if bar:
                drain(scalar, "EngineType.Activation")
                P.inc(scalar.nop(), "BAR")
        @block.gpsimd
        def _(gpsimd):
            P.wge(gpsimd, "BAR", bar_n[0])
            if gpsimd_f:
                gpsimd_f(gpsimd)
            if bar:
                drain(gpsimd, "EngineType.Pool")
                P.inc(gpsimd.nop(), "BAR")
        if bar:
            bar_n[0] = P.val("BAR")

    def p0_sync(sync):
        for dst, src in [(cs_cos, csc_in), (cs_sin, css_in), (m0, m0_in),
                         (m4, m4_in), (biask, bk_in), (ident, id_in)]:
            P.inc(sync.dma_start(dst[:], src[:]), "cLD", 16)
        P.wge(sync, "cLD", P.val("cLD"))

    def p0_vec(vector):
        vector.memset(c_zero[:], 0.0)
        vector.memset(c_eps[:], EPS)
        vector.memset(ones_r[:], 1.0)
        P.inc(vector.memset(ones_c[:], 1.0), "vP")
        P.wge(vector, "vP", P.val("vP"))

    bodies(sync_f=p0_sync, vector_f=p0_vec)

    qsl_g = [P.sb(f"qsl_g{i}", [128, H], I32, OFF_D + i * 16 * KB) for i in range(3)]
    qsl_u = [P.sb(f"qsl_u{i}", [128, H], I32, [OFF_D + 48 * KB, OFF_E, OFF_E + 16 * KB][i]) for i in range(3)]
    wb_g = [P.sb(f"wb_g{i}", [128, H], BF16, OFF_A + 44 * KB + i * 8 * KB) for i in range(2)]
    wb_u = [P.sb(f"wb_u{i}", [128, H], BF16, OFF_G + i * 8 * KB) for i in range(2)]
    s_g = P.sb("s_g", [128, NIT * 32], F32, OFF_A)
    nzs_g = P.sb("nzs_g", [128, NIT * 32], F32, OFF_A + 11 * KB)
    s_u = P.sb("s_u", [128, NIT * 32], F32, OFF_A + 22 * KB)
    nzs_u = P.sb("nzs_u", [128, NIT * 32], F32, OFF_A + 33 * KB)

    vp0 = P.val("vP")
    USPL = 22

    def p1gu_sync(sync):
        for dst, src in [(s_g, sg_in), (nzs_g, zg_in), (s_u, su_in), (nzs_u, zu_in)]:
            P.inc(sync.dma_start(
                dst.ap().rearrange("p (a b) -> p a b", b=32),
                src.rearrange("(a p) b -> p a b", p=128)), "szA", 16)
        for i in range(NIT):
            if i >= 3:
                P.wge(sync, "vDq", i - 2)
            P.inc(sync.dma_start(qsl_g[i % 3][:], qg_in[i * 128:(i + 1) * 128, :]),
                  "qg%d" % (i % 3), 16)
            if i >= 3:
                P.wge(sync, "aDq", i - 2)
                P.wge(sync, "uDv", i - 2)
            P.inc(sync.dma_start(qsl_u[i % 3][:], qu_in[i * 128:(i + 1) * 128, :]),
                  "qu%d" % (i % 3), 16)

    def p1gu_vec(vector):
        P.wge(vector, "szA", 64)
        P.inc(vector.tensor_scalar(nzs_u[:], nzs_u[:], -1.0, None, ALU.mult), "vP")
        P.wge(vector, "vP", P.val("vP"))
        P.inc(vector.tensor_tensor(nzs_u[:], nzs_u[:], s_u[:], ALU.mult), "vP")
        P.inc(vector.tensor_scalar(nzs_g[:], nzs_g[:], -1.0, None, ALU.mult), "vP")
        P.wge(vector, "vP", P.val("vP"))
        P.inc(vector.tensor_tensor(nzs_g[:], nzs_g[:], s_g[:], ALU.mult), "vP")
        P.wge(vector, "vP", P.val("vP"))
        sg3 = s_g.ap().rearrange("p (a b) -> p a b", b=32)
        ng3 = nzs_g.ap().rearrange("p (a b) -> p a b", b=32)
        su3 = s_u.ap().rearrange("p (a b) -> p a b", b=32)
        nu3 = nzs_u.ap().rearrange("p (a b) -> p a b", b=32)
        for i in range(NIT):
            P.wge(vector, "qg%d" % (i % 3), (i // 3 + 1) * 16)
            if i >= 2:
                P.wge(vector, "wg%d" % (i % 2), ((i - 2) // 2 + 1) * 16)
            last = None
            for g in range(32):
                last = vector.tensor_scalar(
                    wb_g[i % 2][:, g * 128:(g + 1) * 128],
                    qsl_g[i % 3][:, g * 128:(g + 1) * 128],
                    sg3[:, i, g:g + 1], ng3[:, i, g:g + 1],
                    ALU.mult, ALU.add)
            P.inc(last, "vDq")
            P.wge(vector, "qu%d" % (i % 3), (i // 3 + 1) * 16)
            if i >= 2:
                P.wge(vector, "wu%d" % (i % 2), ((i - 2) // 2 + 1) * 16)
            last = None
            for g in range(USPL, 32):
                last = vector.tensor_scalar(
                    wb_u[i % 2][:, g * 128:(g + 1) * 128],
                    qsl_u[i % 3][:, g * 128:(g + 1) * 128],
                    su3[:, i, g:g + 1], nu3[:, i, g:g + 1],
                    ALU.mult, ALU.add)
            P.inc(last, "uDv")

    def p1gu_sca(scalar):
        P.wge(scalar, "szA", 64)
        P.wge(scalar, "vP", vp0 + 2)
        su3 = s_u.ap().rearrange("p (a b) -> p a b", b=32)
        nu3 = nzs_u.ap().rearrange("p (a b) -> p a b", b=32)
        for i in range(NIT):
            P.wge(scalar, "qu%d" % (i % 3), (i // 3 + 1) * 16)
            if i >= 2:
                P.wge(scalar, "wu%d" % (i % 2), ((i - 2) // 2 + 1) * 16)
            last = None
            for g in range(USPL):
                last = scalar.activation(
                    wb_u[i % 2][:, g * 128:(g + 1) * 128],
                    qsl_u[i % 3][:, g * 128:(g + 1) * 128],
                    AF.Identity, bias=nu3[:, i, g:g + 1], scale=su3[:, i, g:g + 1])
            P.inc(last, "aDq")
            P.wge(scalar, "aDq", i + 1)
            P.wge(scalar, "uDv", i + 1)
            P.inc(scalar.dma_start(scr_u[i * 128:(i + 1) * 128, :], wb_u[i % 2][:]),
                  "wu%d" % (i % 2), 16)
            P.wge(scalar, "vDq", i + 1)
            P.inc(scalar.dma_start(scr_g[i * 128:(i + 1) * 128, :], wb_g[i % 2][:]),
                  "wg%d" % (i % 2), 16)

    bodies(sync_f=p1gu_sync, vector_f=p1gu_vec, scalar_f=p1gu_sca)

    DS = 16
    qsl_d = [P.sb(f"qsl_d{j}", [128, NHT * 128], I32, OFF_D + j * 16 * KB) for j in range(2)]
    wb_d = [P.sb(f"wb_d{j}", [128, H], BF16, OFF_D + 32 * KB + j * 8 * KB) for j in range(2)]
    sst = [P.sb(f"sst{j}", [128, H], BF16, OFF_D + 48 * KB + j * 8 * KB) for j in range(2)]
    s_d = P.sb("s_d", [128, NHT * NIT], F32, OFF_A + 42 * KB)
    nzs_d = P.sb("nzs_d", [128, NHT * NIT], F32, OFF_A + 53 * KB)

    stt = [P.sb(f"stt{i}", [128, 8], F32, 15008 + i * 64) for i in range(2)]
    x0st_h = P.sb("x0st_h", [128, H], F32, OFF_E)
    junk_h = [P.sb(f"junk_h{i}", [128, H], BF16, [OFF_E + 16 * KB, OFF_F][i])
              for i in range(2)]
    xnst_h = P.sb("xnst_h", [128, H], BF16, OFF_F + 8 * KB)

    ptr_cm = [nc.psum_tensor(f"ptr{i}", [128, 1024], BF16) for i in range(2)]
    ptr = [cm.__enter__() for cm in ptr_cm]
    ptd_cm = [nc.psum_tensor(f"ptd{j}", [128, 2048], BF16) for j in range(2)]
    ptd = [cm.__enter__() for cm in ptd_cm]
    qd3 = qd_in.rearrange("(a p) q -> p a q", p=128)

    def p2a_sca_a(scalar, t):
        st = stt[t % 2]
        P.wge(scalar, "x0h", (t + 1) * 16)
        if t >= 2:
            P.wge(scalar, "vSh", 5 * (t - 1))
        P.inc(scalar.activation(junk_h[0][:], x0st_h[:], AF.Square,
                                bias=c_zero[:], accum_out=st[:, 1:2]), "aSh")
        P.inc(scalar.activation(junk_h[1][:], x0st_h[:], AF.Copy,
                                accum_out=st[:, 0:1]), "aSh")

    def p2a_sca_b(scalar, t):
        st = stt[t % 2]
        P.wge(scalar, "vSh", 5 * t + 4)
        P.inc(scalar.activation(st[:, 4:5], st[:, 3:4], AF.Ln, bias=c_eps[:]), "aLh")
        P.wge(scalar, "aLh", P.val("aLh"))
        P.inc(scalar.activation(st[:, 4:5], st[:, 4:5], AF.Exp, bias=c_zero[:],
                                scale=-0.5), "aLh")

    def p2a_sca_c(scalar, t):
        st = stt[t % 2]
        P.wge(scalar, "vSh", 5 * t + 5)
        if t >= 1:
            P.wge(scalar, "pTh", 32 * t)
        P.inc(scalar.activation(xnst_h[:], x0st_h[:], AF.Identity,
                                bias=st[:, 5:6], scale=st[:, 4:5]), "aNh")

    def p2a_vec_a(vector, t):
        st = stt[t % 2]
        P.wge(vector, "aSh", 2 * t + 2)
        P.inc(vector.tensor_scalar(st[:, 2:3], st[:, 0:1], 1.0 / H, None,
                                   ALU.mult), "vSh")
        P.inc(vector.tensor_scalar(st[:, 3:4], st[:, 1:2], 1.0 / H, None,
                                   ALU.mult), "vSh")
        P.wge(vector, "vSh", 5 * t + 2)
        P.inc(vector.tensor_tensor(st[:, 6:7], st[:, 2:3], st[:, 2:3], ALU.mult), "vSh")
        P.wge(vector, "vSh", 5 * t + 3)
        P.inc(vector.tensor_tensor(st[:, 3:4], st[:, 3:4], st[:, 6:7],
                                   ALU.subtract), "vSh")

    def p2a_vec_b(vector, t):
        st = stt[t % 2]
        P.wge(vector, "aLh", 2 * t + 2)
        P.inc(vector.tensor_scalar(st[:, 5:6], st[:, 2:3], st[:, 4:5], -1.0,
                                   ALU.mult, ALU.mult), "vSh")

    def p2a_vec_c(vector, t):
        for ht in range(NHT):
            P.wge(vector, "pTh", 32 * t + ht + 1)
            P.inc(vector.tensor_copy(
                x1nT_h[:, ht * HALO + t * 128: ht * HALO + t * 128 + 128],
                ptr[(32 * t + ht) % 2][:, 0:128]), "vTh")

    def p2a_pe(tensor, t):
        P.wge(tensor, "aNh", t + 1)
        for ht in range(NHT):
            if 32 * t + ht >= 2:
                P.wge(tensor, "vTh", 32 * t + ht - 1)
            P.inc(tensor.transpose(ptr[(32 * t + ht) % 2][:, 0:128],
                                   xnst_h[:, ht * 128:(ht + 1) * 128], ident[:]), "pTh")

    sca_pieces = {}
    vec_pieces = {}
    pe_pieces = {}
    for t in range(4):
        sca_pieces[36 + 10 * t] = [(p2a_sca_a, t)]
        sca_pieces[38 + 10 * t] = [(p2a_sca_b, t)]
        sca_pieces[40 + 10 * t] = [(p2a_sca_c, t)]
        vec_pieces[37 + 10 * t] = [(p2a_vec_a, t)]
        vec_pieces[39 + 10 * t] = [(p2a_vec_b, t)]
        vec_pieces[42 + 10 * t] = [(p2a_vec_c, t)]
        pe_pieces[41 + 10 * t] = [(p2a_pe, t)]

    def p1d_sync(sync):
        for dst, src in [(s_d, sd_in), (nzs_d, zd_in)]:
            P.inc(sync.dma_start(
                dst.ap().rearrange("p (a b) -> p a b", b=NIT),
                src.rearrange("(a p) b -> p a b", p=128)), "szD", 16)
        for it in range(NIT):
            if it >= 2:
                P.wge(sync, "vDd", it - 1)
                P.wge(sync, "aDd", it - 1)
            P.inc(sync.dma_start(
                qsl_d[it % 2].ap().rearrange("p (a q) -> p a q", q=128),
                qd3[:, :, it * 128:(it + 1) * 128]), "qd%d" % (it % 2), 16)

    def p1d_gp(gpsimd):
        for t in range(4):
            if t >= 1:
                P.wge(gpsimd, "aNh", t)
            P.inc(gpsimd.dma_start(x0st_h[:], x0s[t * 128:(t + 1) * 128, :]),
                  "x0h", 16)

    def p1d_pe(tensor):
        for it in range(NIT):
            for f, t in pe_pieces.get(it, ()):
                f(tensor, t)
            P.wge(tensor, "vDd", it + 1)
            if it >= 1:
                P.wge(tensor, "vEv", it)
            last = None
            for hs in range(DS):
                last = tensor.transpose(
                    ptd[0][:, hs * 128:(hs + 1) * 128],
                    wb_d[it % 2][:, hs * 128:(hs + 1) * 128], ident[:])
            P.inc(last, "pTd")
            P.wge(tensor, "aDd", it + 1)
            if it >= 1:
                P.wge(tensor, "aEv", it)
            last = None
            for hs in range(DS, NHT):
                last = tensor.transpose(
                    ptd[1][:, (hs - DS) * 128:(hs - DS + 1) * 128],
                    wb_d[it % 2][:, hs * 128:(hs + 1) * 128], ident[:])
            P.inc(last, "pTd")

    def p1d_vec(vector):
        P.wge(vector, "szD", 32)
        P.inc(vector.tensor_scalar(nzs_d[:], nzs_d[:], -1.0, None, ALU.mult), "vP")
        P.wge(vector, "vP", P.val("vP"))
        P.inc(vector.tensor_tensor(nzs_d[:], nzs_d[:], s_d[:], ALU.mult), "vP")
        P.wge(vector, "vP", P.val("vP"))
        sd3 = s_d.ap().rearrange("p (a b) -> p a b", b=NIT)
        nd3 = nzs_d.ap().rearrange("p (a b) -> p a b", b=NIT)
        q3 = [q.ap().rearrange("p (a q) -> p a q", q=128) for q in qsl_d]

        def evac(it):
            P.wge(vector, "pTd", 2 * it + 1)
            if it >= 2:
                P.wge(vector, "w2%d" % (it % 2), (it // 2) * 16)
            P.inc(vector.tensor_copy(sst[it % 2][:, 0:DS * 128], ptd[0][:]), "vEv")

        for it in range(NIT):
            for f, t in vec_pieces.get(it, ()):
                f(vector, t)
            P.wge(vector, "qd%d" % (it % 2), (it // 2 + 1) * 16)
            if it >= 2:
                P.wge(vector, "pTd", 2 * (it - 1))
            last = None
            for hs in range(DS):
                last = vector.tensor_scalar(
                    wb_d[it % 2][:, hs * 128:(hs + 1) * 128],
                    q3[it % 2][:, hs, :],
                    sd3[:, hs, it:it + 1], nd3[:, hs, it:it + 1],
                    ALU.mult, ALU.add)
            P.inc(last, "vDd")
            if it >= 1:
                evac(it - 1)
        evac(NIT - 1)

    def p1d_sca(scalar):
        P.wge(scalar, "vP", P.val("vP"))
        sd3 = s_d.ap().rearrange("p (a b) -> p a b", b=NIT)
        nd3 = nzs_d.ap().rearrange("p (a b) -> p a b", b=NIT)
        q3 = [q.ap().rearrange("p (a q) -> p a q", q=128) for q in qsl_d]

        def evst(it):
            P.wge(scalar, "pTd", 2 * it + 2)
            if it >= 2:
                P.wge(scalar, "w2%d" % (it % 2), (it // 2) * 16)
            P.inc(scalar.activation(sst[it % 2][:, DS * 128:], ptd[1][:],
                                    AF.Copy), "aEv")
            P.wge(scalar, "aEv", it + 1)
            P.wge(scalar, "vEv", it + 1)
            P.inc(scalar.dma_start(scr_d2[it, :, :], sst[it % 2][:]),
                  "w2%d" % (it % 2), 16)

        for it in range(NIT):
            for f, t in sca_pieces.get(it, ()):
                f(scalar, t)
            P.wge(scalar, "qd%d" % (it % 2), (it // 2 + 1) * 16)
            if it >= 2:
                P.wge(scalar, "pTd", 2 * (it - 1))
            last = None
            for hs in range(DS, NHT):
                last = scalar.activation(
                    wb_d[it % 2][:, hs * 128:(hs + 1) * 128],
                    q3[it % 2][:, hs, :],
                    AF.Identity, bias=nd3[:, hs, it:it + 1], scale=sd3[:, hs, it:it + 1])
            P.inc(last, "aDd")
            if it >= 1:
                evst(it - 1)
        evst(NIT - 1)

    bodies(sync_f=p1d_sync, tensor_f=p1d_pe, vector_f=p1d_vec, scalar_f=p1d_sca,
           gpsimd_f=p1d_gp, bar=False)
    for cm in ptd_cm[::-1]:
        cm.__exit__(None, None, None)

    x0st = [P.sb(f"x0st{i}", [128, H], F32, OFF_D + i * 16 * KB) for i in range(2)]
    junk = [P.sb(f"junk{i}", [128, H], BF16, OFF_D + 32 * KB + i * 8 * KB) for i in range(2)]
    xnst = [P.sb(f"xnst{i}", [128, H], BF16, OFF_D + 48 * KB + i * 8 * KB) for i in range(2)]

    def emit_ln_tiles(rows_list, dst, dstW, tag, prewaits=None):
        n = len(rows_list)
        aN = "aN" + tag; aS = "aS" + tag; aL = "aL" + tag
        vS = "vS" + tag; vT = "vT" + tag; pT = "pT" + tag
        pw = prewaits or {}

        def f_sync(sync):
            for eng_sem, v in pw.get("sync", []):
                P.wge(sync, eng_sem, v)
            for t in range(n):
                if t >= 2:
                    P.wge(sync, aN, t - 1)
                P.inc(sync.dma_start(x0st[t % 2][:],
                                     x0s[rows_list[t][0]:rows_list[t][0] + 128, :]),
                      "x0%s%d" % (tag, t % 2), 16)

        def f_sca(scalar):
            for eng_sem, v in pw.get("scalar", []):
                P.wge(scalar, eng_sem, v)
            for t in range(n):
                P.wge(scalar, "x0%s%d" % (tag, t % 2), (t // 2 + 1) * 16)
                if t >= 2:
                    P.wge(scalar, vS, 5 * (t - 1))
                st = stt[t % 2]
                P.inc(scalar.activation(junk[0][:], x0st[t % 2][:], AF.Square,
                                        bias=c_zero[:], accum_out=st[:, 1:2]), aS)
                P.inc(scalar.activation(junk[1][:], x0st[t % 2][:], AF.Copy,
                                        accum_out=st[:, 0:1]), aS)
                P.wge(scalar, vS, 5 * t + 4)
                P.inc(scalar.activation(st[:, 4:5], st[:, 3:4], AF.Ln, bias=c_eps[:]), aL)
                P.wge(scalar, aL, P.val(aL))
                P.inc(scalar.activation(st[:, 4:5], st[:, 4:5], AF.Exp, bias=c_zero[:],
                                        scale=-0.5), aL)
                P.wge(scalar, vS, 5 * t + 5)
                P.inc(scalar.activation(xnst[t % 2][:], x0st[t % 2][:], AF.Identity,
                                        bias=st[:, 5:6], scale=st[:, 4:5]), aN)

        def f_vec(vector):
            for eng_sem, v in pw.get("vector", []):
                P.wge(vector, eng_sem, v)
            for t in range(n):
                st = stt[t % 2]
                P.wge(vector, aS, 2 * t + 2)
                P.inc(vector.tensor_scalar(st[:, 2:3], st[:, 0:1], 1.0 / H, None,
                                           ALU.mult), vS)
                P.inc(vector.tensor_scalar(st[:, 3:4], st[:, 1:2], 1.0 / H, None,
                                           ALU.mult), vS)
                P.wge(vector, vS, 5 * t + 2)
                P.inc(vector.tensor_tensor(st[:, 6:7], st[:, 2:3], st[:, 2:3], ALU.mult), vS)
                P.wge(vector, vS, 5 * t + 3)
                P.inc(vector.tensor_tensor(st[:, 3:4], st[:, 3:4], st[:, 6:7],
                                           ALU.subtract), vS)
                P.wge(vector, aL, 2 * t + 2)
                P.inc(vector.tensor_scalar(st[:, 5:6], st[:, 2:3], st[:, 4:5], -1.0,
                                           ALU.mult, ALU.mult), vS)
                col0 = rows_list[t][1]
                for ht in range(NHT):
                    P.wge(vector, pT, NHT * t + ht + 1)
                    P.inc(vector.tensor_copy(
                        dst[:, ht * dstW + col0: ht * dstW + col0 + 128],
                        ptr[(NHT * t + ht) % 2][:, 0:128]), vT)

        def f_pe(tensor):
            for eng_sem, v in pw.get("tensor", []):
                P.wge(tensor, eng_sem, v)
            for t in range(n):
                P.wge(tensor, aN, t + 1)
                for ht in range(NHT):
                    if NHT * t + ht >= 2:
                        P.wge(tensor, vT, NHT * t + ht - 1)
                    P.inc(tensor.transpose(ptr[(NHT * t + ht) % 2][:, 0:128],
                                           xnst[t % 2][:, ht * 128:(ht + 1) * 128],
                                           ident[:]), pT)

        return f_sync, f_pe, f_vec, f_sca

    wkv_sl = P.sb("wkv_sl", [128, NHT * 256], BF16, OFF_G)
    wk_sl = P.sb("wk_sl", [128, NHT * 128], BF16, OFF_G)
    rt1a = P.sb("rt1a", [128, 512], F32, OFF_G + 8 * KB)
    rt2a = P.sb("rt2a", [128, 512], F32, OFF_G + 10 * KB)

    pk_cm = [nc.psum_tensor(f"pk{i}", [128, 512], F32) for i in range(2)]
    pk = [cm.__enter__() for cm in pk_cm]
    pvv_cm = [nc.psum_tensor(f"pvv{i}", [128, 512], F32) for i in range(2)]
    pvv = [cm.__enter__() for cm in pvv_cm]

    def rope_evac(vector, psrc, c0, dst, dcols, tag, rt1, rt2):
        P.wge(vector, tag, P.val(tag))
        W = 512
        P.inc(vector.tensor_tensor(rt1[0:64, :], psrc[64:128, :],
                                   cs_sin[0:64, c0:c0 + W], ALU.mult), "vR")
        P.inc(vector.tensor_tensor(rt1[64:128, :], psrc[0:64, :],
                                   cs_sin[64:128, c0:c0 + W], ALU.mult), "vR")
        P.inc(vector.tensor_tensor(rt2[:, :], psrc[:, :],
                                   cs_cos[:, c0:c0 + W], ALU.mult), "vR")
        P.wge(vector, "vR", P.val("vR"))
        P.inc(vector.tensor_tensor(dst[:, dcols:dcols + W], rt2[:, :], rt1[:, :],
                                   ALU.add), tag)

    def rope_evac_q(vector, psrc, c0, dst, dcols, tag, rt1):
        P.wge(vector, tag, P.val(tag))
        W = 512
        P.inc(vector.tensor_tensor(rt1[0:64, :], psrc[64:128, :],
                                   cs_sin[0:64, c0:c0 + W], ALU.mult), "vR")
        P.inc(vector.tensor_tensor(rt1[64:128, :], psrc[0:64, :],
                                   cs_sin[64:128, c0:c0 + W], ALU.mult), "vR")
        P.inc(vector.tensor_tensor(dst[:, dcols:dcols + W], psrc[:, :],
                                   cs_cos[:, c0:c0 + W], ALU.mult), "vR")
        P.wge(vector, "vR", P.val("vR"))
        P.inc(vector.tensor_tensor(dst[:, dcols:dcols + W], dst[:, dcols:dcols + W],
                                   rt1[:, :], ALU.add), tag)

    def p3a_gp(gpsimd):
        for kv in range(NKV):
            if kv >= 1:
                P.wge(gpsimd, "pK", kv)
                P.wge(gpsimd, "vK", kv)
            P.inc(gpsimd.dma_start(wk_sl[:], wk_t[kv]), "lKV", 16)

    lkv_a = P.val("lKV")

    def p3a_pe(tensor):
        P.wge(tensor, "vTh", 4 * NHT)
        P.wge(tensor, "vEv", NIT)
        P.wge(tensor, "aEv", NIT)
        for kv in range(NKV):
            P.wge(tensor, "lKV", lkv_a + (kv + 1) * 16)
            if kv >= 2:
                P.wge(tensor, "vK", kv - 1)
            last = None
            for ht in range(NHT):
                last = tensor.matmul(pk[kv % 2][:],
                                     wk_sl[:, ht * 128:(ht + 1) * 128],
                                     x1nT_h[:, ht * HALO:(ht + 1) * HALO],
                                     start=(ht == 0), stop=(ht == NHT - 1))
            P.inc(last, "pK")

    def p3a_vec(vector):
        for kv in range(NKV):
            P.wge(vector, "pK", kv + 1)
            rope_evac(vector, pk[kv % 2], 0, kT, kv * LTOK, "vK", rt1a, rt2a)

    bodies(tensor_f=p3a_pe, vector_f=p3a_vec, gpsimd_f=p3a_gp, bar=False)

    def emit_v(jts, xsrc, xW, jt_base, tag):
        lkv_b = P.val("lKV")
        pV = "pV" + tag; vV = "vV" + tag
        nj = len(jts)

        def f_gp(gpsimd):
            for vc in range(4):
                if vc >= 1:
                    P.wge(gpsimd, pV, vc * nj)
                else:
                    P.wge(gpsimd, "pK", P.val("pK"))
                    P.wge(gpsimd, "vK", P.val("vK"))
                P.inc(gpsimd.dma_start(wkv_sl[:], wv_t[vc]), "lKV", 16)

        def f_pe(tensor):
            for vc in range(4):
                P.wge(tensor, "lKV", lkv_b + (vc + 1) * 16)
                for ji, jt in enumerate(jts):
                    k = vc * nj + ji
                    if k >= 2:
                        P.wge(tensor, vV, k - 1)
                    elif tag == "b":
                        P.wge(tensor, "vVa", 15 + k)
                    col = (jt - jt_base) * 128
                    last = None
                    for ht in range(NHT):
                        last = tensor.matmul(
                            pvv[k % 2][:, 0:256],
                            xsrc[:, ht * xW + col: ht * xW + col + 128],
                            wkv_sl[:, ht * 256:(ht + 1) * 256],
                            start=(ht == 0), stop=(ht == NHT - 1))
                    P.inc(last, pV)

        def f_vec(vector):
            for vc in range(4):
                for ji, jt in enumerate(jts):
                    k = vc * nj + ji
                    P.wge(vector, pV, k + 1)
                    P.inc(vector.tensor_copy(
                        vB[:, jt * 1024 + vc * 256: jt * 1024 + (vc + 1) * 256],
                        pvv[k % 2][:, 0:256]), vV)

        return f_gp, f_pe, f_vec

    fg, fpe, fve = emit_v([0, 1, 2, 3], x1nT_h, HALO, 0, "a")
    bodies(tensor_f=fpe, vector_f=fve, gpsimd_f=fg, bar=False)

    own_rows = [(HALO + i * 128, i * 128) for i in range(8)]
    fs, fpe, fve, fae = emit_ln_tiles(
        own_rows, x1nT_o, TOK, "o",
        prewaits={
            "scalar": [("aNh", 4), ("vSh", 20), ("pTh", 4 * NHT), ("pK", 8),
                       ("pVa", 16), ("pTd", 2 * NIT),
                       ("w20", P.val("w20")), ("w21", P.val("w21"))],
            "sync": [("aNh", 4), ("vDd", NIT), ("aDd", NIT),
                     ("w20", P.val("w20")), ("w21", P.val("w21"))],
            "vector": [("vSh", 20), ("pTd", 2 * NIT)],
            "tensor": [("vTh", 4 * NHT), ("pVa", 16)],
        })
    bodies(sync_f=fs, tensor_f=fpe, vector_f=fve, scalar_f=fae, bar=False)

    pk_b = P.val("pK")
    lkv_c = P.val("lKV")

    def p3b_gp(gpsimd):
        for kv in range(NKV):
            if kv >= 1:
                P.wge(gpsimd, "pK", pk_b + 2 * kv)
                P.wge(gpsimd, "vK", NKV + 2 * kv)
            else:
                P.wge(gpsimd, "pVa", 16)
                P.wge(gpsimd, "vVa", 16)
            P.inc(gpsimd.dma_start(wk_sl[:], wk_t[kv]), "lKV", 16)

    def p3b_pe(tensor):
        P.wge(tensor, "vTo", 8 * NHT)
        for kv in range(NKV):
            P.wge(tensor, "lKV", lkv_c + (kv + 1) * 16)
            for ch in range(2):
                k = 2 * kv + ch
                if k >= 2:
                    P.wge(tensor, "vK", NKV + k - 1)
                else:
                    P.wge(tensor, "vK", NKV - 2 + k + 1)
                last = None
                for ht in range(NHT):
                    last = tensor.matmul(
                        pk[k % 2][:],
                        wk_sl[:, ht * 128:(ht + 1) * 128],
                        x1nT_o[:, ht * TOK + ch * 512:ht * TOK + (ch + 1) * 512],
                        start=(ht == 0), stop=(ht == NHT - 1))
                P.inc(last, "pK")

    def p3b_vec(vector):
        for kv in range(NKV):
            for ch in range(2):
                k = 2 * kv + ch
                P.wge(vector, "pK", pk_b + k + 1)
                rope_evac(vector, pk[k % 2], HALO + ch * 512, kT,
                          kv * LTOK + HALO + ch * 512, "vK", rt1a, rt2a)

    bodies(tensor_f=p3b_pe, vector_f=p3b_vec, gpsimd_f=p3b_gp, bar=False)

    fg, fpe, fve = emit_v([4, 5, 6, 7, 8, 9, 10, 11], x1nT_o, TOK, 4, "b")
    bodies(tensor_f=fpe, vector_f=fve, gpsimd_f=fg, bar=False)


    for cm in pvv_cm[::-1]:
        cm.__exit__(None, None, None)
    for cm in pk_cm[::-1]:
        cm.__exit__(None, None, None)

    wq_sl = [P.sb(f"wq_sl{i}", [128, 8 * 128], BF16, OFF_G + i * 2 * KB) for i in range(6)]
    wqf = [P.sb(f"wqf{i}", [128, 512], F32, OFF_G + 12 * KB + i * 2 * KB) for i in range(2)]
    pq_cm = [nc.psum_tensor(f"pq{i}", [128, 512], F32) for i in range(2)]
    pq = [cm.__enter__() for cm in pq_cm]
    rt1q_cm = nc.psum_tensor("rt1q", [128, 512], F32)
    rt1q = rt1q_cm.__enter__()

    def p4_sync(sync):
        P.wge(sync, "pVb", 32)
        P.wge(sync, "vVb", 32)
        for u in range(8 * NH):
            qt, a, hf = u // 8, (u % 8) // 2, u % 2
            if u >= 2:
                P.wge(sync, "aQC", u - 1)
            P.inc(sync.dma_start(
                wqf[u % 2][:],
                wq_t[qt, a, :, hf * 512:(hf + 1) * 512]), "lqf%d" % (u % 2), 16)

    def p4_sca(scalar):
        P.wge(scalar, "pVb", 32)
        P.wge(scalar, "vK", NKV + 2 * NKV)
        for u in range(8 * NH):
            qt, a, hf = u // 8, (u % 8) // 2, u % 2
            P.wge(scalar, "lqf%d" % (u % 2), (u // 2 + 1) * 16)
            q = 4 * qt + a
            if hf == 0 and q >= 6:
                P.wge(scalar, "pQ", 2 * ((q - 6) // 4 + 1))
            P.inc(scalar.activation(
                wq_sl[q % 6][:, hf * 512:(hf + 1) * 512],
                wqf[u % 2][:], AF.Copy), "aQC")

    def p4_pe(tensor):
        P.wge(tensor, "vTo", 8 * NHT)
        P.wge(tensor, "vVb", 32)
        for qt in range(NH):
            P.wge(tensor, "aQC", 8 * (qt + 1))
            for ch in range(2):
                k = 2 * qt + ch
                if k >= 2:
                    P.wge(tensor, "vQ", k - 1)
                else:
                    P.wge(tensor, "vK", NKV + 16)
                last = None
                for ht in range(NHT):
                    last = tensor.matmul(
                        pq[k % 2][:],
                        wq_sl[(4 * qt + ht // 8) % 6][:, (ht % 8) * 128:(ht % 8 + 1) * 128],
                        x1nT_o[:, ht * TOK + ch * 512:ht * TOK + (ch + 1) * 512],
                        start=(ht == 0), stop=(ht == NHT - 1))
                P.inc(last, "pQ")

    def p4_vec(vector):
        P.wge(vector, "pVb", 32)
        P.wge(vector, "vK", NKV + 2 * NKV)
        for qt in range(NH):
            for ch in range(2):
                k = 2 * qt + ch
                P.wge(vector, "pQ", k + 1)
                rope_evac_q(vector, pq[k % 2], HALO + ch * 512, qT,
                            qt * TOK + ch * 512, "vQ", rt1q)

    bodies(sync_f=p4_sync, tensor_f=p4_pe, vector_f=p4_vec, scalar_f=p4_sca, bar=False)

    rt1q_cm.__exit__(None, None, None)
    for cm in pq_cm[::-1]:
        cm.__exit__(None, None, None)
    for cm in ptr_cm[::-1]:
        cm.__exit__(None, None, None)

    probs = [P.sb(f"probs{i}", [128, 5 * 512], BF16, OFF_G + i * 5 * KB) for i in range(2)]
    bc_sb = [P.sb(f"bc_sb{i}", [128, 512], BF16, OFF_G + 10 * KB + i * KB) for i in range(2)]
    r_sb = P.sb("r_sb", [1, 512], F32, OFF_G + 12 * KB)
    r_bf = P.sb("r_bf", [1, 512], BF16, OFF_G + 14 * KB)

    ps_s_cm = nc.psum_tensor("ps_s", [128, 5 * 512], F32)
    ps_s = ps_s_cm.__enter__()
    ps_pv_cm = nc.psum_tensor("ps_pv", [128, 512], F32)
    ps_pv = ps_pv_cm.__enter__()
    ps_sum_cm = nc.psum_tensor("ps_sum", [1, 512], F32)
    ps_sum = ps_sum_cm.__enter__()
    ps_bc_cm = nc.psum_tensor("ps_bc", [128, 512], F32)
    ps_bc = ps_bc_cm.__enter__()

    qT3 = qT.ap().rearrange("p (h t) -> p h t", t=TOK)
    attnT3 = attnT.ap().rearrange("p (h t) -> p h t", t=TOK)

    NID = NKV * 8

    def p5a_scores(tensor, idx):
        kv, qt = idx // 8, idx % 8
        rhs_q = qT3[:, kv * GQ:(kv + 1) * GQ, qt * 128:(qt + 1) * 128]
        for j in range(5):
            if idx >= 1:
                P.wge(tensor, "aEX", 5 * (idx - 1) + j + 1)
            P.inc(tensor.matmul(
                ps_s[:, j * 512:(j + 1) * 512],
                kT[:, kv * LTOK + (qt + j) * 128: kv * LTOK + (qt + j + 1) * 128],
                rhs_q, start=True, stop=True), "pSC")

    def p5a_tail(tensor, idx):
        kv, qt = idx // 8, idx % 8
        pb = probs[idx % 2]
        P.wge(tensor, "vMK", 2 * (idx + 1))
        P.wge(tensor, "vEV", idx)
        last = None
        for j in range(5):
            last = tensor.matmul(
                ps_pv[:],
                vB[:, (qt + j) * 1024 + kv * 128:(qt + j) * 1024 + (kv + 1) * 128],
                pb[:, j * 512:(j + 1) * 512],
                start=(j == 0), stop=(j == 4))
        P.inc(last, "pPV")
        P.wge(tensor, "aRC", 2 * idx)
        last = None
        for j in range(5):
            last = tensor.matmul(
                ps_sum[:],
                ones_c[:],
                pb[:, j * 512:(j + 1) * 512],
                start=(j == 0), stop=(j == 4))
        P.inc(last, "pSM")
        P.wge(tensor, "aRC", 2 * (idx + 1))
        P.wge(tensor, "vBC", idx)
        P.inc(tensor.matmul(ps_bc[:], ones_r[:], r_bf[:],
                            start=True, stop=True), "pBC")

    def p5a_pe(tensor):
        P.wge(tensor, "vQ", 2 * NH)
        p5a_scores(tensor, 0)
        for idx in range(1, NID):
            p5a_scores(tensor, idx)
            p5a_tail(tensor, idx - 1)
        p5a_tail(tensor, NID - 1)

    def p5a_sca(scalar):
        for idx in range(NID):
            for j in range(5):
                P.wge(scalar, "pSC", 5 * idx + j + 1)
                if idx >= 2 and j == 0:
                    P.wge(scalar, "pSM", idx - 1)
                P.inc(scalar.activation(
                    probs[idx % 2][:, j * 512:(j + 1) * 512],
                    ps_s[:, j * 512:(j + 1) * 512],
                    AF.Exp, bias=biask[:, idx % 8 + j:idx % 8 + j + 1], scale=SCALE),
                    "aEX")
            P.wge(scalar, "pSM", idx + 1)
            P.inc(scalar.activation(r_sb[:], ps_sum[:], AF.Ln, bias=c_zero[0:1, :]), "aRC")
            P.wge(scalar, "aRC", P.val("aRC"))
            P.inc(scalar.activation(r_bf[:], r_sb[:], AF.Exp, bias=c_zero[0:1, :],
                                    scale=-1.0), "aRC")

    def p5a_vec(vector):
        for idx in range(NID):
            kv, qt = idx // 8, idx % 8
            pb = probs[idx % 2]
            P.wge(vector, "aEX", 5 * idx + 5)
            P.inc(vector.tensor_tensor(pb[:, 0:512], pb[:, 0:512], m0[:],
                                       ALU.mult), "vMK")
            P.inc(vector.tensor_tensor(pb[:, 4 * 512:5 * 512],
                                       pb[:, 4 * 512:5 * 512], m4[:],
                                       ALU.mult), "vMK")
            P.wge(vector, "pPV", idx + 1)
            P.wge(vector, "pBC", idx + 1)
            P.inc(vector.tensor_copy(bc_sb[idx % 2][:], ps_bc[:]), "vBC")
            P.wge(vector, "vBC", idx + 1)
            P.inc(vector.tensor_tensor(
                attnT3[:, kv * GQ:(kv + 1) * GQ, qt * 128:(qt + 1) * 128],
                ps_pv[:], bc_sb[idx % 2][:], ALU.mult), "vEV")

    bodies(tensor_f=p5a_pe, vector_f=p5a_vec, scalar_f=p5a_sca)

    for cm in [ps_bc_cm, ps_sum_cm, ps_pv_cm, ps_s_cm]:
        cm.__exit__(None, None, None)

    wo_sl = [P.sb(f"wo_sl{i}", [128, NHT * 256], BF16, OFF_E + i * 16 * KB) for i in range(3)]
    x0c = P.sb("x0c", [128, 8 * 512], F32, OFF_G)
    po_cm = [nc.psum_tensor(f"po{i}", [128, 512], F32) for i in range(2)]
    po = [cm.__enter__() for cm in po_cm]
    def p5b_sync(sync):
        for oc in range(8):
            if oc >= 1:
                P.wge(sync, "vO", 8 * oc)
            P.inc(sync.dma_start(
                x0c.ap().rearrange("p (t q) -> p t q", q=512),
                x0s.rearrange("(t p) q -> p t q", p=128)[:, 4:12, oc * 512:(oc + 1) * 512]),
                "lxc", 16)

    def p5b_gp(gpsimd):
        h = 0
        for oc in range(8):
            for half in range(2):
                if h >= 3:
                    P.wge(gpsimd, "pO", 8 * ((h - 3) // 2) + 8)
                P.inc(gpsimd.dma_start(wo_sl[h % 3][:], wo_t[oc, half]),
                      "lo%d" % (h % 3), 16)
                h += 1

    def p5b_pe(tensor):
        for oc in range(8):
            P.wge(tensor, "lo%d" % ((2 * oc) % 3), ((2 * oc) // 3 + 1) * 16)
            P.wge(tensor, "lo%d" % ((2 * oc + 1) % 3), ((2 * oc + 1) // 3 + 1) * 16)
            for tt in range(8):
                k = 8 * oc + tt
                if k >= 2:
                    P.wge(tensor, "vO", k - 1)
                last = None
                for half in range(2):
                    sl = wo_sl[(2 * oc + half) % 3]
                    for dt in range(NHT):
                        last = tensor.matmul(
                            po[k % 2][:, half * 256:(half + 1) * 256],
                            attnT[:, dt * TOK + tt * 128:dt * TOK + (tt + 1) * 128],
                            sl[:, dt * 256:(dt + 1) * 256],
                            start=(dt == 0), stop=(dt == NHT - 1))
                P.inc(last, "pO")

    def p5b_vec(vector):
        for oc in range(8):
            for tt in range(8):
                k = 8 * oc + tt
                P.wge(vector, "pO", k + 1)
                P.wge(vector, "lxc", (oc + 1) * 16)
                P.inc(vector.tensor_tensor(
                    x1[:, tt * H + oc * 512:tt * H + (oc + 1) * 512],
                    po[k % 2][:], x0c[:, tt * 512:(tt + 1) * 512], ALU.add), "vO")

    bodies(sync_f=p5b_sync, tensor_f=p5b_pe, vector_f=p5b_vec, gpsimd_f=p5b_gp)

    for cm in po_cm[::-1]:
        cm.__exit__(None, None, None)

    x2st = [P.sb(f"x2st{i}", [128, H], BF16, OFF_E + i * 8 * KB) for i in range(2)]
    x1tst = [P.sb(f"x1tst{i}", [128, NHT * 128], BF16, OFF_E + 16 * KB + i * 8 * KB) for i in range(2)]
    junkb = [P.sb(f"junkb{i}", [128, H], BF16, OFF_F + 8 * KB + i * 8 * KB) for i in range(2)]

    ptr2_cm = [nc.psum_tensor(f"ptr2_{i}", [128, 1024], BF16) for i in range(2)]
    ptr2 = [cm.__enter__() for cm in ptr2_cm]

    def p5c_sca(scalar):
        for tt in range(8):
            st = stt[tt % 2]
            if tt >= 2:
                P.wge(scalar, "vS2", 5 * (tt - 1))
            xt = x1[:, tt * H:(tt + 1) * H]
            P.inc(scalar.activation(junkb[0][:], xt, AF.Square, bias=c_zero[:],
                                    accum_out=st[:, 1:2]), "aS2")
            P.inc(scalar.activation(junkb[1][:], xt, AF.Copy, accum_out=st[:, 0:1]), "aS2")
            P.wge(scalar, "vS2", 5 * tt + 4)
            P.inc(scalar.activation(st[:, 4:5], st[:, 3:4], AF.Ln, bias=c_eps[:]), "aL2")
            P.wge(scalar, "aL2", P.val("aL2"))
            P.inc(scalar.activation(st[:, 4:5], st[:, 4:5], AF.Exp, bias=c_zero[:],
                                    scale=-0.5), "aL2")
            P.wge(scalar, "vS2", 5 * tt + 5)
            if tt >= 2:
                P.wge(scalar, "vT2", 64 * (tt - 1))
            P.inc(scalar.activation(x2st[tt % 2][:], xt, AF.Identity,
                                    bias=st[:, 5:6], scale=st[:, 4:5]), "aN2")

    def p5c_vec(vector):
        for tt in range(8):
            st = stt[tt % 2]
            P.wge(vector, "aS2", 2 * tt + 2)
            P.inc(vector.tensor_scalar(st[:, 2:3], st[:, 0:1], 1.0 / H, None, ALU.mult), "vS2")
            P.inc(vector.tensor_scalar(st[:, 3:4], st[:, 1:2], 1.0 / H, None, ALU.mult), "vS2")
            P.wge(vector, "vS2", 5 * tt + 2)
            P.inc(vector.tensor_tensor(st[:, 6:7], st[:, 2:3], st[:, 2:3], ALU.mult), "vS2")
            P.wge(vector, "vS2", 5 * tt + 3)
            P.inc(vector.tensor_tensor(st[:, 3:4], st[:, 3:4], st[:, 6:7], ALU.subtract), "vS2")
            P.wge(vector, "aL2", 2 * tt + 2)
            P.inc(vector.tensor_scalar(st[:, 5:6], st[:, 2:3], st[:, 4:5], -1.0,
                                       ALU.mult, ALU.mult), "vS2")
            for ht in range(NHT):
                P.wge(vector, "pT2", 64 * tt + ht + 1)
                P.inc(vector.tensor_copy(
                    x2nT[:, ht * TOK + tt * 128:ht * TOK + (tt + 1) * 128],
                    ptr2[(64 * tt + ht) % 2][:, 0:128]), "vT2")
            for ht in range(NHT):
                P.wge(vector, "pT2", 64 * tt + 32 + ht + 1)
                P.inc(vector.tensor_copy(
                    x1tst[tt % 2][:, ht * 128:(ht + 1) * 128],
                    ptr2[(64 * tt + 32 + ht) % 2][:, 0:128]), "vT2")


    def p5c_pe(tensor):
        for tt in range(8):
            P.wge(tensor, "aN2", tt + 1)
            for ht in range(NHT):
                if 64 * tt + ht >= 2:
                    P.wge(tensor, "vT2", 64 * tt + ht - 1)
                P.inc(tensor.transpose(ptr2[(64 * tt + ht) % 2][:, 0:128],
                                       x2st[tt % 2][:, ht * 128:(ht + 1) * 128],
                                       ident[:]), "pT2")
            if tt >= 2:
                P.wge(tensor, "sx%d" % (tt % 2), (tt // 2) * 32)
            for ht in range(NHT):
                P.wge(tensor, "vT2", 64 * tt + 32 + ht - 1)
                P.inc(tensor.transpose(ptr2[(64 * tt + 32 + ht) % 2][:, 0:128],
                                       x1[:, tt * H + ht * 128:tt * H + (ht + 1) * 128],
                                       ident[:]), "pT2")

    def p5c_gp(gpsimd):
        for tt in range(8):
            P.wge(gpsimd, "vT2", 64 * tt + 64)
            for hh in range(2):
                P.inc(gpsimd.dma_start(
                    scr_x1t.rearrange("(a p) t -> p a t", p=128)[:, hh * 16:(hh + 1) * 16,
                                                                tt * 128:(tt + 1) * 128],
                    x1tst[tt % 2].ap().rearrange("p (a q) -> p a q", q=128)[:, hh * 16:(hh + 1) * 16, :]),
                    "sx%d" % (tt % 2), 16)
        for s in ["sx0", "sx1"]:
            P.wge(gpsimd, s, P.val(s))

    bodies(tensor_f=p5c_pe, vector_f=p5c_vec, scalar_f=p5c_sca, gpsimd_f=p5c_gp)

    for cm in ptr2_cm[::-1]:
        cm.__exit__(None, None, None)

    wgu_sl = [P.sb(f"wgu_sl{i}", [128, NHT * 128], BF16, OFF_F + i * 8 * KB) for i in range(4)]
    wgu_sl[3] = P.sb("wgu_sl3", [128, NHT * 128], BF16, OFF_G)
    sil_st = [P.sb(f"sil_st{i}", [128, 512], F32, OFF_G + 8 * KB + i * 2 * KB) for i in range(2)]
    wd_sl = [P.sb(f"wd_sl{i}", [128, 8 * 1024], BF16, OFF_F + i * 16 * KB) for i in range(2)]
    x1t_sl = [P.sb(f"x1t_sl{i}", [128, 512], BF16, OFF_G + 8 * KB + i * KB) for i in range(2)]
    out_st = [P.sb(f"out_st{i}", [128, 512], F32, OFF_G + 10 * KB + i * 2 * KB) for i in range(2)]

    NGRP = 11

    for tc in range(2):
        pg_cm = [nc.psum_tensor(f"pgg{tc}_{i}", [128, 512], F32) for i in range(2)]
        pgg = [cm.__enter__() for cm in pg_cm]
        pu_cm = [nc.psum_tensor(f"puu{tc}_{i}", [128, 512], F32) for i in range(2)]
        puu = [cm.__enter__() for cm in pu_cm]
        tg_base = [P.val("tg0"), P.val("tg1")]
        tu_base = [P.val("tu0"), P.val("tu1")]

        def p6gu_sync(sync, tc=tc):
            for it in range(NIT):
                if it >= 2:
                    P.wge(sync, "pGU", tc * NIT + it - 1)
                P.inc(sync.dma_start(
                    wgu_sl[2 * (it % 2)].ap().rearrange("p (t q) -> p t q", q=128),
                    scr_g[it * 128:(it + 1) * 128, :], transpose=True),
                    "tg%d" % (it % 2), 16)
                P.inc(sync.dma_start(
                    wgu_sl[2 * (it % 2) + 1].ap().rearrange("p (t q) -> p t q", q=128),
                    scr_u[it * 128:(it + 1) * 128, :], transpose=True),
                    "tu%d" % (it % 2), 16)

        def p6gu_pe(tensor, tc=tc, tg_base=tg_base, tu_base=tu_base):
            for it in range(NIT):
                P.wge(tensor, "tg%d" % (it % 2), tg_base[it % 2] + (it // 2 + 1) * 16)
                P.wge(tensor, "tu%d" % (it % 2), tu_base[it % 2] + (it // 2 + 1) * 16)
                if it >= 2:
                    P.wge(tensor, "vH", tc * NIT + it - 1)
                last = None
                for ht in range(NHT):
                    last = tensor.matmul(
                        pgg[it % 2][:],
                        wgu_sl[2 * (it % 2)][:, ht * 128:(ht + 1) * 128],
                        x2nT[:, ht * TOK + tc * 512:ht * TOK + (tc + 1) * 512],
                        start=(ht == 0), stop=(ht == NHT - 1))
                P.inc(last, "pGs")
                last = None
                for ht in range(NHT):
                    last = tensor.matmul(
                        puu[it % 2][:],
                        wgu_sl[2 * (it % 2) + 1][:, ht * 128:(ht + 1) * 128],
                        x2nT[:, ht * TOK + tc * 512:ht * TOK + (tc + 1) * 512],
                        start=(ht == 0), stop=(ht == NHT - 1))
                P.inc(last, "pGU")

        def p6gu_sca(scalar, tc=tc):
            for it in range(NIT):
                P.wge(scalar, "pGs", tc * NIT + it + 1)
                if it >= 2:
                    P.wge(scalar, "vH", tc * NIT + it - 1)
                P.inc(scalar.activation(sil_st[it % 2][:], pgg[it % 2][:], AF.Silu,
                                        bias=c_zero[:]), "aSI")

        def p6gu_vec(vector, tc=tc):
            for it in range(NIT):
                P.wge(vector, "aSI", tc * NIT + it + 1)
                P.wge(vector, "pGU", tc * NIT + it + 1)
                P.inc(vector.tensor_tensor(
                    hT[:, it * 512:(it + 1) * 512],
                    sil_st[it % 2][:], puu[it % 2][:], ALU.mult), "vH")

        bodies(sync_f=p6gu_sync, tensor_f=p6gu_pe, vector_f=p6gu_vec, scalar_f=p6gu_sca)

        for cm in pu_cm[::-1]:
            cm.__exit__(None, None, None)
        for cm in pg_cm[::-1]:
            cm.__exit__(None, None, None)

        pd_cm = nc.psum_tensor(f"pd{tc}", [128, 8 * 512], F32)
        pd = pd_cm.__enter__()
        ld_base = [P.val("ld0"), P.val("ld1")]
        gy_base = [P.val("gy0"), P.val("gy1")]
        lx_base = [P.val("lx0"), P.val("lx1")]
        vdo_d = P.val("vDO")
        pd_d = P.val("pD")

        def p6d_sync(sync, tc=tc, pd_d=pd_d):
            for og in range(4):
                for grp in range(NGRP):
                    nit_g = 8 if grp < 10 else 6
                    it0 = grp * 8
                    k = og * NGRP + grp
                    if k >= 2:
                        P.wge(sync, "pD", pd_d + k - 1)
                    P.inc(sync.dma_start(
                        wd_sl[k % 2].ap().rearrange("p (a q) -> p a q", q=1024)[:, 0:nit_g, :],
                        scr_d2.rearrange("a p q -> p a q")[:, it0:it0 + nit_g,
                                                          og * 1024:(og + 1) * 1024]),
                        "ld%d" % (k % 2), 16)

        def p6d_pe(tensor, tc=tc, ld_base=ld_base, vdo_d=vdo_d):
            for og in range(4):
                if og >= 1:
                    P.wge(tensor, "vDO", vdo_d + og * 8)
                for grp in range(NGRP):
                    nit_g = 8 if grp < 10 else 6
                    it0 = grp * 8
                    k = og * NGRP + grp
                    P.wge(tensor, "ld%d" % (k % 2), ld_base[k % 2] + (k // 2 + 1) * 16)
                    last = None
                    for gi in range(nit_g):
                        it = it0 + gi
                        for oi in range(8):
                            last = tensor.matmul(
                                pd[:, oi * 512:(oi + 1) * 512],
                                wd_sl[(og * NGRP + grp) % 2][:, gi * 1024 + oi * 128:
                                                             gi * 1024 + (oi + 1) * 128],
                                hT[:, it * 512:(it + 1) * 512],
                                start=(it == 0), stop=(it == NIT - 1))
                    P.inc(last, "pD")

        def p6d_vec(vector, tc=tc, gy_base=gy_base, lx_base=lx_base, pd_d=pd_d):
            for og in range(4):
                P.wge(vector, "pD", pd_d + (og + 1) * NGRP)
                for oi in range(8):
                    k = og * 8 + oi
                    P.wge(vector, "lx%d" % (k % 2), lx_base[k % 2] + (k // 2 + 1) * 16)
                    if k >= 2:
                        P.wge(vector, "gy%d" % (k % 2), gy_base[k % 2] + (k // 2) * 16)
                    P.inc(vector.tensor_tensor(
                        out_st[k % 2][:], pd[:, oi * 512:(oi + 1) * 512],
                        x1t_sl[k % 2][:], ALU.add), "vDO")

        def p6d_gp(gpsimd, tc=tc, vdo_d=vdo_d):
            for og in range(4):
                for oi in range(8):
                    k = og * 8 + oi
                    if k >= 2:
                        P.wge(gpsimd, "vDO", vdo_d + k - 1)
                    ot = og * 8 + oi
                    P.inc(gpsimd.dma_start(
                        x1t_sl[k % 2][:],
                        scr_x1t[ot * 128:(ot + 1) * 128, tc * 512:(tc + 1) * 512]),
                        "lx%d" % (k % 2), 16)
                    P.wge(gpsimd, "vDO", vdo_d + k + 1)
                    P.inc(gpsimd.dma_start(
                        y_t[ot * 128:(ot + 1) * 128, tc * 512:(tc + 1) * 512],
                        out_st[k % 2][:]), "gy%d" % (k % 2), 16)

        bodies(sync_f=p6d_sync, tensor_f=p6d_pe, vector_f=p6d_vec, gpsimd_f=p6d_gp)
        pd_cm.__exit__(None, None, None)

    def fin_gp(gpsimd):
        for s in ["gy0", "gy1"]:
            P.wge(gpsimd, s, P.val(s))

    bodies(gpsimd_f=fin_gp, bar=False)

    block_cm.__exit__(None, None, None)
    nc.finalize()
    return nc



_CACHE = {}


def _host_inputs(inputs):
    x0 = np.ascontiguousarray(np.asarray(inputs["L0_attn_hidden_states"],
                                         dtype=np.float32).reshape(S, H))
    wq = np.ascontiguousarray(np.asarray(inputs["L0_attn_Wq"], dtype=np.float32))
    wk = np.ascontiguousarray(np.asarray(inputs["L0_attn_Wk"], dtype=np.float32))
    wv = np.ascontiguousarray(np.asarray(inputs["L0_attn_Wv"], dtype=np.float32))
    wo = np.ascontiguousarray(np.asarray(inputs["L0_attn_Wo"], dtype=np.float32))

    wq_t = np.ascontiguousarray(
        wq.reshape(4, 8, 128, NH, 128).transpose(3, 0, 2, 1, 4)).reshape(NH, 4, 128, 1024)
    wk_t = np.ascontiguousarray(
        wk.reshape(32, 128, 8, 128).transpose(2, 1, 0, 3)).reshape(8, 128, 32 * 128)
    wv_t = np.ascontiguousarray(
        wv.reshape(32, 128, 4, 256).transpose(2, 1, 0, 3)).reshape(4, 128, 32 * 256)
    wo_t = np.ascontiguousarray(
        wo.reshape(32, 128, 8, 2, 256).transpose(2, 3, 1, 0, 4)).reshape(8, 2, 128, 32 * 256)

    qg = np.ascontiguousarray(np.asarray(inputs["L0_ffn_qw_gate"], dtype=np.int32))
    qu = np.ascontiguousarray(np.asarray(inputs["L0_ffn_qw_up"], dtype=np.int32))
    qd = np.ascontiguousarray(np.asarray(inputs["L0_ffn_qw_down"], dtype=np.int32))
    sg = np.ascontiguousarray(np.asarray(inputs["L0_ffn_s_gate"], dtype=np.float32))
    zg = np.ascontiguousarray(np.asarray(inputs["L0_ffn_z_gate"], dtype=np.float32))
    su = np.ascontiguousarray(np.asarray(inputs["L0_ffn_s_up"], dtype=np.float32))
    zu = np.ascontiguousarray(np.asarray(inputs["L0_ffn_z_up"], dtype=np.float32))
    sd = np.ascontiguousarray(np.asarray(inputs["L0_ffn_s_down"], dtype=np.float32))
    zd = np.ascontiguousarray(np.asarray(inputs["L0_ffn_z_down"], dtype=np.float32))

    ident = np.eye(128, dtype=np.float32).astype(ml_dtypes.bfloat16)
    r = np.arange(128)
    m0b = (r[None, :] <= r[:, None]).astype(np.float32)
    m4b = (r[None, :] >= r[:, None]).astype(np.float32)
    m0 = np.ascontiguousarray(np.tile(m0b[:, None, :], (1, 4, 1)).reshape(128, 512)
                              ).astype(ml_dtypes.bfloat16)
    m4 = np.ascontiguousarray(np.tile(m4b[:, None, :], (1, 4, 1)).reshape(128, 512)
                              ).astype(ml_dtypes.bfloat16)

    inv = 1.0 / THETA ** (np.arange(0, HD, 2, dtype=np.float64) / HD)

    in_maps = []
    for c in range(NCORES):
        lo = c * TOK - HALO
        if lo < 0:
            xs = np.concatenate([np.zeros((-lo, H), np.float32), x0[:c * TOK + TOK]], axis=0)
        else:
            xs = x0[lo:c * TOK + TOK]
        xs = np.ascontiguousarray(xs)

        pos = np.maximum(np.arange(lo, c * TOK + TOK, dtype=np.float64), 0.0)
        ang = pos[None, :] * inv[:, None]
        cosv = np.cos(ang)
        sinv = np.sin(ang)
        csc = np.ascontiguousarray(np.concatenate([cosv, cosv], axis=0).astype(np.float32))
        css = np.ascontiguousarray(np.concatenate([-sinv, sinv], axis=0).astype(np.float32))

        bk = np.full((128, 12), -CSH, dtype=np.float32)
        if c == 0:
            bk[:, 0:4] = -1e30
        in_maps.append({
            "x0s": xs, "wq_t": wq_t, "wk_t": wk_t, "wv_t": wv_t, "wo_t": wo_t,
            "csc": csc, "css": css, "bk_in": bk, "m0_in": m0, "m4_in": m4,
            "id_in": ident,
            "qg_in": qg, "qu_in": qu, "qd_in": qd,
            "sg_in": sg, "zg_in": zg, "su_in": su, "zu_in": zu,
            "sd_in": sd, "zd_in": zd,
        })
    return in_maps


def kernel(**inputs):
    if "nc" not in _CACHE:
        _CACHE["nc"] = build_program()
    nc = _CACHE["nc"]
    in_maps = _host_inputs(inputs)
    res = run_bass_kernel_spmd(nc, in_maps, core_ids=list(range(NCORES)))
    parts = [np.asarray(res.results[c]["y_t"], dtype=np.float32).T for c in range(NCORES)]
    return np.concatenate(parts, axis=0).reshape(1, S, H)


# version 15
# speedup vs baseline: 1.0270x; 1.0058x over previous
import math
import numpy as np
import ml_dtypes
import concourse.bass as bass
import concourse.mybir as mybir
from concourse.bass_utils import run_bass_kernel_spmd

F32 = mybir.dt.float32
BF16 = mybir.dt.bfloat16
I32 = mybir.dt.int32
AF = mybir.ActivationFunctionType
ALU = mybir.AluOpType

NCORES = 8
S = 8192
H = 4096
TOK = 1024
HALO = 512
LTOK = TOK + HALO
NHT = H // 128
NH = 32
NKV = 8
HD = 128
GQ = NH // NKV
II = 11008
NIT = II // 128
SCALE = 1.0 / math.sqrt(HD)
CSH = 12.0
EPS = 1e-5
THETA = 10000.0
KB = 1024


class Prog:
    def __init__(self):
        self.nc = bass.Bass("TRN2", target_bir_lowering=False)
        self.sems = {}
        self.cnt = {}
        self.eng_sems = {}

    def sem(self, name):
        if name not in self.sems:
            self.sems[name] = self.nc.alloc_semaphore(name)
            self.cnt[name] = 0
        return self.sems[name]

    def inc(self, instr, name, n=1):
        instr.then_inc(self.sem(name), n)
        self.cnt[name] += n
        eng = str(instr.ins.engine)
        self.eng_sems.setdefault(eng, set()).add(name)
        return self.cnt[name]

    def val(self, name):
        self.sem(name)
        return self.cnt[name]

    def wge(self, eng, name, v):
        eng.wait_ge(self.sem(name), v)

    SBUF_BASE = 16512

    def sb(self, name, shape, dtype, off):
        return self.nc.alloc_sbuf_tensor_at(name, shape, dtype,
                                            offset=self.SBUF_BASE + off)


def build_program():
    P = Prog()
    nc = P.nc

    din = {}
    def dram_in(name, shape, dt):
        din[name] = nc.dram_tensor(name, list(shape), dt, kind="ExternalInput").ap()
        return din[name]

    x0s = dram_in("x0s", [LTOK, H], F32)
    wq_t = dram_in("wq_t", [NH, 4, 128, 1024], F32)
    wk_t = dram_in("wk_t", [NKV, 128, NHT * 128], F32)
    wv_t = dram_in("wv_t", [4, 128, NHT * 256], F32)
    wo_t = dram_in("wo_t", [8, 2, 128, NHT * 256], F32)
    csc_in = dram_in("csc", [128, LTOK], F32)
    css_in = dram_in("css", [128, LTOK], F32)
    bk_in = dram_in("bk_in", [128, 12], F32)
    m0_in = dram_in("m0_in", [128, 512], BF16)
    m4_in = dram_in("m4_in", [128, 512], BF16)
    id_in = dram_in("id_in", [128, 128], BF16)
    qg_in = dram_in("qg_in", [II, H], I32)
    qu_in = dram_in("qu_in", [II, H], I32)
    qd_in = dram_in("qd_in", [H, II], I32)
    sg_in = dram_in("sg_in", [II, 32], F32)
    zg_in = dram_in("zg_in", [II, 32], F32)
    su_in = dram_in("su_in", [II, 32], F32)
    zu_in = dram_in("zu_in", [II, 32], F32)
    sd_in = dram_in("sd_in", [H, NIT], F32)
    zd_in = dram_in("zd_in", [H, NIT], F32)

    y_t = nc.dram_tensor("y_t", [H, TOK], F32, kind="ExternalOutput").ap()

    scr_g = nc.dram_tensor("scr_g", [II, H], BF16, kind="Internal").ap()
    scr_u = nc.dram_tensor("scr_u", [II, H], BF16, kind="Internal").ap()
    scr_d2 = nc.dram_tensor("scr_d2", [NIT, 128, H], BF16, kind="Internal").ap()
    scr_x1t = nc.dram_tensor("scr_x1t", [H, TOK], BF16, kind="Internal").ap()

    cs_cos = P.sb("cs_cos", [128, LTOK], F32, 0)
    cs_sin = P.sb("cs_sin", [128, LTOK], F32, 6144)
    m0 = P.sb("m0", [128, 512], BF16, 12288)
    m4 = P.sb("m4", [128, 512], BF16, 13312)
    biask = P.sb("biask", [128, 12], F32, 14336)
    ident = P.sb("ident", [128, 128], BF16, 14400)
    ones_r = P.sb("ones_r", [1, 128], BF16, 14656)
    ones_c = P.sb("ones_c", [128, 1], BF16, 14912)
    c_zero = P.sb("c_zero", [128, 1], F32, 14944)
    c_eps = P.sb("c_eps", [128, 1], F32, 14976)
    OFF_A = 15 * KB
    OFF_D = 79 * KB
    OFF_E = 143 * KB
    OFF_F = 167 * KB
    OFF_G = 191 * KB

    x1nT_h = P.sb("x1nT_h", [128, NHT * HALO], BF16, OFF_A)
    x1nT_o = P.sb("x1nT_o", [128, NHT * TOK], BF16, OFF_A)
    attnT = P.sb("attnT", [128, NH * TOK], BF16, OFF_A)
    x2nT = P.sb("x2nT", [128, NHT * TOK], BF16, OFF_A)
    qT = P.sb("qT", [128, NH * TOK], BF16, OFF_D)
    x1 = P.sb("x1", [128, 8 * H], BF16, OFF_D)
    hT = P.sb("hT", [128, NIT * 512], BF16, OFF_D)
    kT = P.sb("kT", [128, NKV * LTOK], BF16, OFF_E)
    vB = P.sb("vB", [128, 12 * 1024], BF16, OFF_F)

    bar_n = [0]

    block_cm = nc.Block()
    block = block_cm.__enter__()

    def bodies(sync_f=None, tensor_f=None, vector_f=None, scalar_f=None,
               gpsimd_f=None, bar=True):
        def drain(eng, ename):
            for s in sorted(P.eng_sems.get(ename, ())):
                P.wge(eng, s, P.val(s))
            P.eng_sems[ename] = set()
        @block.sync
        def _(sync):
            P.wge(sync, "BAR", bar_n[0])
            if sync_f:
                sync_f(sync)
            if bar:
                drain(sync, "EngineType.SP")
                P.inc(sync.nop(), "BAR")
        @block.tensor
        def _(tensor):
            P.wge(tensor, "BAR", bar_n[0])
            if tensor_f:
                tensor_f(tensor)
            if bar:
                drain(tensor, "EngineType.PE")
                P.inc(tensor.nop(), "BAR")
        @block.vector
        def _(vector):
            P.wge(vector, "BAR", bar_n[0])
            if vector_f:
                vector_f(vector)
            if bar:
                drain(vector, "EngineType.DVE")
                P.inc(vector.nop(), "BAR")
        @block.scalar
        def _(scalar):
            P.wge(scalar, "BAR", bar_n[0])
            if scalar_f:
                scalar_f(scalar)
            if bar:
                drain(scalar, "EngineType.Activation")
                P.inc(scalar.nop(), "BAR")
        @block.gpsimd
        def _(gpsimd):
            P.wge(gpsimd, "BAR", bar_n[0])
            if gpsimd_f:
                gpsimd_f(gpsimd)
            if bar:
                drain(gpsimd, "EngineType.Pool")
                P.inc(gpsimd.nop(), "BAR")
        if bar:
            bar_n[0] = P.val("BAR")

    def p0_sync(sync):
        for dst, src in [(cs_cos, csc_in), (cs_sin, css_in), (m0, m0_in),
                         (m4, m4_in), (biask, bk_in), (ident, id_in)]:
            P.inc(sync.dma_start(dst[:], src[:]), "cLD", 16)
        P.wge(sync, "cLD", P.val("cLD"))

    def p0_vec(vector):
        vector.memset(c_zero[:], 0.0)
        vector.memset(c_eps[:], EPS)
        vector.memset(ones_r[:], 1.0)
        P.inc(vector.memset(ones_c[:], 1.0), "vP")
        P.wge(vector, "vP", P.val("vP"))

    bodies(sync_f=p0_sync, vector_f=p0_vec)

    qsl_g = [P.sb(f"qsl_g{i}", [128, H], I32, OFF_D + i * 16 * KB) for i in range(3)]
    qsl_u = [P.sb(f"qsl_u{i}", [128, H], I32, [OFF_D + 48 * KB, OFF_E, OFF_E + 16 * KB][i]) for i in range(3)]
    wb_g = [P.sb(f"wb_g{i}", [128, H], BF16, OFF_A + 44 * KB + i * 8 * KB) for i in range(2)]
    wb_u = [P.sb(f"wb_u{i}", [128, H], BF16, OFF_G + i * 8 * KB) for i in range(2)]
    s_g = P.sb("s_g", [128, NIT * 32], F32, OFF_A)
    nzs_g = P.sb("nzs_g", [128, NIT * 32], F32, OFF_A + 11 * KB)
    s_u = P.sb("s_u", [128, NIT * 32], F32, OFF_A + 22 * KB)
    nzs_u = P.sb("nzs_u", [128, NIT * 32], F32, OFF_A + 33 * KB)

    vp0 = P.val("vP")
    USPL = 22

    def p1gu_sync(sync):
        for dst, src in [(s_g, sg_in), (nzs_g, zg_in), (s_u, su_in), (nzs_u, zu_in)]:
            P.inc(sync.dma_start(
                dst.ap().rearrange("p (a b) -> p a b", b=32),
                src.rearrange("(a p) b -> p a b", p=128)), "szA", 16)
        for i in range(NIT):
            if i >= 3:
                P.wge(sync, "vDq", i - 2)
            P.inc(sync.dma_start(qsl_g[i % 3][:], qg_in[i * 128:(i + 1) * 128, :]),
                  "qg%d" % (i % 3), 16)
            if i >= 3:
                P.wge(sync, "aDq", i - 2)
                P.wge(sync, "uDv", i - 2)
            P.inc(sync.dma_start(qsl_u[i % 3][:], qu_in[i * 128:(i + 1) * 128, :]),
                  "qu%d" % (i % 3), 16)

    def p1gu_vec(vector):
        P.wge(vector, "szA", 64)
        P.inc(vector.tensor_scalar(nzs_u[:], nzs_u[:], -1.0, None, ALU.mult), "vP")
        P.wge(vector, "vP", P.val("vP"))
        P.inc(vector.tensor_tensor(nzs_u[:], nzs_u[:], s_u[:], ALU.mult), "vP")
        P.inc(vector.tensor_scalar(nzs_g[:], nzs_g[:], -1.0, None, ALU.mult), "vP")
        P.wge(vector, "vP", P.val("vP"))
        P.inc(vector.tensor_tensor(nzs_g[:], nzs_g[:], s_g[:], ALU.mult), "vP")
        P.wge(vector, "vP", P.val("vP"))
        sg3 = s_g.ap().rearrange("p (a b) -> p a b", b=32)
        ng3 = nzs_g.ap().rearrange("p (a b) -> p a b", b=32)
        su3 = s_u.ap().rearrange("p (a b) -> p a b", b=32)
        nu3 = nzs_u.ap().rearrange("p (a b) -> p a b", b=32)
        for i in range(NIT):
            P.wge(vector, "qg%d" % (i % 3), (i // 3 + 1) * 16)
            if i >= 2:
                P.wge(vector, "wg%d" % (i % 2), ((i - 2) // 2 + 1) * 16)
            last = None
            for g in range(32):
                last = vector.tensor_scalar(
                    wb_g[i % 2][:, g * 128:(g + 1) * 128],
                    qsl_g[i % 3][:, g * 128:(g + 1) * 128],
                    sg3[:, i, g:g + 1], ng3[:, i, g:g + 1],
                    ALU.mult, ALU.add)
            P.inc(last, "vDq")
            P.wge(vector, "qu%d" % (i % 3), (i // 3 + 1) * 16)
            if i >= 2:
                P.wge(vector, "wu%d" % (i % 2), ((i - 2) // 2 + 1) * 16)
            last = None
            for g in range(USPL, 32):
                last = vector.tensor_scalar(
                    wb_u[i % 2][:, g * 128:(g + 1) * 128],
                    qsl_u[i % 3][:, g * 128:(g + 1) * 128],
                    su3[:, i, g:g + 1], nu3[:, i, g:g + 1],
                    ALU.mult, ALU.add)
            P.inc(last, "uDv")

    def p1gu_sca(scalar):
        P.wge(scalar, "szA", 64)
        P.wge(scalar, "vP", vp0 + 2)
        su3 = s_u.ap().rearrange("p (a b) -> p a b", b=32)
        nu3 = nzs_u.ap().rearrange("p (a b) -> p a b", b=32)
        for i in range(NIT):
            P.wge(scalar, "qu%d" % (i % 3), (i // 3 + 1) * 16)
            if i >= 2:
                P.wge(scalar, "wu%d" % (i % 2), ((i - 2) // 2 + 1) * 16)
            last = None
            for g in range(USPL):
                last = scalar.activation(
                    wb_u[i % 2][:, g * 128:(g + 1) * 128],
                    qsl_u[i % 3][:, g * 128:(g + 1) * 128],
                    AF.Identity, bias=nu3[:, i, g:g + 1], scale=su3[:, i, g:g + 1])
            P.inc(last, "aDq")
            P.wge(scalar, "aDq", i + 1)
            P.wge(scalar, "uDv", i + 1)
            P.inc(scalar.dma_start(scr_u[i * 128:(i + 1) * 128, :], wb_u[i % 2][:]),
                  "wu%d" % (i % 2), 16)
            P.wge(scalar, "vDq", i + 1)
            P.inc(scalar.dma_start(scr_g[i * 128:(i + 1) * 128, :], wb_g[i % 2][:]),
                  "wg%d" % (i % 2), 16)

    bodies(sync_f=p1gu_sync, vector_f=p1gu_vec, scalar_f=p1gu_sca)

    DS = 16
    qsl_d = [P.sb(f"qsl_d{j}", [128, NHT * 128], I32, OFF_D + j * 16 * KB) for j in range(2)]
    wb_d = [P.sb(f"wb_d{j}", [128, H], BF16, OFF_D + 32 * KB + j * 8 * KB) for j in range(2)]
    sst = [P.sb(f"sst{j}", [128, H], BF16, OFF_D + 48 * KB + j * 8 * KB) for j in range(2)]
    s_d = P.sb("s_d", [128, NHT * NIT], F32, OFF_A)
    nzs_d = P.sb("nzs_d", [128, NHT * NIT], F32, OFF_A + 11 * KB)

    ptd_cm = [nc.psum_tensor(f"ptd{j}", [128, H], BF16) for j in range(2)]
    ptd = [cm.__enter__() for cm in ptd_cm]
    qd3 = qd_in.rearrange("(a p) q -> p a q", p=128)

    def p1d_sync(sync):
        for dst, src in [(s_d, sd_in), (nzs_d, zd_in)]:
            P.inc(sync.dma_start(
                dst.ap().rearrange("p (a b) -> p a b", b=NIT),
                src.rearrange("(a p) b -> p a b", p=128)), "szD", 16)
        for it in range(NIT):
            if it >= 2:
                P.wge(sync, "vDd", it - 1)
                P.wge(sync, "aDd", it - 1)
            P.inc(sync.dma_start(
                qsl_d[it % 2].ap().rearrange("p (a q) -> p a q", q=128),
                qd3[:, :, it * 128:(it + 1) * 128]), "qd%d" % (it % 2), 16)

    def p1d_pe(tensor):
        for it in range(NIT):
            P.wge(tensor, "vDd", it + 1)
            P.wge(tensor, "aDd", it + 1)
            if it >= 2:
                P.wge(tensor, "vEv", it - 1)
                P.wge(tensor, "aEv", it - 1)
            last = None
            for hs in range(NHT):
                last = tensor.transpose(
                    ptd[it % 2][:, hs * 128:(hs + 1) * 128],
                    wb_d[it % 2][:, hs * 128:(hs + 1) * 128], ident[:])
            P.inc(last, "pTd")

    def p1d_vec(vector):
        P.wge(vector, "szD", 32)
        P.inc(vector.tensor_scalar(nzs_d[:], nzs_d[:], -1.0, None, ALU.mult), "vP")
        P.wge(vector, "vP", P.val("vP"))
        P.inc(vector.tensor_tensor(nzs_d[:], nzs_d[:], s_d[:], ALU.mult), "vP")
        P.wge(vector, "vP", P.val("vP"))
        sd3 = s_d.ap().rearrange("p (a b) -> p a b", b=NIT)
        nd3 = nzs_d.ap().rearrange("p (a b) -> p a b", b=NIT)
        q3 = [q.ap().rearrange("p (a q) -> p a q", q=128) for q in qsl_d]

        def evac(it):
            P.wge(vector, "pTd", it + 1)
            if it >= 2:
                P.wge(vector, "w2%d" % (it % 2), (it // 2) * 16)
            P.inc(vector.tensor_copy(sst[it % 2][:, 0:DS * 128],
                                     ptd[it % 2][:, 0:DS * 128]), "vEv")

        for it in range(NIT):
            P.wge(vector, "qd%d" % (it % 2), (it // 2 + 1) * 16)
            if it >= 2:
                P.wge(vector, "pTd", it - 1)
            last = None
            for hs in range(DS):
                last = vector.tensor_scalar(
                    wb_d[it % 2][:, hs * 128:(hs + 1) * 128],
                    q3[it % 2][:, hs, :],
                    sd3[:, hs, it:it + 1], nd3[:, hs, it:it + 1],
                    ALU.mult, ALU.add)
            P.inc(last, "vDd")
            if it >= 1:
                evac(it - 1)
        evac(NIT - 1)

    def p1d_sca(scalar):
        P.wge(scalar, "vP", P.val("vP"))
        sd3 = s_d.ap().rearrange("p (a b) -> p a b", b=NIT)
        nd3 = nzs_d.ap().rearrange("p (a b) -> p a b", b=NIT)
        q3 = [q.ap().rearrange("p (a q) -> p a q", q=128) for q in qsl_d]

        def evst(it):
            P.wge(scalar, "pTd", it + 1)
            if it >= 2:
                P.wge(scalar, "w2%d" % (it % 2), (it // 2) * 16)
            P.inc(scalar.activation(sst[it % 2][:, DS * 128:], ptd[it % 2][:, DS * 128:],
                                    AF.Copy), "aEv")
            P.wge(scalar, "aEv", it + 1)
            P.wge(scalar, "vEv", it + 1)
            P.inc(scalar.dma_start(scr_d2[it, :, :], sst[it % 2][:]),
                  "w2%d" % (it % 2), 16)

        for it in range(NIT):
            P.wge(scalar, "qd%d" % (it % 2), (it // 2 + 1) * 16)
            if it >= 2:
                P.wge(scalar, "pTd", it - 1)
            last = None
            for hs in range(DS, NHT):
                last = scalar.activation(
                    wb_d[it % 2][:, hs * 128:(hs + 1) * 128],
                    q3[it % 2][:, hs, :],
                    AF.Identity, bias=nd3[:, hs, it:it + 1], scale=sd3[:, hs, it:it + 1])
            P.inc(last, "aDd")
            if it >= 1:
                evst(it - 1)
        evst(NIT - 1)

    bodies(sync_f=p1d_sync, tensor_f=p1d_pe, vector_f=p1d_vec, scalar_f=p1d_sca)
    for cm in ptd_cm[::-1]:
        cm.__exit__(None, None, None)

    x0st = [P.sb(f"x0st{i}", [128, H], F32, OFF_D + i * 16 * KB) for i in range(2)]
    junk = [P.sb(f"junk{i}", [128, H], BF16, OFF_D + 32 * KB + i * 8 * KB) for i in range(2)]
    xnst = [P.sb(f"xnst{i}", [128, H], BF16, OFF_D + 48 * KB + i * 8 * KB) for i in range(2)]
    stt = [P.sb(f"stt{i}", [128, 8], F32, 15008 + i * 64) for i in range(2)]

    ptr_cm = [nc.psum_tensor(f"ptr{i}", [128, 1024], BF16) for i in range(2)]
    ptr = [cm.__enter__() for cm in ptr_cm]

    def emit_ln_tiles(rows_list, dst, dstW, tag, prewaits=None):
        n = len(rows_list)
        aN = "aN" + tag; aS = "aS" + tag; aL = "aL" + tag
        vS = "vS" + tag; vT = "vT" + tag; pT = "pT" + tag
        pw = prewaits or {}

        def f_sync(sync):
            for eng_sem, v in pw.get("sync", []):
                P.wge(sync, eng_sem, v)
            for t in range(n):
                if t >= 2:
                    P.wge(sync, aN, t - 1)
                P.inc(sync.dma_start(x0st[t % 2][:],
                                     x0s[rows_list[t][0]:rows_list[t][0] + 128, :]),
                      "x0%s%d" % (tag, t % 2), 16)

        def f_sca(scalar):
            for eng_sem, v in pw.get("scalar", []):
                P.wge(scalar, eng_sem, v)
            for t in range(n):
                P.wge(scalar, "x0%s%d" % (tag, t % 2), (t // 2 + 1) * 16)
                if t >= 2:
                    P.wge(scalar, vS, 5 * (t - 1))
                st = stt[t % 2]
                P.inc(scalar.activation(junk[0][:], x0st[t % 2][:], AF.Square,
                                        bias=c_zero[:], accum_out=st[:, 1:2]), aS)
                P.inc(scalar.activation(junk[1][:], x0st[t % 2][:], AF.Copy,
                                        accum_out=st[:, 0:1]), aS)
                P.wge(scalar, vS, 5 * t + 4)
                P.inc(scalar.activation(st[:, 4:5], st[:, 3:4], AF.Ln, bias=c_eps[:]), aL)
                P.wge(scalar, aL, P.val(aL))
                P.inc(scalar.activation(st[:, 4:5], st[:, 4:5], AF.Exp, bias=c_zero[:],
                                        scale=-0.5), aL)
                P.wge(scalar, vS, 5 * t + 5)
                P.inc(scalar.activation(xnst[t % 2][:], x0st[t % 2][:], AF.Identity,
                                        bias=st[:, 5:6], scale=st[:, 4:5]), aN)

        def f_vec(vector):
            for eng_sem, v in pw.get("vector", []):
                P.wge(vector, eng_sem, v)
            for t in range(n):
                st = stt[t % 2]
                P.wge(vector, aS, 2 * t + 2)
                P.inc(vector.tensor_scalar(st[:, 2:3], st[:, 0:1], 1.0 / H, None,
                                           ALU.mult), vS)
                P.inc(vector.tensor_scalar(st[:, 3:4], st[:, 1:2], 1.0 / H, None,
                                           ALU.mult), vS)
                P.wge(vector, vS, 5 * t + 2)
                P.inc(vector.tensor_tensor(st[:, 6:7], st[:, 2:3], st[:, 2:3], ALU.mult), vS)
                P.wge(vector, vS, 5 * t + 3)
                P.inc(vector.tensor_tensor(st[:, 3:4], st[:, 3:4], st[:, 6:7],
                                           ALU.subtract), vS)
                P.wge(vector, aL, 2 * t + 2)
                P.inc(vector.tensor_scalar(st[:, 5:6], st[:, 2:3], st[:, 4:5], -1.0,
                                           ALU.mult, ALU.mult), vS)
                col0 = rows_list[t][1]
                for ht in range(NHT):
                    P.wge(vector, pT, NHT * t + ht + 1)
                    P.inc(vector.tensor_copy(
                        dst[:, ht * dstW + col0: ht * dstW + col0 + 128],
                        ptr[(NHT * t + ht) % 2][:, 0:128]), vT)

        def f_pe(tensor):
            for eng_sem, v in pw.get("tensor", []):
                P.wge(tensor, eng_sem, v)
            for t in range(n):
                P.wge(tensor, aN, t + 1)
                for ht in range(NHT):
                    if NHT * t + ht >= 2:
                        P.wge(tensor, vT, NHT * t + ht - 1)
                    P.inc(tensor.transpose(ptr[(NHT * t + ht) % 2][:, 0:128],
                                           xnst[t % 2][:, ht * 128:(ht + 1) * 128],
                                           ident[:]), pT)

        return f_sync, f_pe, f_vec, f_sca

    halo_rows = [(i * 128, i * 128) for i in range(4)]
    fs, fp, fv, fa = emit_ln_tiles(halo_rows, x1nT_h, HALO, "h")
    bodies(sync_f=fs, tensor_f=fp, vector_f=fv, scalar_f=fa, bar=False)

    wkv_sl = P.sb("wkv_sl", [128, NHT * 256], BF16, OFF_G)
    wk_sl = P.sb("wk_sl", [128, NHT * 128], BF16, OFF_G)
    rt1a = P.sb("rt1a", [128, 512], F32, OFF_G + 8 * KB)
    rt2a = P.sb("rt2a", [128, 512], F32, OFF_G + 10 * KB)

    pk_cm = [nc.psum_tensor(f"pk{i}", [128, 512], F32) for i in range(2)]
    pk = [cm.__enter__() for cm in pk_cm]
    pvv_cm = [nc.psum_tensor(f"pvv{i}", [128, 512], F32) for i in range(2)]
    pvv = [cm.__enter__() for cm in pvv_cm]

    def rope_evac(vector, psrc, c0, dst, dcols, tag, rt1, rt2):
        P.wge(vector, tag, P.val(tag))
        W = 512
        P.inc(vector.tensor_tensor(rt1[0:64, :], psrc[64:128, :],
                                   cs_sin[0:64, c0:c0 + W], ALU.mult), "vR")
        P.inc(vector.tensor_tensor(rt1[64:128, :], psrc[0:64, :],
                                   cs_sin[64:128, c0:c0 + W], ALU.mult), "vR")
        P.inc(vector.tensor_tensor(rt2[:, :], psrc[:, :],
                                   cs_cos[:, c0:c0 + W], ALU.mult), "vR")
        P.wge(vector, "vR", P.val("vR"))
        P.inc(vector.tensor_tensor(dst[:, dcols:dcols + W], rt2[:, :], rt1[:, :],
                                   ALU.add), tag)

    def rope_evac_q(vector, psrc, c0, dst, dcols, tag, rt1):
        P.wge(vector, tag, P.val(tag))
        W = 512
        P.inc(vector.tensor_tensor(rt1[0:64, :], psrc[64:128, :],
                                   cs_sin[0:64, c0:c0 + W], ALU.mult), "vR")
        P.inc(vector.tensor_tensor(rt1[64:128, :], psrc[0:64, :],
                                   cs_sin[64:128, c0:c0 + W], ALU.mult), "vR")
        P.inc(vector.tensor_tensor(dst[:, dcols:dcols + W], psrc[:, :],
                                   cs_cos[:, c0:c0 + W], ALU.mult), "vR")
        P.wge(vector, "vR", P.val("vR"))
        P.inc(vector.tensor_tensor(dst[:, dcols:dcols + W], dst[:, dcols:dcols + W],
                                   rt1[:, :], ALU.add), tag)

    def p3a_gp(gpsimd):
        for kv in range(NKV):
            if kv >= 1:
                P.wge(gpsimd, "pK", kv)
                P.wge(gpsimd, "vK", kv)
            P.inc(gpsimd.dma_start(wk_sl[:], wk_t[kv]), "lKV", 16)

    lkv_a = P.val("lKV")

    def p3a_pe(tensor):
        P.wge(tensor, "vTh", 4 * NHT)
        for kv in range(NKV):
            P.wge(tensor, "lKV", lkv_a + (kv + 1) * 16)
            if kv >= 2:
                P.wge(tensor, "vK", kv - 1)
            last = None
            for ht in range(NHT):
                last = tensor.matmul(pk[kv % 2][:],
                                     wk_sl[:, ht * 128:(ht + 1) * 128],
                                     x1nT_h[:, ht * HALO:(ht + 1) * HALO],
                                     start=(ht == 0), stop=(ht == NHT - 1))
            P.inc(last, "pK")

    def p3a_vec(vector):
        for kv in range(NKV):
            P.wge(vector, "pK", kv + 1)
            rope_evac(vector, pk[kv % 2], 0, kT, kv * LTOK, "vK", rt1a, rt2a)

    bodies(tensor_f=p3a_pe, vector_f=p3a_vec, gpsimd_f=p3a_gp, bar=False)

    def emit_v(jts, xsrc, xW, jt_base, tag):
        lkv_b = P.val("lKV")
        pV = "pV" + tag; vV = "vV" + tag
        nj = len(jts)

        def f_gp(gpsimd):
            for vc in range(4):
                if vc >= 1:
                    P.wge(gpsimd, pV, vc * nj)
                else:
                    P.wge(gpsimd, "pK", P.val("pK"))
                    P.wge(gpsimd, "vK", P.val("vK"))
                P.inc(gpsimd.dma_start(wkv_sl[:], wv_t[vc]), "lKV", 16)

        def f_pe(tensor):
            for vc in range(4):
                P.wge(tensor, "lKV", lkv_b + (vc + 1) * 16)
                for ji, jt in enumerate(jts):
                    k = vc * nj + ji
                    if k >= 2:
                        P.wge(tensor, vV, k - 1)
                    elif tag == "b":
                        P.wge(tensor, "vVa", 15 + k)
                    col = (jt - jt_base) * 128
                    last = None
                    for ht in range(NHT):
                        last = tensor.matmul(
                            pvv[k % 2][:, 0:256],
                            xsrc[:, ht * xW + col: ht * xW + col + 128],
                            wkv_sl[:, ht * 256:(ht + 1) * 256],
                            start=(ht == 0), stop=(ht == NHT - 1))
                    P.inc(last, pV)

        def f_vec(vector):
            for vc in range(4):
                for ji, jt in enumerate(jts):
                    k = vc * nj + ji
                    P.wge(vector, pV, k + 1)
                    P.inc(vector.tensor_copy(
                        vB[:, jt * 1024 + vc * 256: jt * 1024 + (vc + 1) * 256],
                        pvv[k % 2][:, 0:256]), vV)

        return f_gp, f_pe, f_vec

    fg, fpe, fve = emit_v([0, 1, 2, 3], x1nT_h, HALO, 0, "a")
    bodies(tensor_f=fpe, vector_f=fve, gpsimd_f=fg, bar=False)

    own_rows = [(HALO + i * 128, i * 128) for i in range(8)]
    fs, fpe, fve, fae = emit_ln_tiles(
        own_rows, x1nT_o, TOK, "o",
        prewaits={
            "scalar": [("aNh", 4), ("vSh", 20), ("pTh", 4 * NHT), ("pK", 8), ("pVa", 16)],
            "sync": [("aNh", 4)],
            "vector": [("vSh", 20)],
            "tensor": [("vTh", 4 * NHT), ("pVa", 16)],
        })
    bodies(sync_f=fs, tensor_f=fpe, vector_f=fve, scalar_f=fae, bar=False)

    pk_b = P.val("pK")
    lkv_c = P.val("lKV")

    def p3b_gp(gpsimd):
        for kv in range(NKV):
            if kv >= 1:
                P.wge(gpsimd, "pK", pk_b + 2 * kv)
                P.wge(gpsimd, "vK", NKV + 2 * kv)
            else:
                P.wge(gpsimd, "pVa", 16)
                P.wge(gpsimd, "vVa", 16)
            P.inc(gpsimd.dma_start(wk_sl[:], wk_t[kv]), "lKV", 16)

    def p3b_pe(tensor):
        P.wge(tensor, "vTo", 8 * NHT)
        for kv in range(NKV):
            P.wge(tensor, "lKV", lkv_c + (kv + 1) * 16)
            for ch in range(2):
                k = 2 * kv + ch
                if k >= 2:
                    P.wge(tensor, "vK", NKV + k - 1)
                else:
                    P.wge(tensor, "vK", NKV - 2 + k + 1)
                last = None
                for ht in range(NHT):
                    last = tensor.matmul(
                        pk[k % 2][:],
                        wk_sl[:, ht * 128:(ht + 1) * 128],
                        x1nT_o[:, ht * TOK + ch * 512:ht * TOK + (ch + 1) * 512],
                        start=(ht == 0), stop=(ht == NHT - 1))
                P.inc(last, "pK")

    def p3b_vec(vector):
        for kv in range(NKV):
            for ch in range(2):
                k = 2 * kv + ch
                P.wge(vector, "pK", pk_b + k + 1)
                rope_evac(vector, pk[k % 2], HALO + ch * 512, kT,
                          kv * LTOK + HALO + ch * 512, "vK", rt1a, rt2a)

    bodies(tensor_f=p3b_pe, vector_f=p3b_vec, gpsimd_f=p3b_gp, bar=False)

    fg, fpe, fve = emit_v([4, 5, 6, 7, 8, 9, 10, 11], x1nT_o, TOK, 4, "b")
    bodies(tensor_f=fpe, vector_f=fve, gpsimd_f=fg, bar=False)


    for cm in pvv_cm[::-1]:
        cm.__exit__(None, None, None)
    for cm in pk_cm[::-1]:
        cm.__exit__(None, None, None)

    wq_sl = [P.sb(f"wq_sl{i}", [128, 8 * 128], BF16, OFF_G + i * 2 * KB) for i in range(6)]
    wqf = [P.sb(f"wqf{i}", [128, 512], F32, OFF_G + 12 * KB + i * 2 * KB) for i in range(2)]
    pq_cm = [nc.psum_tensor(f"pq{i}", [128, 512], F32) for i in range(2)]
    pq = [cm.__enter__() for cm in pq_cm]
    rt1q_cm = nc.psum_tensor("rt1q", [128, 512], F32)
    rt1q = rt1q_cm.__enter__()

    def p4_sync(sync):
        P.wge(sync, "pVb", 32)
        P.wge(sync, "vVb", 32)
        for u in range(8 * NH):
            qt, a, hf = u // 8, (u % 8) // 2, u % 2
            if u >= 2:
                P.wge(sync, "aQC", u - 1)
            P.inc(sync.dma_start(
                wqf[u % 2][:],
                wq_t[qt, a, :, hf * 512:(hf + 1) * 512]), "lqf%d" % (u % 2), 16)

    def p4_sca(scalar):
        P.wge(scalar, "pVb", 32)
        P.wge(scalar, "vK", NKV + 2 * NKV)
        for u in range(8 * NH):
            qt, a, hf = u // 8, (u % 8) // 2, u % 2
            P.wge(scalar, "lqf%d" % (u % 2), (u // 2 + 1) * 16)
            q = 4 * qt + a
            if hf == 0 and q >= 6:
                P.wge(scalar, "pQ", 2 * ((q - 6) // 4 + 1))
            P.inc(scalar.activation(
                wq_sl[q % 6][:, hf * 512:(hf + 1) * 512],
                wqf[u % 2][:], AF.Copy), "aQC")

    def p4_pe(tensor):
        P.wge(tensor, "vTo", 8 * NHT)
        P.wge(tensor, "vVb", 32)
        for qt in range(NH):
            P.wge(tensor, "aQC", 8 * (qt + 1))
            for ch in range(2):
                k = 2 * qt + ch
                if k >= 2:
                    P.wge(tensor, "vQ", k - 1)
                else:
                    P.wge(tensor, "vK", NKV + 16)
                last = None
                for ht in range(NHT):
                    last = tensor.matmul(
                        pq[k % 2][:],
                        wq_sl[(4 * qt + ht // 8) % 6][:, (ht % 8) * 128:(ht % 8 + 1) * 128],
                        x1nT_o[:, ht * TOK + ch * 512:ht * TOK + (ch + 1) * 512],
                        start=(ht == 0), stop=(ht == NHT - 1))
                P.inc(last, "pQ")

    def p4_vec(vector):
        P.wge(vector, "pVb", 32)
        P.wge(vector, "vK", NKV + 2 * NKV)
        for qt in range(NH):
            for ch in range(2):
                k = 2 * qt + ch
                P.wge(vector, "pQ", k + 1)
                rope_evac_q(vector, pq[k % 2], HALO + ch * 512, qT,
                            qt * TOK + ch * 512, "vQ", rt1q)

    bodies(sync_f=p4_sync, tensor_f=p4_pe, vector_f=p4_vec, scalar_f=p4_sca, bar=False)

    rt1q_cm.__exit__(None, None, None)
    for cm in pq_cm[::-1]:
        cm.__exit__(None, None, None)
    for cm in ptr_cm[::-1]:
        cm.__exit__(None, None, None)

    probs = [P.sb(f"probs{i}", [128, 5 * 512], BF16, OFF_G + i * 5 * KB) for i in range(2)]
    bc_sb = [P.sb(f"bc_sb{i}", [128, 512], BF16, OFF_G + 10 * KB + i * KB) for i in range(2)]
    r_sb = P.sb("r_sb", [1, 512], F32, OFF_G + 12 * KB)
    r_bf = P.sb("r_bf", [1, 512], BF16, OFF_G + 14 * KB)

    ps_s_cm = nc.psum_tensor("ps_s", [128, 5 * 512], F32)
    ps_s = ps_s_cm.__enter__()
    ps_pv_cm = nc.psum_tensor("ps_pv", [128, 512], F32)
    ps_pv = ps_pv_cm.__enter__()
    ps_sum_cm = nc.psum_tensor("ps_sum", [1, 512], F32)
    ps_sum = ps_sum_cm.__enter__()
    ps_bc_cm = nc.psum_tensor("ps_bc", [128, 512], F32)
    ps_bc = ps_bc_cm.__enter__()

    qT3 = qT.ap().rearrange("p (h t) -> p h t", t=TOK)
    attnT3 = attnT.ap().rearrange("p (h t) -> p h t", t=TOK)

    NID = NKV * 8

    def p5a_scores(tensor, idx):
        kv, qt = idx // 8, idx % 8
        rhs_q = qT3[:, kv * GQ:(kv + 1) * GQ, qt * 128:(qt + 1) * 128]
        for j in range(5):
            if idx >= 1:
                P.wge(tensor, "aEX", 5 * (idx - 1) + j + 1)
            P.inc(tensor.matmul(
                ps_s[:, j * 512:(j + 1) * 512],
                kT[:, kv * LTOK + (qt + j) * 128: kv * LTOK + (qt + j + 1) * 128],
                rhs_q, start=True, stop=True), "pSC")

    def p5a_tail(tensor, idx):
        kv, qt = idx // 8, idx % 8
        pb = probs[idx % 2]
        P.wge(tensor, "vMK", 2 * (idx + 1))
        P.wge(tensor, "vEV", idx)
        last = None
        for j in range(5):
            last = tensor.matmul(
                ps_pv[:],
                vB[:, (qt + j) * 1024 + kv * 128:(qt + j) * 1024 + (kv + 1) * 128],
                pb[:, j * 512:(j + 1) * 512],
                start=(j == 0), stop=(j == 4))
        P.inc(last, "pPV")
        P.wge(tensor, "aRC", 2 * idx)
        last = None
        for j in range(5):
            last = tensor.matmul(
                ps_sum[:],
                ones_c[:],
                pb[:, j * 512:(j + 1) * 512],
                start=(j == 0), stop=(j == 4))
        P.inc(last, "pSM")
        P.wge(tensor, "aRC", 2 * (idx + 1))
        P.wge(tensor, "vBC", idx)
        P.inc(tensor.matmul(ps_bc[:], ones_r[:], r_bf[:],
                            start=True, stop=True), "pBC")

    def p5a_pe(tensor):
        P.wge(tensor, "vQ", 2 * NH)
        p5a_scores(tensor, 0)
        for idx in range(1, NID):
            p5a_scores(tensor, idx)
            p5a_tail(tensor, idx - 1)
        p5a_tail(tensor, NID - 1)

    def p5a_sca(scalar):
        for idx in range(NID):
            for j in range(5):
                P.wge(scalar, "pSC", 5 * idx + j + 1)
                if idx >= 2 and j == 0:
                    P.wge(scalar, "pSM", idx - 1)
                P.inc(scalar.activation(
                    probs[idx % 2][:, j * 512:(j + 1) * 512],
                    ps_s[:, j * 512:(j + 1) * 512],
                    AF.Exp, bias=biask[:, idx % 8 + j:idx % 8 + j + 1], scale=SCALE),
                    "aEX")
            P.wge(scalar, "pSM", idx + 1)
            P.inc(scalar.activation(r_sb[:], ps_sum[:], AF.Ln, bias=c_zero[0:1, :]), "aRC")
            P.wge(scalar, "aRC", P.val("aRC"))
            P.inc(scalar.activation(r_bf[:], r_sb[:], AF.Exp, bias=c_zero[0:1, :],
                                    scale=-1.0), "aRC")

    def p5a_vec(vector):
        for idx in range(NID):
            kv, qt = idx // 8, idx % 8
            pb = probs[idx % 2]
            P.wge(vector, "aEX", 5 * idx + 5)
            P.inc(vector.tensor_tensor(pb[:, 0:512], pb[:, 0:512], m0[:],
                                       ALU.mult), "vMK")
            P.inc(vector.tensor_tensor(pb[:, 4 * 512:5 * 512],
                                       pb[:, 4 * 512:5 * 512], m4[:],
                                       ALU.mult), "vMK")
            P.wge(vector, "pPV", idx + 1)
            P.wge(vector, "pBC", idx + 1)
            P.inc(vector.tensor_copy(bc_sb[idx % 2][:], ps_bc[:]), "vBC")
            P.wge(vector, "vBC", idx + 1)
            P.inc(vector.tensor_tensor(
                attnT3[:, kv * GQ:(kv + 1) * GQ, qt * 128:(qt + 1) * 128],
                ps_pv[:], bc_sb[idx % 2][:], ALU.mult), "vEV")

    bodies(tensor_f=p5a_pe, vector_f=p5a_vec, scalar_f=p5a_sca)

    for cm in [ps_bc_cm, ps_sum_cm, ps_pv_cm, ps_s_cm]:
        cm.__exit__(None, None, None)

    wo_sl = [P.sb(f"wo_sl{i}", [128, NHT * 256], BF16, OFF_E + i * 16 * KB) for i in range(3)]
    x0c = P.sb("x0c", [128, 8 * 512], F32, OFF_G)
    po_cm = [nc.psum_tensor(f"po{i}", [128, 512], F32) for i in range(2)]
    po = [cm.__enter__() for cm in po_cm]
    def p5b_sync(sync):
        for oc in range(8):
            if oc >= 1:
                P.wge(sync, "vO", 8 * oc)
            P.inc(sync.dma_start(
                x0c.ap().rearrange("p (t q) -> p t q", q=512),
                x0s.rearrange("(t p) q -> p t q", p=128)[:, 4:12, oc * 512:(oc + 1) * 512]),
                "lxc", 16)

    def p5b_gp(gpsimd):
        h = 0
        for oc in range(8):
            for half in range(2):
                if h >= 3:
                    P.wge(gpsimd, "pO", 8 * ((h - 3) // 2) + 8)
                P.inc(gpsimd.dma_start(wo_sl[h % 3][:], wo_t[oc, half]),
                      "lo%d" % (h % 3), 16)
                h += 1

    def p5b_pe(tensor):
        for oc in range(8):
            P.wge(tensor, "lo%d" % ((2 * oc) % 3), ((2 * oc) // 3 + 1) * 16)
            P.wge(tensor, "lo%d" % ((2 * oc + 1) % 3), ((2 * oc + 1) // 3 + 1) * 16)
            for tt in range(8):
                k = 8 * oc + tt
                if k >= 2:
                    P.wge(tensor, "vO", k - 1)
                last = None
                for half in range(2):
                    sl = wo_sl[(2 * oc + half) % 3]
                    for dt in range(NHT):
                        last = tensor.matmul(
                            po[k % 2][:, half * 256:(half + 1) * 256],
                            attnT[:, dt * TOK + tt * 128:dt * TOK + (tt + 1) * 128],
                            sl[:, dt * 256:(dt + 1) * 256],
                            start=(dt == 0), stop=(dt == NHT - 1))
                P.inc(last, "pO")

    def p5b_vec(vector):
        for oc in range(8):
            for tt in range(8):
                k = 8 * oc + tt
                P.wge(vector, "pO", k + 1)
                P.wge(vector, "lxc", (oc + 1) * 16)
                P.inc(vector.tensor_tensor(
                    x1[:, tt * H + oc * 512:tt * H + (oc + 1) * 512],
                    po[k % 2][:], x0c[:, tt * 512:(tt + 1) * 512], ALU.add), "vO")

    bodies(sync_f=p5b_sync, tensor_f=p5b_pe, vector_f=p5b_vec, gpsimd_f=p5b_gp)

    for cm in po_cm[::-1]:
        cm.__exit__(None, None, None)

    x2st = [P.sb(f"x2st{i}", [128, H], BF16, OFF_E + i * 8 * KB) for i in range(2)]
    x1tst = [P.sb(f"x1tst{i}", [128, NHT * 128], BF16, OFF_E + 16 * KB + i * 8 * KB) for i in range(2)]
    junkb = [P.sb(f"junkb{i}", [128, H], BF16, OFF_F + 8 * KB + i * 8 * KB) for i in range(2)]

    ptr2_cm = [nc.psum_tensor(f"ptr2_{i}", [128, 1024], BF16) for i in range(2)]
    ptr2 = [cm.__enter__() for cm in ptr2_cm]

    def p5c_sca(scalar):
        for tt in range(8):
            st = stt[tt % 2]
            if tt >= 2:
                P.wge(scalar, "vS2", 5 * (tt - 1))
            xt = x1[:, tt * H:(tt + 1) * H]
            P.inc(scalar.activation(junkb[0][:], xt, AF.Square, bias=c_zero[:],
                                    accum_out=st[:, 1:2]), "aS2")
            P.inc(scalar.activation(junkb[1][:], xt, AF.Copy, accum_out=st[:, 0:1]), "aS2")
            P.wge(scalar, "vS2", 5 * tt + 4)
            P.inc(scalar.activation(st[:, 4:5], st[:, 3:4], AF.Ln, bias=c_eps[:]), "aL2")
            P.wge(scalar, "aL2", P.val("aL2"))
            P.inc(scalar.activation(st[:, 4:5], st[:, 4:5], AF.Exp, bias=c_zero[:],
                                    scale=-0.5), "aL2")
            P.wge(scalar, "vS2", 5 * tt + 5)
            if tt >= 2:
                P.wge(scalar, "vT2", 64 * (tt - 1))
            P.inc(scalar.activation(x2st[tt % 2][:], xt, AF.Identity,
                                    bias=st[:, 5:6], scale=st[:, 4:5]), "aN2")

    def p5c_vec(vector):
        for tt in range(8):
            st = stt[tt % 2]
            P.wge(vector, "aS2", 2 * tt + 2)
            P.inc(vector.tensor_scalar(st[:, 2:3], st[:, 0:1], 1.0 / H, None, ALU.mult), "vS2")
            P.inc(vector.tensor_scalar(st[:, 3:4], st[:, 1:2], 1.0 / H, None, ALU.mult), "vS2")
            P.wge(vector, "vS2", 5 * tt + 2)
            P.inc(vector.tensor_tensor(st[:, 6:7], st[:, 2:3], st[:, 2:3], ALU.mult), "vS2")
            P.wge(vector, "vS2", 5 * tt + 3)
            P.inc(vector.tensor_tensor(st[:, 3:4], st[:, 3:4], st[:, 6:7], ALU.subtract), "vS2")
            P.wge(vector, "aL2", 2 * tt + 2)
            P.inc(vector.tensor_scalar(st[:, 5:6], st[:, 2:3], st[:, 4:5], -1.0,
                                       ALU.mult, ALU.mult), "vS2")
            for ht in range(NHT):
                P.wge(vector, "pT2", 64 * tt + ht + 1)
                P.inc(vector.tensor_copy(
                    x2nT[:, ht * TOK + tt * 128:ht * TOK + (tt + 1) * 128],
                    ptr2[(64 * tt + ht) % 2][:, 0:128]), "vT2")
            for ht in range(NHT):
                P.wge(vector, "pT2", 64 * tt + 32 + ht + 1)
                P.inc(vector.tensor_copy(
                    x1tst[tt % 2][:, ht * 128:(ht + 1) * 128],
                    ptr2[(64 * tt + 32 + ht) % 2][:, 0:128]), "vT2")


    def p5c_pe(tensor):
        for tt in range(8):
            P.wge(tensor, "aN2", tt + 1)
            for ht in range(NHT):
                if 64 * tt + ht >= 2:
                    P.wge(tensor, "vT2", 64 * tt + ht - 1)
                P.inc(tensor.transpose(ptr2[(64 * tt + ht) % 2][:, 0:128],
                                       x2st[tt % 2][:, ht * 128:(ht + 1) * 128],
                                       ident[:]), "pT2")
            if tt >= 2:
                P.wge(tensor, "sx%d" % (tt % 2), (tt // 2) * 32)
            for ht in range(NHT):
                P.wge(tensor, "vT2", 64 * tt + 32 + ht - 1)
                P.inc(tensor.transpose(ptr2[(64 * tt + 32 + ht) % 2][:, 0:128],
                                       x1[:, tt * H + ht * 128:tt * H + (ht + 1) * 128],
                                       ident[:]), "pT2")

    def p5c_gp(gpsimd):
        for tt in range(8):
            P.wge(gpsimd, "vT2", 64 * tt + 64)
            for hh in range(2):
                P.inc(gpsimd.dma_start(
                    scr_x1t.rearrange("(a p) t -> p a t", p=128)[:, hh * 16:(hh + 1) * 16,
                                                                tt * 128:(tt + 1) * 128],
                    x1tst[tt % 2].ap().rearrange("p (a q) -> p a q", q=128)[:, hh * 16:(hh + 1) * 16, :]),
                    "sx%d" % (tt % 2), 16)
        for s in ["sx0", "sx1"]:
            P.wge(gpsimd, s, P.val(s))

    bodies(tensor_f=p5c_pe, vector_f=p5c_vec, scalar_f=p5c_sca, gpsimd_f=p5c_gp)

    for cm in ptr2_cm[::-1]:
        cm.__exit__(None, None, None)

    wgu_sl = [P.sb(f"wgu_sl{i}", [128, NHT * 128], BF16, OFF_F + i * 8 * KB) for i in range(4)]
    wgu_sl[3] = P.sb("wgu_sl3", [128, NHT * 128], BF16, OFF_G)
    sil_st = [P.sb(f"sil_st{i}", [128, 512], F32, OFF_G + 8 * KB + i * 2 * KB) for i in range(2)]
    wd_sl = [P.sb(f"wd_sl{i}", [128, 8 * 1024], BF16, OFF_F + i * 16 * KB) for i in range(2)]
    x1t_sl = [P.sb(f"x1t_sl{i}", [128, 512], BF16, OFF_G + 8 * KB + i * KB) for i in range(2)]
    out_st = [P.sb(f"out_st{i}", [128, 512], F32, OFF_G + 10 * KB + i * 2 * KB) for i in range(2)]

    NGRP = 11

    for tc in range(2):
        pg_cm = [nc.psum_tensor(f"pgg{tc}_{i}", [128, 512], F32) for i in range(2)]
        pgg = [cm.__enter__() for cm in pg_cm]
        pu_cm = [nc.psum_tensor(f"puu{tc}_{i}", [128, 512], F32) for i in range(2)]
        puu = [cm.__enter__() for cm in pu_cm]
        tg_base = [P.val("tg0"), P.val("tg1")]
        tu_base = [P.val("tu0"), P.val("tu1")]

        def p6gu_sync(sync, tc=tc):
            for it in range(NIT):
                if it >= 2:
                    P.wge(sync, "pGU", tc * NIT + it - 1)
                P.inc(sync.dma_start(
                    wgu_sl[2 * (it % 2)].ap().rearrange("p (t q) -> p t q", q=128),
                    scr_g[it * 128:(it + 1) * 128, :], transpose=True),
                    "tg%d" % (it % 2), 16)
                P.inc(sync.dma_start(
                    wgu_sl[2 * (it % 2) + 1].ap().rearrange("p (t q) -> p t q", q=128),
                    scr_u[it * 128:(it + 1) * 128, :], transpose=True),
                    "tu%d" % (it % 2), 16)

        def p6gu_pe(tensor, tc=tc, tg_base=tg_base, tu_base=tu_base):
            for it in range(NIT):
                P.wge(tensor, "tg%d" % (it % 2), tg_base[it % 2] + (it // 2 + 1) * 16)
                P.wge(tensor, "tu%d" % (it % 2), tu_base[it % 2] + (it // 2 + 1) * 16)
                if it >= 2:
                    P.wge(tensor, "vH", tc * NIT + it - 1)
                last = None
                for ht in range(NHT):
                    last = tensor.matmul(
                        pgg[it % 2][:],
                        wgu_sl[2 * (it % 2)][:, ht * 128:(ht + 1) * 128],
                        x2nT[:, ht * TOK + tc * 512:ht * TOK + (tc + 1) * 512],
                        start=(ht == 0), stop=(ht == NHT - 1))
                P.inc(last, "pGs")
                last = None
                for ht in range(NHT):
                    last = tensor.matmul(
                        puu[it % 2][:],
                        wgu_sl[2 * (it % 2) + 1][:, ht * 128:(ht + 1) * 128],
                        x2nT[:, ht * TOK + tc * 512:ht * TOK + (tc + 1) * 512],
                        start=(ht == 0), stop=(ht == NHT - 1))
                P.inc(last, "pGU")

        def p6gu_sca(scalar, tc=tc):
            for it in range(NIT):
                P.wge(scalar, "pGs", tc * NIT + it + 1)
                if it >= 2:
                    P.wge(scalar, "vH", tc * NIT + it - 1)
                P.inc(scalar.activation(sil_st[it % 2][:], pgg[it % 2][:], AF.Silu,
                                        bias=c_zero[:]), "aSI")

        def p6gu_vec(vector, tc=tc):
            for it in range(NIT):
                P.wge(vector, "aSI", tc * NIT + it + 1)
                P.wge(vector, "pGU", tc * NIT + it + 1)
                P.inc(vector.tensor_tensor(
                    hT[:, it * 512:(it + 1) * 512],
                    sil_st[it % 2][:], puu[it % 2][:], ALU.mult), "vH")

        bodies(sync_f=p6gu_sync, tensor_f=p6gu_pe, vector_f=p6gu_vec, scalar_f=p6gu_sca)

        for cm in pu_cm[::-1]:
            cm.__exit__(None, None, None)
        for cm in pg_cm[::-1]:
            cm.__exit__(None, None, None)

        pd_cm = nc.psum_tensor(f"pd{tc}", [128, 8 * 512], F32)
        pd = pd_cm.__enter__()
        ld_base = [P.val("ld0"), P.val("ld1")]
        gy_base = [P.val("gy0"), P.val("gy1")]
        lx_base = [P.val("lx0"), P.val("lx1")]
        vdo_d = P.val("vDO")
        pd_d = P.val("pD")

        def p6d_sync(sync, tc=tc, pd_d=pd_d):
            for og in range(4):
                for grp in range(NGRP):
                    nit_g = 8 if grp < 10 else 6
                    it0 = grp * 8
                    k = og * NGRP + grp
                    if k >= 2:
                        P.wge(sync, "pD", pd_d + k - 1)
                    P.inc(sync.dma_start(
                        wd_sl[k % 2].ap().rearrange("p (a q) -> p a q", q=1024)[:, 0:nit_g, :],
                        scr_d2.rearrange("a p q -> p a q")[:, it0:it0 + nit_g,
                                                          og * 1024:(og + 1) * 1024]),
                        "ld%d" % (k % 2), 16)

        def p6d_pe(tensor, tc=tc, ld_base=ld_base, vdo_d=vdo_d):
            for og in range(4):
                if og >= 1:
                    P.wge(tensor, "vDO", vdo_d + og * 8)
                for grp in range(NGRP):
                    nit_g = 8 if grp < 10 else 6
                    it0 = grp * 8
                    k = og * NGRP + grp
                    P.wge(tensor, "ld%d" % (k % 2), ld_base[k % 2] + (k // 2 + 1) * 16)
                    last = None
                    for gi in range(nit_g):
                        it = it0 + gi
                        for oi in range(8):
                            last = tensor.matmul(
                                pd[:, oi * 512:(oi + 1) * 512],
                                wd_sl[(og * NGRP + grp) % 2][:, gi * 1024 + oi * 128:
                                                             gi * 1024 + (oi + 1) * 128],
                                hT[:, it * 512:(it + 1) * 512],
                                start=(it == 0), stop=(it == NIT - 1))
                    P.inc(last, "pD")

        def p6d_vec(vector, tc=tc, gy_base=gy_base, lx_base=lx_base, pd_d=pd_d):
            for og in range(4):
                P.wge(vector, "pD", pd_d + (og + 1) * NGRP)
                for oi in range(8):
                    k = og * 8 + oi
                    P.wge(vector, "lx%d" % (k % 2), lx_base[k % 2] + (k // 2 + 1) * 16)
                    if k >= 2:
                        P.wge(vector, "gy%d" % (k % 2), gy_base[k % 2] + (k // 2) * 16)
                    P.inc(vector.tensor_tensor(
                        out_st[k % 2][:], pd[:, oi * 512:(oi + 1) * 512],
                        x1t_sl[k % 2][:], ALU.add), "vDO")

        def p6d_gp(gpsimd, tc=tc, vdo_d=vdo_d):
            for og in range(4):
                for oi in range(8):
                    k = og * 8 + oi
                    if k >= 2:
                        P.wge(gpsimd, "vDO", vdo_d + k - 1)
                    ot = og * 8 + oi
                    P.inc(gpsimd.dma_start(
                        x1t_sl[k % 2][:],
                        scr_x1t[ot * 128:(ot + 1) * 128, tc * 512:(tc + 1) * 512]),
                        "lx%d" % (k % 2), 16)
                    P.wge(gpsimd, "vDO", vdo_d + k + 1)
                    P.inc(gpsimd.dma_start(
                        y_t[ot * 128:(ot + 1) * 128, tc * 512:(tc + 1) * 512],
                        out_st[k % 2][:]), "gy%d" % (k % 2), 16)

        bodies(sync_f=p6d_sync, tensor_f=p6d_pe, vector_f=p6d_vec, gpsimd_f=p6d_gp)
        pd_cm.__exit__(None, None, None)

    def fin_gp(gpsimd):
        for s in ["gy0", "gy1"]:
            P.wge(gpsimd, s, P.val(s))

    bodies(gpsimd_f=fin_gp, bar=False)

    block_cm.__exit__(None, None, None)
    nc.finalize()
    return nc



_CACHE = {}


def _host_inputs(inputs):
    x0 = np.ascontiguousarray(np.asarray(inputs["L0_attn_hidden_states"],
                                         dtype=np.float32).reshape(S, H))
    wq = np.ascontiguousarray(np.asarray(inputs["L0_attn_Wq"], dtype=np.float32))
    wk = np.ascontiguousarray(np.asarray(inputs["L0_attn_Wk"], dtype=np.float32))
    wv = np.ascontiguousarray(np.asarray(inputs["L0_attn_Wv"], dtype=np.float32))
    wo = np.ascontiguousarray(np.asarray(inputs["L0_attn_Wo"], dtype=np.float32))

    wq_t = np.ascontiguousarray(
        wq.reshape(4, 8, 128, NH, 128).transpose(3, 0, 2, 1, 4)).reshape(NH, 4, 128, 1024)
    wk_t = np.ascontiguousarray(
        wk.reshape(32, 128, 8, 128).transpose(2, 1, 0, 3)).reshape(8, 128, 32 * 128)
    wv_t = np.ascontiguousarray(
        wv.reshape(32, 128, 4, 256).transpose(2, 1, 0, 3)).reshape(4, 128, 32 * 256)
    wo_t = np.ascontiguousarray(
        wo.reshape(32, 128, 8, 2, 256).transpose(2, 3, 1, 0, 4)).reshape(8, 2, 128, 32 * 256)

    qg = np.ascontiguousarray(np.asarray(inputs["L0_ffn_qw_gate"], dtype=np.int32))
    qu = np.ascontiguousarray(np.asarray(inputs["L0_ffn_qw_up"], dtype=np.int32))
    qd = np.ascontiguousarray(np.asarray(inputs["L0_ffn_qw_down"], dtype=np.int32))
    sg = np.ascontiguousarray(np.asarray(inputs["L0_ffn_s_gate"], dtype=np.float32))
    zg = np.ascontiguousarray(np.asarray(inputs["L0_ffn_z_gate"], dtype=np.float32))
    su = np.ascontiguousarray(np.asarray(inputs["L0_ffn_s_up"], dtype=np.float32))
    zu = np.ascontiguousarray(np.asarray(inputs["L0_ffn_z_up"], dtype=np.float32))
    sd = np.ascontiguousarray(np.asarray(inputs["L0_ffn_s_down"], dtype=np.float32))
    zd = np.ascontiguousarray(np.asarray(inputs["L0_ffn_z_down"], dtype=np.float32))

    ident = np.eye(128, dtype=np.float32).astype(ml_dtypes.bfloat16)
    r = np.arange(128)
    m0b = (r[None, :] <= r[:, None]).astype(np.float32)
    m4b = (r[None, :] >= r[:, None]).astype(np.float32)
    m0 = np.ascontiguousarray(np.tile(m0b[:, None, :], (1, 4, 1)).reshape(128, 512)
                              ).astype(ml_dtypes.bfloat16)
    m4 = np.ascontiguousarray(np.tile(m4b[:, None, :], (1, 4, 1)).reshape(128, 512)
                              ).astype(ml_dtypes.bfloat16)

    inv = 1.0 / THETA ** (np.arange(0, HD, 2, dtype=np.float64) / HD)

    in_maps = []
    for c in range(NCORES):
        lo = c * TOK - HALO
        if lo < 0:
            xs = np.concatenate([np.zeros((-lo, H), np.float32), x0[:c * TOK + TOK]], axis=0)
        else:
            xs = x0[lo:c * TOK + TOK]
        xs = np.ascontiguousarray(xs)

        pos = np.maximum(np.arange(lo, c * TOK + TOK, dtype=np.float64), 0.0)
        ang = pos[None, :] * inv[:, None]
        cosv = np.cos(ang)
        sinv = np.sin(ang)
        csc = np.ascontiguousarray(np.concatenate([cosv, cosv], axis=0).astype(np.float32))
        css = np.ascontiguousarray(np.concatenate([-sinv, sinv], axis=0).astype(np.float32))

        bk = np.full((128, 12), -CSH, dtype=np.float32)
        if c == 0:
            bk[:, 0:4] = -1e30
        in_maps.append({
            "x0s": xs, "wq_t": wq_t, "wk_t": wk_t, "wv_t": wv_t, "wo_t": wo_t,
            "csc": csc, "css": css, "bk_in": bk, "m0_in": m0, "m4_in": m4,
            "id_in": ident,
            "qg_in": qg, "qu_in": qu, "qd_in": qd,
            "sg_in": sg, "zg_in": zg, "su_in": su, "zu_in": zu,
            "sd_in": sd, "zd_in": zd,
        })
    return in_maps


def kernel(**inputs):
    if "nc" not in _CACHE:
        _CACHE["nc"] = build_program()
    nc = _CACHE["nc"]
    in_maps = _host_inputs(inputs)
    res = run_bass_kernel_spmd(nc, in_maps, core_ids=list(range(NCORES)))
    parts = [np.asarray(res.results[c]["y_t"], dtype=np.float32).T for c in range(NCORES)]
    return np.concatenate(parts, axis=0).reshape(1, S, H)


# version 16
# speedup vs baseline: 1.0598x; 1.0319x over previous
import math
import numpy as np
import ml_dtypes
import concourse.bass as bass
import concourse.mybir as mybir
from concourse.bass_utils import run_bass_kernel_spmd

F32 = mybir.dt.float32
BF16 = mybir.dt.bfloat16
I32 = mybir.dt.int32
AF = mybir.ActivationFunctionType
ALU = mybir.AluOpType

NCORES = 8
S = 8192
H = 4096
TOK = 1024
HALO = 512
LTOK = TOK + HALO
NHT = H // 128
NH = 32
NKV = 8
HD = 128
GQ = NH // NKV
II = 11008
NIT = II // 128
SCALE = 1.0 / math.sqrt(HD)
CSH = 12.0
EPS = 1e-5
THETA = 10000.0
KB = 1024


class Prog:
    def __init__(self):
        self.nc = bass.Bass("TRN2", target_bir_lowering=False)
        self.sems = {}
        self.cnt = {}
        self.eng_sems = {}

    def sem(self, name):
        if name not in self.sems:
            self.sems[name] = self.nc.alloc_semaphore(name)
            self.cnt[name] = 0
        return self.sems[name]

    def inc(self, instr, name, n=1):
        instr.then_inc(self.sem(name), n)
        self.cnt[name] += n
        eng = str(instr.ins.engine)
        self.eng_sems.setdefault(eng, set()).add(name)
        return self.cnt[name]

    def val(self, name):
        self.sem(name)
        return self.cnt[name]

    def wge(self, eng, name, v):
        eng.wait_ge(self.sem(name), v)

    SBUF_BASE = 16512

    def sb(self, name, shape, dtype, off):
        return self.nc.alloc_sbuf_tensor_at(name, shape, dtype,
                                            offset=self.SBUF_BASE + off)


def build_program():
    P = Prog()
    nc = P.nc

    din = {}
    def dram_in(name, shape, dt):
        din[name] = nc.dram_tensor(name, list(shape), dt, kind="ExternalInput").ap()
        return din[name]

    x0s = dram_in("x0s", [LTOK, H], F32)
    wq_t = dram_in("wq_t", [NH, 4, 128, 1024], F32)
    wk_t = dram_in("wk_t", [NKV, 128, NHT * 128], F32)
    wv_t = dram_in("wv_t", [4, 128, NHT * 256], F32)
    wo_t = dram_in("wo_t", [8, 2, 128, NHT * 256], F32)
    csc_in = dram_in("csc", [128, LTOK], F32)
    css_in = dram_in("css", [128, LTOK], F32)
    bk_in = dram_in("bk_in", [128, 12], F32)
    m0_in = dram_in("m0_in", [128, 512], BF16)
    m4_in = dram_in("m4_in", [128, 512], BF16)
    id_in = dram_in("id_in", [128, 128], BF16)
    qg_in = dram_in("qg_in", [II, H], I32)
    qu_in = dram_in("qu_in", [II, H], I32)
    qd_in = dram_in("qd_in", [H, II], I32)
    sg_in = dram_in("sg_in", [II, 32], F32)
    zg_in = dram_in("zg_in", [II, 32], F32)
    su_in = dram_in("su_in", [II, 32], F32)
    zu_in = dram_in("zu_in", [II, 32], F32)
    sd_in = dram_in("sd_in", [H, NIT], F32)
    zd_in = dram_in("zd_in", [H, NIT], F32)

    y_t = nc.dram_tensor("y_t", [H, TOK], F32, kind="ExternalOutput").ap()

    scr_g = nc.dram_tensor("scr_g", [II, H], BF16, kind="Internal").ap()
    scr_u = nc.dram_tensor("scr_u", [II, H], BF16, kind="Internal").ap()
    scr_d2 = nc.dram_tensor("scr_d2", [NIT, 128, H], BF16, kind="Internal").ap()
    scr_x1t = nc.dram_tensor("scr_x1t", [H, TOK], BF16, kind="Internal").ap()

    cs_cos = P.sb("cs_cos", [128, LTOK], F32, 0)
    cs_sin = P.sb("cs_sin", [128, LTOK], F32, 6144)
    m0 = P.sb("m0", [128, 512], BF16, 12288)
    m4 = P.sb("m4", [128, 512], BF16, 13312)
    biask = P.sb("biask", [128, 12], F32, 14336)
    ident = P.sb("ident", [128, 128], BF16, 14400)
    ones_r = P.sb("ones_r", [1, 128], BF16, 14656)
    ones_c = P.sb("ones_c", [128, 1], BF16, 14912)
    c_zero = P.sb("c_zero", [128, 1], F32, 14944)
    c_eps = P.sb("c_eps", [128, 1], F32, 14976)
    OFF_A = 15 * KB
    OFF_D = 79 * KB
    OFF_E = 143 * KB
    OFF_F = 167 * KB
    OFF_G = 191 * KB

    x1nT_h = P.sb("x1nT_h", [128, NHT * HALO], BF16, OFF_A)
    x1nT_o = P.sb("x1nT_o", [128, NHT * TOK], BF16, OFF_A)
    attnT = P.sb("attnT", [128, NH * TOK], BF16, OFF_A)
    x2nT = P.sb("x2nT", [128, NHT * TOK], BF16, OFF_A)
    qT = P.sb("qT", [128, NH * TOK], BF16, OFF_D)
    x1 = P.sb("x1", [128, 8 * H], BF16, OFF_D)
    hT = P.sb("hT", [128, NIT * 512], BF16, OFF_D)
    kT = P.sb("kT", [128, NKV * LTOK], BF16, OFF_E)
    vB = P.sb("vB", [128, 12 * 1024], BF16, OFF_F)

    bar_n = [0]

    block_cm = nc.Block()
    block = block_cm.__enter__()

    def bodies(sync_f=None, tensor_f=None, vector_f=None, scalar_f=None,
               gpsimd_f=None, bar=True):
        def drain(eng, ename):
            for s in sorted(P.eng_sems.get(ename, ())):
                P.wge(eng, s, P.val(s))
            P.eng_sems[ename] = set()
        @block.sync
        def _(sync):
            P.wge(sync, "BAR", bar_n[0])
            if sync_f:
                sync_f(sync)
            if bar:
                drain(sync, "EngineType.SP")
                P.inc(sync.nop(), "BAR")
        @block.tensor
        def _(tensor):
            P.wge(tensor, "BAR", bar_n[0])
            if tensor_f:
                tensor_f(tensor)
            if bar:
                drain(tensor, "EngineType.PE")
                P.inc(tensor.nop(), "BAR")
        @block.vector
        def _(vector):
            P.wge(vector, "BAR", bar_n[0])
            if vector_f:
                vector_f(vector)
            if bar:
                drain(vector, "EngineType.DVE")
                P.inc(vector.nop(), "BAR")
        @block.scalar
        def _(scalar):
            P.wge(scalar, "BAR", bar_n[0])
            if scalar_f:
                scalar_f(scalar)
            if bar:
                drain(scalar, "EngineType.Activation")
                P.inc(scalar.nop(), "BAR")
        @block.gpsimd
        def _(gpsimd):
            P.wge(gpsimd, "BAR", bar_n[0])
            if gpsimd_f:
                gpsimd_f(gpsimd)
            if bar:
                drain(gpsimd, "EngineType.Pool")
                P.inc(gpsimd.nop(), "BAR")
        if bar:
            bar_n[0] = P.val("BAR")

    def p0_sync(sync):
        for dst, src in [(cs_cos, csc_in), (cs_sin, css_in), (m0, m0_in),
                         (m4, m4_in), (biask, bk_in), (ident, id_in)]:
            P.inc(sync.dma_start(dst[:], src[:]), "cLD", 16)
        P.wge(sync, "cLD", P.val("cLD"))

    def p0_vec(vector):
        vector.memset(c_zero[:], 0.0)
        vector.memset(c_eps[:], EPS)
        vector.memset(ones_r[:], 1.0)
        P.inc(vector.memset(ones_c[:], 1.0), "vP")
        P.wge(vector, "vP", P.val("vP"))

    bodies(sync_f=p0_sync, vector_f=p0_vec)

    qsl_g = [P.sb(f"qsl_g{i}", [128, H], I32, OFF_D + i * 16 * KB) for i in range(3)]
    qsl_u = [P.sb(f"qsl_u{i}", [128, H], I32, [OFF_D + 48 * KB, OFF_E, OFF_E + 16 * KB][i]) for i in range(3)]
    wb_g = [P.sb(f"wb_g{i}", [128, H], BF16, OFF_A + 44 * KB + i * 8 * KB) for i in range(2)]
    wb_u = [P.sb(f"wb_u{i}", [128, H], BF16, OFF_G + i * 8 * KB) for i in range(2)]
    s_g = P.sb("s_g", [128, NIT * 32], F32, OFF_A)
    nzs_g = P.sb("nzs_g", [128, NIT * 32], F32, OFF_A + 11 * KB)
    s_u = P.sb("s_u", [128, NIT * 32], F32, OFF_A + 22 * KB)
    nzs_u = P.sb("nzs_u", [128, NIT * 32], F32, OFF_A + 33 * KB)

    vp0 = P.val("vP")
    USPL = 22

    def p1gu_sync(sync):
        for dst, src in [(s_g, sg_in), (nzs_g, zg_in), (s_u, su_in), (nzs_u, zu_in)]:
            P.inc(sync.dma_start(
                dst.ap().rearrange("p (a b) -> p a b", b=32),
                src.rearrange("(a p) b -> p a b", p=128)), "szA", 16)
        for i in range(NIT):
            if i >= 3:
                P.wge(sync, "vDq", i - 2)
            P.inc(sync.dma_start(qsl_g[i % 3][:], qg_in[i * 128:(i + 1) * 128, :]),
                  "qg%d" % (i % 3), 16)
            if i >= 3:
                P.wge(sync, "aDq", i - 2)
                P.wge(sync, "uDv", i - 2)
            P.inc(sync.dma_start(qsl_u[i % 3][:], qu_in[i * 128:(i + 1) * 128, :]),
                  "qu%d" % (i % 3), 16)

    def p1gu_vec(vector):
        P.wge(vector, "szA", 64)
        P.inc(vector.tensor_scalar(nzs_u[:], nzs_u[:], -1.0, None, ALU.mult), "vP")
        P.wge(vector, "vP", P.val("vP"))
        P.inc(vector.tensor_tensor(nzs_u[:], nzs_u[:], s_u[:], ALU.mult), "vP")
        P.inc(vector.tensor_scalar(nzs_g[:], nzs_g[:], -1.0, None, ALU.mult), "vP")
        P.wge(vector, "vP", P.val("vP"))
        P.inc(vector.tensor_tensor(nzs_g[:], nzs_g[:], s_g[:], ALU.mult), "vP")
        P.wge(vector, "vP", P.val("vP"))
        sg3 = s_g.ap().rearrange("p (a b) -> p a b", b=32)
        ng3 = nzs_g.ap().rearrange("p (a b) -> p a b", b=32)
        su3 = s_u.ap().rearrange("p (a b) -> p a b", b=32)
        nu3 = nzs_u.ap().rearrange("p (a b) -> p a b", b=32)
        for i in range(NIT):
            P.wge(vector, "qg%d" % (i % 3), (i // 3 + 1) * 16)
            if i >= 2:
                P.wge(vector, "wg%d" % (i % 2), ((i - 2) // 2 + 1) * 16)
            last = None
            for g in range(32):
                last = vector.tensor_scalar(
                    wb_g[i % 2][:, g * 128:(g + 1) * 128],
                    qsl_g[i % 3][:, g * 128:(g + 1) * 128],
                    sg3[:, i, g:g + 1], ng3[:, i, g:g + 1],
                    ALU.mult, ALU.add)
            P.inc(last, "vDq")
            P.wge(vector, "qu%d" % (i % 3), (i // 3 + 1) * 16)
            if i >= 2:
                P.wge(vector, "wu%d" % (i % 2), ((i - 2) // 2 + 1) * 16)
            last = None
            for g in range(USPL, 32):
                last = vector.tensor_scalar(
                    wb_u[i % 2][:, g * 128:(g + 1) * 128],
                    qsl_u[i % 3][:, g * 128:(g + 1) * 128],
                    su3[:, i, g:g + 1], nu3[:, i, g:g + 1],
                    ALU.mult, ALU.add)
            P.inc(last, "uDv")

    def p1gu_sca(scalar):
        P.wge(scalar, "szA", 64)
        P.wge(scalar, "vP", vp0 + 2)
        su3 = s_u.ap().rearrange("p (a b) -> p a b", b=32)
        nu3 = nzs_u.ap().rearrange("p (a b) -> p a b", b=32)
        for i in range(NIT):
            P.wge(scalar, "qu%d" % (i % 3), (i // 3 + 1) * 16)
            if i >= 2:
                P.wge(scalar, "wu%d" % (i % 2), ((i - 2) // 2 + 1) * 16)
            last = None
            for g in range(USPL):
                last = scalar.activation(
                    wb_u[i % 2][:, g * 128:(g + 1) * 128],
                    qsl_u[i % 3][:, g * 128:(g + 1) * 128],
                    AF.Identity, bias=nu3[:, i, g:g + 1], scale=su3[:, i, g:g + 1])
            P.inc(last, "aDq")
            P.wge(scalar, "aDq", i + 1)
            P.wge(scalar, "uDv", i + 1)
            P.inc(scalar.dma_start(scr_u[i * 128:(i + 1) * 128, :], wb_u[i % 2][:]),
                  "wu%d" % (i % 2), 16)
            P.wge(scalar, "vDq", i + 1)
            P.inc(scalar.dma_start(scr_g[i * 128:(i + 1) * 128, :], wb_g[i % 2][:]),
                  "wg%d" % (i % 2), 16)

    bodies(sync_f=p1gu_sync, vector_f=p1gu_vec, scalar_f=p1gu_sca)

    DS = 16
    qsl_d = [P.sb(f"qsl_d{j}", [128, NHT * 128], I32, OFF_D + j * 16 * KB) for j in range(2)]
    wb_d = [P.sb(f"wb_d{j}", [128, H], BF16, OFF_D + 32 * KB + j * 8 * KB) for j in range(2)]
    sst = [P.sb(f"sst{j}", [128, H], BF16, OFF_D + 48 * KB + j * 8 * KB) for j in range(2)]
    s_d = P.sb("s_d", [128, NHT * NIT], F32, OFF_A)
    nzs_d = P.sb("nzs_d", [128, NHT * NIT], F32, OFF_A + 11 * KB)

    ptd_cm = [nc.psum_tensor(f"ptd{j}", [128, H], BF16) for j in range(2)]
    ptd = [cm.__enter__() for cm in ptd_cm]
    qd3 = qd_in.rearrange("(a p) q -> p a q", p=128)

    def p1d_sync(sync):
        for dst, src in [(s_d, sd_in), (nzs_d, zd_in)]:
            P.inc(sync.dma_start(
                dst.ap().rearrange("p (a b) -> p a b", b=NIT),
                src.rearrange("(a p) b -> p a b", p=128)), "szD", 16)
        for it in range(NIT):
            if it >= 2:
                P.wge(sync, "vDd", it - 1)
                P.wge(sync, "aDd", it - 1)
            P.inc(sync.dma_start(
                qsl_d[it % 2].ap().rearrange("p (a q) -> p a q", q=128),
                qd3[:, :, it * 128:(it + 1) * 128]), "qd%d" % (it % 2), 16)

    def p1d_pe(tensor):
        for it in range(NIT):
            P.wge(tensor, "vDd", it + 1)
            P.wge(tensor, "aDd", it + 1)
            if it >= 2:
                P.wge(tensor, "vEv", it - 1)
                P.wge(tensor, "aEv", it - 1)
            last = None
            for hs in range(NHT):
                last = tensor.transpose(
                    ptd[it % 2][:, hs * 128:(hs + 1) * 128],
                    wb_d[it % 2][:, hs * 128:(hs + 1) * 128], ident[:])
            P.inc(last, "pTd")

    def p1d_vec(vector):
        P.wge(vector, "szD", 32)
        P.inc(vector.tensor_scalar(nzs_d[:], nzs_d[:], -1.0, None, ALU.mult), "vP")
        P.wge(vector, "vP", P.val("vP"))
        P.inc(vector.tensor_tensor(nzs_d[:], nzs_d[:], s_d[:], ALU.mult), "vP")
        P.wge(vector, "vP", P.val("vP"))
        sd3 = s_d.ap().rearrange("p (a b) -> p a b", b=NIT)
        nd3 = nzs_d.ap().rearrange("p (a b) -> p a b", b=NIT)
        q3 = [q.ap().rearrange("p (a q) -> p a q", q=128) for q in qsl_d]

        def evac(it):
            P.wge(vector, "pTd", it + 1)
            if it >= 2:
                P.wge(vector, "w2%d" % (it % 2), (it // 2) * 16)
            P.inc(vector.tensor_copy(sst[it % 2][:, 0:DS * 128],
                                     ptd[it % 2][:, 0:DS * 128]), "vEv")

        for it in range(NIT):
            P.wge(vector, "qd%d" % (it % 2), (it // 2 + 1) * 16)
            if it >= 2:
                P.wge(vector, "pTd", it - 1)
            last = None
            for hs in range(DS):
                last = vector.tensor_scalar(
                    wb_d[it % 2][:, hs * 128:(hs + 1) * 128],
                    q3[it % 2][:, hs, :],
                    sd3[:, hs, it:it + 1], nd3[:, hs, it:it + 1],
                    ALU.mult, ALU.add)
            P.inc(last, "vDd")
            if it >= 1:
                evac(it - 1)
        evac(NIT - 1)

    def p1d_sca(scalar):
        P.wge(scalar, "vP", P.val("vP"))
        sd3 = s_d.ap().rearrange("p (a b) -> p a b", b=NIT)
        nd3 = nzs_d.ap().rearrange("p (a b) -> p a b", b=NIT)
        q3 = [q.ap().rearrange("p (a q) -> p a q", q=128) for q in qsl_d]

        def evst(it):
            P.wge(scalar, "pTd", it + 1)
            if it >= 2:
                P.wge(scalar, "w2%d" % (it % 2), (it // 2) * 16)
            P.inc(scalar.activation(sst[it % 2][:, DS * 128:], ptd[it % 2][:, DS * 128:],
                                    AF.Copy), "aEv")
            P.wge(scalar, "aEv", it + 1)
            P.wge(scalar, "vEv", it + 1)
            P.inc(scalar.dma_start(scr_d2[it, :, :], sst[it % 2][:]),
                  "w2%d" % (it % 2), 16)

        for it in range(NIT):
            P.wge(scalar, "qd%d" % (it % 2), (it // 2 + 1) * 16)
            if it >= 2:
                P.wge(scalar, "pTd", it - 1)
            last = None
            for hs in range(DS, NHT):
                last = scalar.activation(
                    wb_d[it % 2][:, hs * 128:(hs + 1) * 128],
                    q3[it % 2][:, hs, :],
                    AF.Identity, bias=nd3[:, hs, it:it + 1], scale=sd3[:, hs, it:it + 1])
            P.inc(last, "aDd")
            if it >= 1:
                evst(it - 1)
        evst(NIT - 1)

    bodies(sync_f=p1d_sync, tensor_f=p1d_pe, vector_f=p1d_vec, scalar_f=p1d_sca)
    for cm in ptd_cm[::-1]:
        cm.__exit__(None, None, None)

    x0st = [P.sb(f"x0st{i}", [128, H], F32, OFF_D + i * 16 * KB) for i in range(2)]
    junk = [P.sb(f"junk{i}", [128, H], BF16, OFF_D + 32 * KB + i * 8 * KB) for i in range(2)]
    xnst = [P.sb(f"xnst{i}", [128, H], BF16, OFF_D + 48 * KB + i * 8 * KB) for i in range(2)]
    stt = [P.sb(f"stt{i}", [128, 8], F32, 15008 + i * 64) for i in range(2)]

    ptr_cm = [nc.psum_tensor(f"ptr{i}", [128, 1024], BF16) for i in range(2)]
    ptr = [cm.__enter__() for cm in ptr_cm]

    def emit_ln_tiles(rows_list, dst, dstW, tag, prewaits=None):
        n = len(rows_list)
        aN = "aN" + tag; aS = "aS" + tag; aL = "aL" + tag
        vS = "vS" + tag; vT = "vT" + tag; pT = "pT" + tag
        pw = prewaits or {}

        def f_sync(sync):
            for eng_sem, v in pw.get("sync", []):
                P.wge(sync, eng_sem, v)
            for t in range(n):
                if t >= 2:
                    P.wge(sync, aN, t - 1)
                P.inc(sync.dma_start(x0st[t % 2][:],
                                     x0s[rows_list[t][0]:rows_list[t][0] + 128, :]),
                      "x0%s%d" % (tag, t % 2), 16)

        def f_sca(scalar):
            for eng_sem, v in pw.get("scalar", []):
                P.wge(scalar, eng_sem, v)
            for t in range(n):
                P.wge(scalar, "x0%s%d" % (tag, t % 2), (t // 2 + 1) * 16)
                if t >= 2:
                    P.wge(scalar, vS, 5 * (t - 1))
                st = stt[t % 2]
                P.inc(scalar.activation(junk[0][:], x0st[t % 2][:], AF.Square,
                                        bias=c_zero[:], accum_out=st[:, 1:2]), aS)
                P.inc(scalar.activation(junk[1][:], x0st[t % 2][:], AF.Copy,
                                        accum_out=st[:, 0:1]), aS)
                P.wge(scalar, vS, 5 * t + 4)
                P.inc(scalar.activation(st[:, 4:5], st[:, 3:4], AF.Ln, bias=c_eps[:]), aL)
                P.wge(scalar, aL, P.val(aL))
                P.inc(scalar.activation(st[:, 4:5], st[:, 4:5], AF.Exp, bias=c_zero[:],
                                        scale=-0.5), aL)
                P.wge(scalar, vS, 5 * t + 5)
                P.inc(scalar.activation(xnst[t % 2][:], x0st[t % 2][:], AF.Identity,
                                        bias=st[:, 5:6], scale=st[:, 4:5]), aN)

        def f_vec(vector):
            for eng_sem, v in pw.get("vector", []):
                P.wge(vector, eng_sem, v)
            for t in range(n):
                st = stt[t % 2]
                P.wge(vector, aS, 2 * t + 2)
                P.inc(vector.tensor_scalar(st[:, 2:3], st[:, 0:1], 1.0 / H, None,
                                           ALU.mult), vS)
                P.inc(vector.tensor_scalar(st[:, 3:4], st[:, 1:2], 1.0 / H, None,
                                           ALU.mult), vS)
                P.wge(vector, vS, 5 * t + 2)
                P.inc(vector.tensor_tensor(st[:, 6:7], st[:, 2:3], st[:, 2:3], ALU.mult), vS)
                P.wge(vector, vS, 5 * t + 3)
                P.inc(vector.tensor_tensor(st[:, 3:4], st[:, 3:4], st[:, 6:7],
                                           ALU.subtract), vS)
                P.wge(vector, aL, 2 * t + 2)
                P.inc(vector.tensor_scalar(st[:, 5:6], st[:, 2:3], st[:, 4:5], -1.0,
                                           ALU.mult, ALU.mult), vS)
                col0 = rows_list[t][1]
                for ht in range(NHT):
                    P.wge(vector, pT, NHT * t + ht + 1)
                    P.inc(vector.tensor_copy(
                        dst[:, ht * dstW + col0: ht * dstW + col0 + 128],
                        ptr[(NHT * t + ht) % 2][:, 0:128]), vT)

        def f_pe(tensor):
            for eng_sem, v in pw.get("tensor", []):
                P.wge(tensor, eng_sem, v)
            for t in range(n):
                P.wge(tensor, aN, t + 1)
                for ht in range(NHT):
                    if NHT * t + ht >= 2:
                        P.wge(tensor, vT, NHT * t + ht - 1)
                    P.inc(tensor.transpose(ptr[(NHT * t + ht) % 2][:, 0:128],
                                           xnst[t % 2][:, ht * 128:(ht + 1) * 128],
                                           ident[:]), pT)

        return f_sync, f_pe, f_vec, f_sca

    halo_rows = [(i * 128, i * 128) for i in range(4)]
    fs, fp, fv, fa = emit_ln_tiles(halo_rows, x1nT_h, HALO, "h")
    bodies(sync_f=fs, tensor_f=fp, vector_f=fv, scalar_f=fa, bar=False)

    wkv_sl = P.sb("wkv_sl", [128, NHT * 256], BF16, OFF_G)
    wk_sl = P.sb("wk_sl", [128, NHT * 128], BF16, OFF_G)
    rt1a = P.sb("rt1a", [128, 512], F32, OFF_G + 8 * KB)
    rt2a = P.sb("rt2a", [128, 512], F32, OFF_G + 10 * KB)

    pk_cm = [nc.psum_tensor(f"pk{i}", [128, 512], F32) for i in range(2)]
    pk = [cm.__enter__() for cm in pk_cm]
    pvv_cm = [nc.psum_tensor(f"pvv{i}", [128, 512], F32) for i in range(2)]
    pvv = [cm.__enter__() for cm in pvv_cm]

    def rope_evac(vector, psrc, c0, dst, dcols, tag, rt1, rt2):
        P.wge(vector, tag, P.val(tag))
        W = 512
        P.inc(vector.tensor_tensor(rt1[0:64, :], psrc[64:128, :],
                                   cs_sin[0:64, c0:c0 + W], ALU.mult), "vR")
        P.inc(vector.tensor_tensor(rt1[64:128, :], psrc[0:64, :],
                                   cs_sin[64:128, c0:c0 + W], ALU.mult), "vR")
        P.inc(vector.tensor_tensor(rt2[:, :], psrc[:, :],
                                   cs_cos[:, c0:c0 + W], ALU.mult), "vR")
        P.wge(vector, "vR", P.val("vR"))
        P.inc(vector.tensor_tensor(dst[:, dcols:dcols + W], rt2[:, :], rt1[:, :],
                                   ALU.add), tag)

    def rope_evac_q(vector, psrc, c0, dst, dcols, tag, rt1):
        P.wge(vector, tag, P.val(tag))
        W = 512
        P.inc(vector.tensor_tensor(rt1[0:64, :], psrc[64:128, :],
                                   cs_sin[0:64, c0:c0 + W], ALU.mult), "vR")
        P.inc(vector.tensor_tensor(rt1[64:128, :], psrc[0:64, :],
                                   cs_sin[64:128, c0:c0 + W], ALU.mult), "vR")
        P.inc(vector.tensor_tensor(dst[:, dcols:dcols + W], psrc[:, :],
                                   cs_cos[:, c0:c0 + W], ALU.mult), "vR")
        P.wge(vector, "vR", P.val("vR"))
        P.inc(vector.tensor_tensor(dst[:, dcols:dcols + W], dst[:, dcols:dcols + W],
                                   rt1[:, :], ALU.add), tag)

    def p3a_gp(gpsimd):
        for kv in range(NKV):
            if kv >= 1:
                P.wge(gpsimd, "pK", kv)
                P.wge(gpsimd, "vK", kv)
            P.inc(gpsimd.dma_start(wk_sl[:], wk_t[kv]), "lKV", 16)

    lkv_a = P.val("lKV")

    def p3a_pe(tensor):
        P.wge(tensor, "vTh", 4 * NHT)
        for kv in range(NKV):
            P.wge(tensor, "lKV", lkv_a + (kv + 1) * 16)
            if kv >= 2:
                P.wge(tensor, "vK", kv - 1)
            last = None
            for ht in range(NHT):
                last = tensor.matmul(pk[kv % 2][:],
                                     wk_sl[:, ht * 128:(ht + 1) * 128],
                                     x1nT_h[:, ht * HALO:(ht + 1) * HALO],
                                     start=(ht == 0), stop=(ht == NHT - 1))
            P.inc(last, "pK")

    def p3a_vec(vector):
        for kv in range(NKV):
            P.wge(vector, "pK", kv + 1)
            rope_evac(vector, pk[kv % 2], 0, kT, kv * LTOK, "vK", rt1a, rt2a)

    bodies(tensor_f=p3a_pe, vector_f=p3a_vec, gpsimd_f=p3a_gp, bar=False)

    def emit_v(jts, xsrc, xW, jt_base, tag):
        lkv_b = P.val("lKV")
        pV = "pV" + tag; vV = "vV" + tag
        nj = len(jts)

        def f_gp(gpsimd):
            for vc in range(4):
                if vc >= 1:
                    P.wge(gpsimd, pV, vc * nj)
                else:
                    P.wge(gpsimd, "pK", P.val("pK"))
                    P.wge(gpsimd, "vK", P.val("vK"))
                P.inc(gpsimd.dma_start(wkv_sl[:], wv_t[vc]), "lKV", 16)

        def f_pe(tensor):
            for vc in range(4):
                P.wge(tensor, "lKV", lkv_b + (vc + 1) * 16)
                for ji, jt in enumerate(jts):
                    k = vc * nj + ji
                    if k >= 2:
                        P.wge(tensor, vV, k - 1)
                    elif tag == "b":
                        P.wge(tensor, "vVa", 15 + k)
                    col = (jt - jt_base) * 128
                    last = None
                    for ht in range(NHT):
                        last = tensor.matmul(
                            pvv[k % 2][:, 0:256],
                            xsrc[:, ht * xW + col: ht * xW + col + 128],
                            wkv_sl[:, ht * 256:(ht + 1) * 256],
                            start=(ht == 0), stop=(ht == NHT - 1))
                    P.inc(last, pV)

        def f_vec(vector):
            for vc in range(4):
                for ji, jt in enumerate(jts):
                    k = vc * nj + ji
                    P.wge(vector, pV, k + 1)
                    P.inc(vector.tensor_copy(
                        vB[:, jt * 1024 + vc * 256: jt * 1024 + (vc + 1) * 256],
                        pvv[k % 2][:, 0:256]), vV)

        return f_gp, f_pe, f_vec

    fg, fpe, fve = emit_v([0, 1, 2, 3], x1nT_h, HALO, 0, "a")
    bodies(tensor_f=fpe, vector_f=fve, gpsimd_f=fg, bar=False)

    own_rows = [(HALO + i * 128, i * 128) for i in range(8)]
    fs, fpe, fve, fae = emit_ln_tiles(
        own_rows, x1nT_o, TOK, "o",
        prewaits={
            "scalar": [("aNh", 4), ("vSh", 20), ("pTh", 4 * NHT), ("pK", 8), ("pVa", 16)],
            "sync": [("aNh", 4)],
            "vector": [("vSh", 20)],
            "tensor": [("vTh", 4 * NHT), ("pVa", 16)],
        })
    bodies(sync_f=fs, tensor_f=fpe, vector_f=fve, scalar_f=fae, bar=False)

    pk_b = P.val("pK")
    lkv_c = P.val("lKV")

    def p3b_gp(gpsimd):
        for kv in range(NKV):
            if kv >= 1:
                P.wge(gpsimd, "pK", pk_b + 2 * kv)
                P.wge(gpsimd, "vK", NKV + 2 * kv)
            else:
                P.wge(gpsimd, "pVa", 16)
                P.wge(gpsimd, "vVa", 16)
            P.inc(gpsimd.dma_start(wk_sl[:], wk_t[kv]), "lKV", 16)

    def p3b_pe(tensor):
        P.wge(tensor, "vTo", 8 * NHT)
        for kv in range(NKV):
            P.wge(tensor, "lKV", lkv_c + (kv + 1) * 16)
            for ch in range(2):
                k = 2 * kv + ch
                if k >= 2:
                    P.wge(tensor, "vK", NKV + k - 1)
                else:
                    P.wge(tensor, "vK", NKV - 2 + k + 1)
                last = None
                for ht in range(NHT):
                    last = tensor.matmul(
                        pk[k % 2][:],
                        wk_sl[:, ht * 128:(ht + 1) * 128],
                        x1nT_o[:, ht * TOK + ch * 512:ht * TOK + (ch + 1) * 512],
                        start=(ht == 0), stop=(ht == NHT - 1))
                P.inc(last, "pK")

    def p3b_vec(vector):
        for kv in range(NKV):
            for ch in range(2):
                k = 2 * kv + ch
                P.wge(vector, "pK", pk_b + k + 1)
                rope_evac(vector, pk[k % 2], HALO + ch * 512, kT,
                          kv * LTOK + HALO + ch * 512, "vK", rt1a, rt2a)

    bodies(tensor_f=p3b_pe, vector_f=p3b_vec, gpsimd_f=p3b_gp, bar=False)

    fg, fpe, fve = emit_v([4, 5, 6, 7, 8, 9, 10, 11], x1nT_o, TOK, 4, "b")
    bodies(tensor_f=fpe, vector_f=fve, gpsimd_f=fg, bar=False)


    for cm in pvv_cm[::-1]:
        cm.__exit__(None, None, None)
    for cm in pk_cm[::-1]:
        cm.__exit__(None, None, None)

    wq_sl = [P.sb(f"wq_sl{i}", [128, 8 * 128], BF16, OFF_G + i * 2 * KB) for i in range(6)]
    wqf = [P.sb(f"wqf{i}", [128, 512], F32, OFF_G + 12 * KB + i * 2 * KB) for i in range(2)]
    pq_cm = [nc.psum_tensor(f"pq{i}", [128, 512], F32) for i in range(2)]
    pq = [cm.__enter__() for cm in pq_cm]
    rt1q_cm = nc.psum_tensor("rt1q", [128, 512], F32)
    rt1q = rt1q_cm.__enter__()

    def p4_sync(sync):
        P.wge(sync, "pVb", 32)
        P.wge(sync, "vVb", 32)
        for u in range(8 * NH):
            qt, a, hf = u // 8, (u % 8) // 2, u % 2
            if u >= 2:
                P.wge(sync, "aQC", u - 1)
            P.inc(sync.dma_start(
                wqf[u % 2][:],
                wq_t[qt, a, :, hf * 512:(hf + 1) * 512]), "lqf%d" % (u % 2), 16)

    def p4_sca(scalar):
        P.wge(scalar, "pVb", 32)
        P.wge(scalar, "vK", NKV + 2 * NKV)
        for u in range(8 * NH):
            qt, a, hf = u // 8, (u % 8) // 2, u % 2
            P.wge(scalar, "lqf%d" % (u % 2), (u // 2 + 1) * 16)
            q = 4 * qt + a
            if hf == 0 and q >= 6:
                P.wge(scalar, "pQ", 2 * ((q - 6) // 4 + 1))
            P.inc(scalar.activation(
                wq_sl[q % 6][:, hf * 512:(hf + 1) * 512],
                wqf[u % 2][:], AF.Copy), "aQC")

    def p4_pe(tensor):
        P.wge(tensor, "vTo", 8 * NHT)
        P.wge(tensor, "vVb", 32)
        for qt in range(NH):
            P.wge(tensor, "aQC", 8 * (qt + 1))
            for ch in range(2):
                k = 2 * qt + ch
                if k >= 2:
                    P.wge(tensor, "vQ", k - 1)
                else:
                    P.wge(tensor, "vK", NKV + 16)
                last = None
                for ht in range(NHT):
                    last = tensor.matmul(
                        pq[k % 2][:],
                        wq_sl[(4 * qt + ht // 8) % 6][:, (ht % 8) * 128:(ht % 8 + 1) * 128],
                        x1nT_o[:, ht * TOK + ch * 512:ht * TOK + (ch + 1) * 512],
                        start=(ht == 0), stop=(ht == NHT - 1))
                P.inc(last, "pQ")

    def p4_vec(vector):
        P.wge(vector, "pVb", 32)
        P.wge(vector, "vK", NKV + 2 * NKV)
        for qt in range(NH):
            for ch in range(2):
                k = 2 * qt + ch
                P.wge(vector, "pQ", k + 1)
                rope_evac_q(vector, pq[k % 2], HALO + ch * 512, qT,
                            qt * TOK + ch * 512, "vQ", rt1q)

    bodies(sync_f=p4_sync, tensor_f=p4_pe, vector_f=p4_vec, scalar_f=p4_sca, bar=False)

    rt1q_cm.__exit__(None, None, None)
    for cm in pq_cm[::-1]:
        cm.__exit__(None, None, None)
    for cm in ptr_cm[::-1]:
        cm.__exit__(None, None, None)

    probs = [P.sb(f"probs{i}", [128, 5 * 512], BF16, OFF_G + i * 5 * KB) for i in range(2)]
    bc_sb = [P.sb(f"bc_sb{i}", [128, 512], BF16, OFF_G + 10 * KB + i * KB) for i in range(2)]
    r_sb = P.sb("r_sb", [1, 512], F32, OFF_G + 12 * KB)
    r_bf = P.sb("r_bf", [1, 512], BF16, OFF_G + 14 * KB)

    ps_s_cm = nc.psum_tensor("ps_s", [128, 5 * 512], F32)
    ps_s = ps_s_cm.__enter__()
    ps_pv_cm = nc.psum_tensor("ps_pv", [128, 512], F32)
    ps_pv = ps_pv_cm.__enter__()
    ps_sum_cm = nc.psum_tensor("ps_sum", [1, 512], F32)
    ps_sum = ps_sum_cm.__enter__()
    ps_bc_cm = nc.psum_tensor("ps_bc", [128, 512], F32)
    ps_bc = ps_bc_cm.__enter__()

    qT3 = qT.ap().rearrange("p (h t) -> p h t", t=TOK)
    attnT3 = attnT.ap().rearrange("p (h t) -> p h t", t=TOK)

    NID = NKV * 8

    def p5a_scores(tensor, idx):
        kv, qt = idx // 8, idx % 8
        rhs_q = qT3[:, kv * GQ:(kv + 1) * GQ, qt * 128:(qt + 1) * 128]
        for j in range(5):
            if idx >= 1:
                P.wge(tensor, "aEX", 5 * (idx - 1) + j + 1)
            P.inc(tensor.matmul(
                ps_s[:, j * 512:(j + 1) * 512],
                kT[:, kv * LTOK + (qt + j) * 128: kv * LTOK + (qt + j + 1) * 128],
                rhs_q, start=True, stop=True), "pSC")

    def p5a_tail(tensor, idx):
        kv, qt = idx // 8, idx % 8
        pb = probs[idx % 2]
        P.wge(tensor, "vMK", 2 * (idx + 1))
        P.wge(tensor, "vEV", idx)
        last = None
        for j in range(5):
            last = tensor.matmul(
                ps_pv[:],
                vB[:, (qt + j) * 1024 + kv * 128:(qt + j) * 1024 + (kv + 1) * 128],
                pb[:, j * 512:(j + 1) * 512],
                start=(j == 0), stop=(j == 4))
        P.inc(last, "pPV")
        P.wge(tensor, "aRC", 2 * idx)
        last = None
        for j in range(5):
            last = tensor.matmul(
                ps_sum[:],
                ones_c[:],
                pb[:, j * 512:(j + 1) * 512],
                start=(j == 0), stop=(j == 4))
        P.inc(last, "pSM")
        P.wge(tensor, "aRC", 2 * (idx + 1))
        P.wge(tensor, "vBC", idx)
        P.inc(tensor.matmul(ps_bc[:], ones_r[:], r_bf[:],
                            start=True, stop=True), "pBC")

    def p5a_pe(tensor):
        P.wge(tensor, "vQ", 2 * NH)
        p5a_scores(tensor, 0)
        for idx in range(1, NID):
            p5a_scores(tensor, idx)
            p5a_tail(tensor, idx - 1)
        p5a_tail(tensor, NID - 1)

    def p5a_rcp(scalar, idx):
        P.wge(scalar, "pSM", idx + 1)
        P.inc(scalar.activation(r_sb[:], ps_sum[:], AF.Ln, bias=c_zero[0:1, :]), "aRC")
        P.wge(scalar, "aRC", P.val("aRC"))
        P.inc(scalar.activation(r_bf[:], r_sb[:], AF.Exp, bias=c_zero[0:1, :],
                                scale=-1.0), "aRC")

    def p5a_sca(scalar):
        for idx in range(NID):
            for j in range(5):
                P.wge(scalar, "pSC", 5 * idx + j + 1)
                if idx >= 2 and j == 0:
                    P.wge(scalar, "pSM", idx - 1)
                P.inc(scalar.activation(
                    probs[idx % 2][:, j * 512:(j + 1) * 512],
                    ps_s[:, j * 512:(j + 1) * 512],
                    AF.Exp, bias=biask[:, idx % 8 + j:idx % 8 + j + 1], scale=SCALE),
                    "aEX")
            if idx >= 1:
                p5a_rcp(scalar, idx - 1)
        p5a_rcp(scalar, NID - 1)

    def p5a_vec(vector):
        for idx in range(NID):
            kv, qt = idx // 8, idx % 8
            pb = probs[idx % 2]
            P.wge(vector, "aEX", 5 * idx + 5)
            P.inc(vector.tensor_tensor(pb[:, 0:512], pb[:, 0:512], m0[:],
                                       ALU.mult), "vMK")
            P.inc(vector.tensor_tensor(pb[:, 4 * 512:5 * 512],
                                       pb[:, 4 * 512:5 * 512], m4[:],
                                       ALU.mult), "vMK")
            P.wge(vector, "pPV", idx + 1)
            P.wge(vector, "pBC", idx + 1)
            P.inc(vector.tensor_copy(bc_sb[idx % 2][:], ps_bc[:]), "vBC")
            P.wge(vector, "vBC", idx + 1)
            P.inc(vector.tensor_tensor(
                attnT3[:, kv * GQ:(kv + 1) * GQ, qt * 128:(qt + 1) * 128],
                ps_pv[:], bc_sb[idx % 2][:], ALU.mult), "vEV")

    bodies(tensor_f=p5a_pe, vector_f=p5a_vec, scalar_f=p5a_sca)

    for cm in [ps_bc_cm, ps_sum_cm, ps_pv_cm, ps_s_cm]:
        cm.__exit__(None, None, None)

    wo_sl = [P.sb(f"wo_sl{i}", [128, NHT * 256], BF16, OFF_E + i * 16 * KB) for i in range(3)]
    x0c = P.sb("x0c", [128, 8 * 512], F32, OFF_G)
    po_cm = [nc.psum_tensor(f"po{i}", [128, 512], F32) for i in range(2)]
    po = [cm.__enter__() for cm in po_cm]
    def p5b_sync(sync):
        for oc in range(8):
            if oc >= 1:
                P.wge(sync, "vO", 8 * oc)
            P.inc(sync.dma_start(
                x0c.ap().rearrange("p (t q) -> p t q", q=512),
                x0s.rearrange("(t p) q -> p t q", p=128)[:, 4:12, oc * 512:(oc + 1) * 512]),
                "lxc", 16)

    def p5b_gp(gpsimd):
        h = 0
        for oc in range(8):
            for half in range(2):
                if h >= 3:
                    P.wge(gpsimd, "pO", 8 * ((h - 3) // 2) + 8)
                P.inc(gpsimd.dma_start(wo_sl[h % 3][:], wo_t[oc, half]),
                      "lo%d" % (h % 3), 16)
                h += 1

    def p5b_pe(tensor):
        for oc in range(8):
            P.wge(tensor, "lo%d" % ((2 * oc) % 3), ((2 * oc) // 3 + 1) * 16)
            P.wge(tensor, "lo%d" % ((2 * oc + 1) % 3), ((2 * oc + 1) // 3 + 1) * 16)
            for tt in range(8):
                k = 8 * oc + tt
                if k >= 2:
                    P.wge(tensor, "vO", k - 1)
                last = None
                for half in range(2):
                    sl = wo_sl[(2 * oc + half) % 3]
                    for dt in range(NHT):
                        last = tensor.matmul(
                            po[k % 2][:, half * 256:(half + 1) * 256],
                            attnT[:, dt * TOK + tt * 128:dt * TOK + (tt + 1) * 128],
                            sl[:, dt * 256:(dt + 1) * 256],
                            start=(dt == 0), stop=(dt == NHT - 1))
                P.inc(last, "pO")

    def p5b_vec(vector):
        for oc in range(8):
            for tt in range(8):
                k = 8 * oc + tt
                P.wge(vector, "pO", k + 1)
                P.wge(vector, "lxc", (oc + 1) * 16)
                P.inc(vector.tensor_tensor(
                    x1[:, tt * H + oc * 512:tt * H + (oc + 1) * 512],
                    po[k % 2][:], x0c[:, tt * 512:(tt + 1) * 512], ALU.add), "vO")

    bodies(sync_f=p5b_sync, tensor_f=p5b_pe, vector_f=p5b_vec, gpsimd_f=p5b_gp)

    for cm in po_cm[::-1]:
        cm.__exit__(None, None, None)

    x2st = [P.sb(f"x2st{i}", [128, H], BF16, OFF_E + i * 8 * KB) for i in range(2)]
    x1tst = [P.sb(f"x1tst{i}", [128, NHT * 128], BF16, OFF_E + 16 * KB + i * 8 * KB) for i in range(2)]
    junkb = [P.sb(f"junkb{i}", [128, H], BF16, OFF_F + 8 * KB + i * 8 * KB) for i in range(2)]

    ptr2_cm = [nc.psum_tensor(f"ptr2_{i}", [128, 1024], BF16) for i in range(2)]
    ptr2 = [cm.__enter__() for cm in ptr2_cm]

    def p5c_sca(scalar):
        for tt in range(8):
            st = stt[tt % 2]
            if tt >= 2:
                P.wge(scalar, "vS2", 5 * (tt - 1))
            xt = x1[:, tt * H:(tt + 1) * H]
            P.inc(scalar.activation(junkb[0][:], xt, AF.Square, bias=c_zero[:],
                                    accum_out=st[:, 1:2]), "aS2")
            P.inc(scalar.activation(junkb[1][:], xt, AF.Copy, accum_out=st[:, 0:1]), "aS2")
            P.wge(scalar, "vS2", 5 * tt + 4)
            P.inc(scalar.activation(st[:, 4:5], st[:, 3:4], AF.Ln, bias=c_eps[:]), "aL2")
            P.wge(scalar, "aL2", P.val("aL2"))
            P.inc(scalar.activation(st[:, 4:5], st[:, 4:5], AF.Exp, bias=c_zero[:],
                                    scale=-0.5), "aL2")
            P.wge(scalar, "vS2", 5 * tt + 5)
            if tt >= 2:
                P.wge(scalar, "vT2", 64 * (tt - 1))
            P.inc(scalar.activation(x2st[tt % 2][:], xt, AF.Identity,
                                    bias=st[:, 5:6], scale=st[:, 4:5]), "aN2")

    def p5c_vec(vector):
        for tt in range(8):
            st = stt[tt % 2]
            P.wge(vector, "aS2", 2 * tt + 2)
            P.inc(vector.tensor_scalar(st[:, 2:3], st[:, 0:1], 1.0 / H, None, ALU.mult), "vS2")
            P.inc(vector.tensor_scalar(st[:, 3:4], st[:, 1:2], 1.0 / H, None, ALU.mult), "vS2")
            P.wge(vector, "vS2", 5 * tt + 2)
            P.inc(vector.tensor_tensor(st[:, 6:7], st[:, 2:3], st[:, 2:3], ALU.mult), "vS2")
            P.wge(vector, "vS2", 5 * tt + 3)
            P.inc(vector.tensor_tensor(st[:, 3:4], st[:, 3:4], st[:, 6:7], ALU.subtract), "vS2")
            P.wge(vector, "aL2", 2 * tt + 2)
            P.inc(vector.tensor_scalar(st[:, 5:6], st[:, 2:3], st[:, 4:5], -1.0,
                                       ALU.mult, ALU.mult), "vS2")
            for ht in range(NHT):
                P.wge(vector, "pT2", 64 * tt + ht + 1)
                P.inc(vector.tensor_copy(
                    x2nT[:, ht * TOK + tt * 128:ht * TOK + (tt + 1) * 128],
                    ptr2[(64 * tt + ht) % 2][:, 0:128]), "vT2")
            for ht in range(NHT):
                P.wge(vector, "pT2", 64 * tt + 32 + ht + 1)
                P.inc(vector.tensor_copy(
                    x1tst[tt % 2][:, ht * 128:(ht + 1) * 128],
                    ptr2[(64 * tt + 32 + ht) % 2][:, 0:128]), "vT2")


    def p5c_pe(tensor):
        for tt in range(8):
            P.wge(tensor, "aN2", tt + 1)
            for ht in range(NHT):
                if 64 * tt + ht >= 2:
                    P.wge(tensor, "vT2", 64 * tt + ht - 1)
                P.inc(tensor.transpose(ptr2[(64 * tt + ht) % 2][:, 0:128],
                                       x2st[tt % 2][:, ht * 128:(ht + 1) * 128],
                                       ident[:]), "pT2")
            if tt >= 2:
                P.wge(tensor, "sx%d" % (tt % 2), (tt // 2) * 32)
            for ht in range(NHT):
                P.wge(tensor, "vT2", 64 * tt + 32 + ht - 1)
                P.inc(tensor.transpose(ptr2[(64 * tt + 32 + ht) % 2][:, 0:128],
                                       x1[:, tt * H + ht * 128:tt * H + (ht + 1) * 128],
                                       ident[:]), "pT2")

    def p5c_gp(gpsimd):
        for tt in range(8):
            P.wge(gpsimd, "vT2", 64 * tt + 64)
            for hh in range(2):
                P.inc(gpsimd.dma_start(
                    scr_x1t.rearrange("(a p) t -> p a t", p=128)[:, hh * 16:(hh + 1) * 16,
                                                                tt * 128:(tt + 1) * 128],
                    x1tst[tt % 2].ap().rearrange("p (a q) -> p a q", q=128)[:, hh * 16:(hh + 1) * 16, :]),
                    "sx%d" % (tt % 2), 16)
        for s in ["sx0", "sx1"]:
            P.wge(gpsimd, s, P.val(s))

    bodies(tensor_f=p5c_pe, vector_f=p5c_vec, scalar_f=p5c_sca, gpsimd_f=p5c_gp)

    for cm in ptr2_cm[::-1]:
        cm.__exit__(None, None, None)

    wgu_sl = [P.sb(f"wgu_sl{i}", [128, NHT * 128], BF16, OFF_F + i * 8 * KB) for i in range(4)]
    wgu_sl[3] = P.sb("wgu_sl3", [128, NHT * 128], BF16, OFF_G)
    sil_st = [P.sb(f"sil_st{i}", [128, 512], F32, OFF_G + 8 * KB + i * 2 * KB) for i in range(2)]
    wd_sl = [P.sb(f"wd_sl{i}", [128, 8 * 1024], BF16, OFF_F + i * 16 * KB) for i in range(2)]
    x1t_sl = [P.sb(f"x1t_sl{i}", [128, 512], BF16, OFF_G + 8 * KB + i * KB) for i in range(2)]
    out_st = [P.sb(f"out_st{i}", [128, 512], F32, OFF_G + 10 * KB + i * 2 * KB) for i in range(2)]

    NGRP = 11

    for tc in range(2):
        pg_cm = [nc.psum_tensor(f"pgg{tc}_{i}", [128, 512], F32) for i in range(2)]
        pgg = [cm.__enter__() for cm in pg_cm]
        pu_cm = [nc.psum_tensor(f"puu{tc}_{i}", [128, 512], F32) for i in range(2)]
        puu = [cm.__enter__() for cm in pu_cm]
        tg_base = [P.val("tg0"), P.val("tg1")]
        tu_base = [P.val("tu0"), P.val("tu1")]

        def p6gu_sync(sync, tc=tc):
            for it in range(NIT):
                if it >= 2:
                    P.wge(sync, "pGU", tc * NIT + it - 1)
                P.inc(sync.dma_start(
                    wgu_sl[2 * (it % 2)].ap().rearrange("p (t q) -> p t q", q=128),
                    scr_g[it * 128:(it + 1) * 128, :], transpose=True),
                    "tg%d" % (it % 2), 16)
                P.inc(sync.dma_start(
                    wgu_sl[2 * (it % 2) + 1].ap().rearrange("p (t q) -> p t q", q=128),
                    scr_u[it * 128:(it + 1) * 128, :], transpose=True),
                    "tu%d" % (it % 2), 16)

        def p6gu_pe(tensor, tc=tc, tg_base=tg_base, tu_base=tu_base):
            for it in range(NIT):
                P.wge(tensor, "tg%d" % (it % 2), tg_base[it % 2] + (it // 2 + 1) * 16)
                P.wge(tensor, "tu%d" % (it % 2), tu_base[it % 2] + (it // 2 + 1) * 16)
                if it >= 2:
                    P.wge(tensor, "vH", tc * NIT + it - 1)
                last = None
                for ht in range(NHT):
                    last = tensor.matmul(
                        pgg[it % 2][:],
                        wgu_sl[2 * (it % 2)][:, ht * 128:(ht + 1) * 128],
                        x2nT[:, ht * TOK + tc * 512:ht * TOK + (tc + 1) * 512],
                        start=(ht == 0), stop=(ht == NHT - 1))
                P.inc(last, "pGs")
                last = None
                for ht in range(NHT):
                    last = tensor.matmul(
                        puu[it % 2][:],
                        wgu_sl[2 * (it % 2) + 1][:, ht * 128:(ht + 1) * 128],
                        x2nT[:, ht * TOK + tc * 512:ht * TOK + (tc + 1) * 512],
                        start=(ht == 0), stop=(ht == NHT - 1))
                P.inc(last, "pGU")

        def p6gu_sca(scalar, tc=tc):
            for it in range(NIT):
                P.wge(scalar, "pGs", tc * NIT + it + 1)
                if it >= 2:
                    P.wge(scalar, "vH", tc * NIT + it - 1)
                P.inc(scalar.activation(sil_st[it % 2][:], pgg[it % 2][:], AF.Silu,
                                        bias=c_zero[:]), "aSI")

        def p6gu_vec(vector, tc=tc):
            for it in range(NIT):
                P.wge(vector, "aSI", tc * NIT + it + 1)
                P.wge(vector, "pGU", tc * NIT + it + 1)
                P.inc(vector.tensor_tensor(
                    hT[:, it * 512:(it + 1) * 512],
                    sil_st[it % 2][:], puu[it % 2][:], ALU.mult), "vH")

        bodies(sync_f=p6gu_sync, tensor_f=p6gu_pe, vector_f=p6gu_vec, scalar_f=p6gu_sca)

        for cm in pu_cm[::-1]:
            cm.__exit__(None, None, None)
        for cm in pg_cm[::-1]:
            cm.__exit__(None, None, None)

        pd_cm = nc.psum_tensor(f"pd{tc}", [128, 8 * 512], F32)
        pd = pd_cm.__enter__()
        ld_base = [P.val("ld0"), P.val("ld1")]
        gy_base = [P.val("gy0"), P.val("gy1")]
        lx_base = [P.val("lx0"), P.val("lx1")]
        vdo_d = P.val("vDO")
        pd_d = P.val("pD")

        def p6d_sync(sync, tc=tc, pd_d=pd_d):
            for og in range(4):
                for grp in range(NGRP):
                    nit_g = 8 if grp < 10 else 6
                    it0 = grp * 8
                    k = og * NGRP + grp
                    if k >= 2:
                        P.wge(sync, "pD", pd_d + k - 1)
                    P.inc(sync.dma_start(
                        wd_sl[k % 2].ap().rearrange("p (a q) -> p a q", q=1024)[:, 0:nit_g, :],
                        scr_d2.rearrange("a p q -> p a q")[:, it0:it0 + nit_g,
                                                          og * 1024:(og + 1) * 1024]),
                        "ld%d" % (k % 2), 16)

        def p6d_pe(tensor, tc=tc, ld_base=ld_base, vdo_d=vdo_d):
            for og in range(4):
                if og >= 1:
                    P.wge(tensor, "vDO", vdo_d + og * 8)
                for grp in range(NGRP):
                    nit_g = 8 if grp < 10 else 6
                    it0 = grp * 8
                    k = og * NGRP + grp
                    P.wge(tensor, "ld%d" % (k % 2), ld_base[k % 2] + (k // 2 + 1) * 16)
                    last = None
                    for gi in range(nit_g):
                        it = it0 + gi
                        for oi in range(8):
                            last = tensor.matmul(
                                pd[:, oi * 512:(oi + 1) * 512],
                                wd_sl[(og * NGRP + grp) % 2][:, gi * 1024 + oi * 128:
                                                             gi * 1024 + (oi + 1) * 128],
                                hT[:, it * 512:(it + 1) * 512],
                                start=(it == 0), stop=(it == NIT - 1))
                    P.inc(last, "pD")

        def p6d_vec(vector, tc=tc, gy_base=gy_base, lx_base=lx_base, pd_d=pd_d):
            for og in range(4):
                P.wge(vector, "pD", pd_d + (og + 1) * NGRP)
                for oi in range(8):
                    k = og * 8 + oi
                    P.wge(vector, "lx%d" % (k % 2), lx_base[k % 2] + (k // 2 + 1) * 16)
                    if k >= 2:
                        P.wge(vector, "gy%d" % (k % 2), gy_base[k % 2] + (k // 2) * 16)
                    P.inc(vector.tensor_tensor(
                        out_st[k % 2][:], pd[:, oi * 512:(oi + 1) * 512],
                        x1t_sl[k % 2][:], ALU.add), "vDO")

        def p6d_gp(gpsimd, tc=tc, vdo_d=vdo_d):
            for og in range(4):
                for oi in range(8):
                    k = og * 8 + oi
                    if k >= 2:
                        P.wge(gpsimd, "vDO", vdo_d + k - 1)
                    ot = og * 8 + oi
                    P.inc(gpsimd.dma_start(
                        x1t_sl[k % 2][:],
                        scr_x1t[ot * 128:(ot + 1) * 128, tc * 512:(tc + 1) * 512]),
                        "lx%d" % (k % 2), 16)
                    P.wge(gpsimd, "vDO", vdo_d + k + 1)
                    P.inc(gpsimd.dma_start(
                        y_t[ot * 128:(ot + 1) * 128, tc * 512:(tc + 1) * 512],
                        out_st[k % 2][:]), "gy%d" % (k % 2), 16)

        bodies(sync_f=p6d_sync, tensor_f=p6d_pe, vector_f=p6d_vec, gpsimd_f=p6d_gp)
        pd_cm.__exit__(None, None, None)

    def fin_gp(gpsimd):
        for s in ["gy0", "gy1"]:
            P.wge(gpsimd, s, P.val(s))

    bodies(gpsimd_f=fin_gp, bar=False)

    block_cm.__exit__(None, None, None)
    nc.finalize()
    return nc



_CACHE = {}


def _host_inputs(inputs):
    x0 = np.ascontiguousarray(np.asarray(inputs["L0_attn_hidden_states"],
                                         dtype=np.float32).reshape(S, H))
    wq = np.ascontiguousarray(np.asarray(inputs["L0_attn_Wq"], dtype=np.float32))
    wk = np.ascontiguousarray(np.asarray(inputs["L0_attn_Wk"], dtype=np.float32))
    wv = np.ascontiguousarray(np.asarray(inputs["L0_attn_Wv"], dtype=np.float32))
    wo = np.ascontiguousarray(np.asarray(inputs["L0_attn_Wo"], dtype=np.float32))

    wq_t = np.ascontiguousarray(
        wq.reshape(4, 8, 128, NH, 128).transpose(3, 0, 2, 1, 4)).reshape(NH, 4, 128, 1024)
    wk_t = np.ascontiguousarray(
        wk.reshape(32, 128, 8, 128).transpose(2, 1, 0, 3)).reshape(8, 128, 32 * 128)
    wv_t = np.ascontiguousarray(
        wv.reshape(32, 128, 4, 256).transpose(2, 1, 0, 3)).reshape(4, 128, 32 * 256)
    wo_t = np.ascontiguousarray(
        wo.reshape(32, 128, 8, 2, 256).transpose(2, 3, 1, 0, 4)).reshape(8, 2, 128, 32 * 256)

    qg = np.ascontiguousarray(np.asarray(inputs["L0_ffn_qw_gate"], dtype=np.int32))
    qu = np.ascontiguousarray(np.asarray(inputs["L0_ffn_qw_up"], dtype=np.int32))
    qd = np.ascontiguousarray(np.asarray(inputs["L0_ffn_qw_down"], dtype=np.int32))
    sg = np.ascontiguousarray(np.asarray(inputs["L0_ffn_s_gate"], dtype=np.float32))
    zg = np.ascontiguousarray(np.asarray(inputs["L0_ffn_z_gate"], dtype=np.float32))
    su = np.ascontiguousarray(np.asarray(inputs["L0_ffn_s_up"], dtype=np.float32))
    zu = np.ascontiguousarray(np.asarray(inputs["L0_ffn_z_up"], dtype=np.float32))
    sd = np.ascontiguousarray(np.asarray(inputs["L0_ffn_s_down"], dtype=np.float32))
    zd = np.ascontiguousarray(np.asarray(inputs["L0_ffn_z_down"], dtype=np.float32))

    ident = np.eye(128, dtype=np.float32).astype(ml_dtypes.bfloat16)
    r = np.arange(128)
    m0b = (r[None, :] <= r[:, None]).astype(np.float32)
    m4b = (r[None, :] >= r[:, None]).astype(np.float32)
    m0 = np.ascontiguousarray(np.tile(m0b[:, None, :], (1, 4, 1)).reshape(128, 512)
                              ).astype(ml_dtypes.bfloat16)
    m4 = np.ascontiguousarray(np.tile(m4b[:, None, :], (1, 4, 1)).reshape(128, 512)
                              ).astype(ml_dtypes.bfloat16)

    inv = 1.0 / THETA ** (np.arange(0, HD, 2, dtype=np.float64) / HD)

    in_maps = []
    for c in range(NCORES):
        lo = c * TOK - HALO
        if lo < 0:
            xs = np.concatenate([np.zeros((-lo, H), np.float32), x0[:c * TOK + TOK]], axis=0)
        else:
            xs = x0[lo:c * TOK + TOK]
        xs = np.ascontiguousarray(xs)

        pos = np.maximum(np.arange(lo, c * TOK + TOK, dtype=np.float64), 0.0)
        ang = pos[None, :] * inv[:, None]
        cosv = np.cos(ang)
        sinv = np.sin(ang)
        csc = np.ascontiguousarray(np.concatenate([cosv, cosv], axis=0).astype(np.float32))
        css = np.ascontiguousarray(np.concatenate([-sinv, sinv], axis=0).astype(np.float32))

        bk = np.full((128, 12), -CSH, dtype=np.float32)
        if c == 0:
            bk[:, 0:4] = -1e30
        in_maps.append({
            "x0s": xs, "wq_t": wq_t, "wk_t": wk_t, "wv_t": wv_t, "wo_t": wo_t,
            "csc": csc, "css": css, "bk_in": bk, "m0_in": m0, "m4_in": m4,
            "id_in": ident,
            "qg_in": qg, "qu_in": qu, "qd_in": qd,
            "sg_in": sg, "zg_in": zg, "su_in": su, "zu_in": zu,
            "sd_in": sd, "zd_in": zd,
        })
    return in_maps


def kernel(**inputs):
    if "nc" not in _CACHE:
        _CACHE["nc"] = build_program()
    nc = _CACHE["nc"]
    in_maps = _host_inputs(inputs)
    res = run_bass_kernel_spmd(nc, in_maps, core_ids=list(range(NCORES)))
    parts = [np.asarray(res.results[c]["y_t"], dtype=np.float32).T for c in range(NCORES)]
    return np.concatenate(parts, axis=0).reshape(1, S, H)
